# version 04
# baseline (speedup 1.0000x reference)
import contextlib
import numpy as np
import concourse.bass as bass
import concourse.mybir as mybir
from concourse.bass_utils import run_bass_kernel_spmd

F32 = mybir.dt.float32
AF = mybir.ActivationFunctionType
ALU = mybir.AluOpType

L = 2048
D = 1024
DI = 2048
DST = 16
DTR = 64
NH, HD = 16, 64
DFF = 4096
NE = 8
NT = D // 128
NTI = DI // 128
QC = 512
NQ = L // QC
EPS = 1e-5

ENGS = ("tensor", "vector", "scalar", "sync", "gpsimd")
DMAQ = {"sync": "syncq", "gpsimd": "gpsq", "scalar": "scalq", "vector": "vecq"}
QLANES = frozenset(DMAQ.values())


class Tr:

    def __init__(self, nc, block, es):
        self.nc = nc
        self.block = block
        self.es = es
        self.sems = {}
        self.cnt = {}
        self.wm = {}
        self.wr = {}
        self.rd = {}
        self.pend = {}
        self.nwaits = 0
        self.nops = 0
        for e in ENGS:
            self.pend[e] = []

    def _sem(self, lane):
        if lane not in self.sems:
            self.sems[lane] = self.es.enter_context(self.nc.semaphore(lane))
            self.cnt[lane] = 0
        return self.sems[lane]

    def _wait(self, eng_name, eng, lane, val):
        if val <= 0:
            return
        if lane.startswith("F_"):
            lane = "P_" + lane[2:]
        key = (eng_name, lane)
        if self.wm.get(key, 0) >= val:
            return
        eng.wait_ge(self._sem(lane), val)
        self.wm[key] = val
        self.nwaits += 1

    def _lane_wait(self, eng_name, eng, lane, val, kind, k):
        if lane in QLANES:
            val = self.cnt[lane]
        elif val is None:
            raise RuntimeError(f"{kind} on {k!r} pending unmarked event")
        self._wait(eng_name, eng, lane, val)

    def _deps(self, eng_name, eng, reads, writes, own_p=True):
        own = ("P_" + eng_name,) if own_p else ()
        for k in reads:
            w = self.wr.get(k)
            if w is not None and w[0] not in own:
                self._lane_wait(eng_name, eng, w[0], w[1], "read", k)
        for k in writes:
            w = self.wr.get(k)
            if w is not None and w[0] not in own:
                self._lane_wait(eng_name, eng, w[0], w[1], "WAW", k)
            for lane, val in self.rd.get(k, ()):
                if lane in own:
                    continue
                self._lane_wait(eng_name, eng, lane, val, "WAR", k)

    def op(self, eng_name, fn, reads=(), writes=(), mark=True, fenced=False):
        if fenced:
            assert mark
        sec = getattr(self.block, eng_name)
        lane = "P_" + eng_name
        self._sem(lane)
        tr = self

        @sec
        def _(eng):
            tr._deps(eng_name, eng, reads, writes)
            inst = fn(eng)
            if mark:
                inst.then_inc(tr._sem(lane), 1)
        if mark:
            self.cnt[lane] += 1
            val = self.cnt[lane]
            for k in self.pend[eng_name]:
                lw = self.wr.get(k)
                if lw is not None and lw[0] == lane and lw[1] is None:
                    self.wr[k] = (lane, val)
                rl = self.rd.get(k)
                if rl:
                    self.rd[k] = [(ln, val if (ln == lane and v is None) else v)
                                  for ln, v in rl]
            self.pend[eng_name] = []
            ev = (("F_" + eng_name) if fenced else lane, val)
        else:
            ev = (lane, None)
        for k in writes:
            self.wr[k] = ev
            self.rd[k] = []
            if not mark:
                self.pend[eng_name].append(k)
        for k in reads:
            self.rd.setdefault(k, []).append(ev)
            if not mark and k not in self.pend[eng_name]:
                self.pend[eng_name].append(k)
        self.nops += 1

    def dma(self, eng_name, out, in_, reads=(), writes=()):
        sec = getattr(self.block, eng_name)
        qlane = DMAQ[eng_name]
        self._sem(qlane)
        tr = self

        @sec
        def _(eng):
            tr._deps(eng_name, eng, reads, writes, own_p=False)
            inst = eng.dma_start(out=out, in_=in_)
            inst.then_inc(tr._sem(qlane), 16)
        self.cnt[qlane] += 16
        ev = (qlane, self.cnt[qlane])
        for k in writes:
            self.wr[k] = ev
            self.rd[k] = []
        for k in reads:
            self.rd.setdefault(k, []).append(ev)
        self.nops += 1

    def barrier(self):
        vals = {ln: self.cnt.get(ln, 0) for ln in list(self.cnt)}
        for e in ENGS:
            lane = "P_" + e
            self._sem(lane)
            sec = getattr(self.block, e)
            tr = self
            myv = dict(vals)

            @sec
            def _(eng, e=e, myv=myv):
                for ln, v in myv.items():
                    if ln != "P_" + e:
                        tr._wait(e, eng, ln, v)
                eng.sem_inc(tr._sem("P_" + e), 1)
            self.cnt[lane] += 1
        for k in list(self.wr):
            lane, val = self.wr[k]
            if val is None:
                raise RuntimeError(f"barrier with pending unmarked write on {k!r}")
        for e in ENGS:
            for ln in list(self.cnt):
                if ln != "P_" + e:
                    self.wm[(e, ln)] = max(self.wm.get((e, ln), 0), vals.get(ln, 0))

    def finish(self):
        tr = self

        @self.block.gpsimd
        def _(gp):
            for ln in ("syncq", "gpsq", "scalq", "vecq"):
                if ln in tr.sems:
                    tr._wait("gpsimd", gp, ln, tr.cnt[ln])
            for e in ENGS:
                ln = "P_" + e
                if ln in tr.sems and e != "gpsimd":
                    tr._wait("gpsimd", gp, ln, tr.cnt[ln])


class Grp:
    def __init__(self, nc):
        self.nc = nc
        self.es = contextlib.ExitStack()

    def sb(self, name, shape, dtype=F32):
        return self.es.enter_context(self.nc.sbuf_tensor(name, shape, dtype))

    def close(self):
        self.es.close()


def qslice(q):
    return slice(q * QC, (q + 1) * QC)


def emit_layernorm(tr, g, pb, ones, cpool, in_tiles, in_keys, out_tiles, out_keys,
                   w_sb=None, b_sb=None, wkey=None, tag="ln", post=None):
    sq = g.sb(tag + "sq", [128, L])
    rowm = g.sb(tag + "rm", [1, L])
    rowr = g.sb(tag + "rr", [1, L])
    bnm = g.sb(tag + "bm", [128, L])
    brs = g.sb(tag + "br", [128, L])
    K = tag
    for t in range(NT):
        tr.op("scalar", lambda e, t=t: e.activation(sq[:], in_tiles[t][:], AF.Square,
                                                    bias=0.0, scale=1.0),
              reads=[in_keys[t]], writes=[K + "sq"])
        for q in range(NQ):
            tr.op("tensor", lambda e, t=t, q=q: e.matmul(
                pb[q][0:1, :], ones[:, 0:1], in_tiles[t][:, qslice(q)],
                start=(t == 0), stop=(t == NT - 1)),
                reads=[in_keys[t], "c_ones"], writes=[f"pb{q}"], mark=False)
            tr.op("tensor", lambda e, t=t, q=q: e.matmul(
                pb[4 + q][0:1, :], ones[:, 0:1], sq[:, qslice(q)],
                start=(t == 0), stop=(t == NT - 1)),
                reads=[K + "sq", "c_ones"], writes=[f"pb{4 + q}"],
                mark=(q == NQ - 1))
    for q in range(NQ):
        tr.op("vector", lambda e, q=q: e.tensor_scalar_mul(
            rowm[:, qslice(q)], pb[q][0:1, :], -1.0 / D),
            reads=[f"pb{q}"], writes=[K + "rm"], mark=(q == NQ - 1))
        tr.op("vector", lambda e, q=q: e.tensor_scalar_mul(
            rowr[:, qslice(q)], pb[4 + q][0:1, :], 1.0 / D),
            reads=[f"pb{4 + q}"], writes=[K + "rr"], mark=(q == NQ - 1))
    tr.op("vector", lambda e: e.tensor_mul(sq[0:1, 0:L], rowm[:], rowm[:]),
          reads=[K + "rm"], writes=[K + "sq"])
    tr.op("vector", lambda e: e.tensor_sub(rowr[:], rowr[:], sq[0:1, 0:L]),
          reads=[K + "sq", K + "rr"], writes=[K + "rr"])
    tr.op("scalar", lambda e: e.activation(rowr[:], rowr[:], AF.Sqrt, bias=cpool[0:1, 0:1], scale=1.0),
          reads=[K + "rr", "cpool"], writes=[K + "rr"])
    tr.op("vector", lambda e: e.reciprocal(rowr[:], rowr[:]),
          reads=[K + "rr"], writes=[K + "rr"])
    for q in range(NQ):
        tr.op("tensor", lambda e, q=q: e.matmul(pb[q][:, :], ones[0:1, :],
                                                rowm[:, qslice(q)], start=True, stop=True),
              reads=[K + "rm", "c_ones"], writes=[f"pb{q}"])
        tr.op("tensor", lambda e, q=q: e.matmul(pb[4 + q][:, :], ones[0:1, :],
                                                rowr[:, qslice(q)], start=True, stop=True),
              reads=[K + "rr", "c_ones"], writes=[f"pb{4 + q}"])
        tr.op("vector", lambda e, q=q: e.tensor_copy(bnm[:, qslice(q)], pb[q][:, :]),
              reads=[f"pb{q}"], writes=[K + "bm"], mark=(q == NQ - 1))
        tr.op("vector", lambda e, q=q: e.tensor_copy(brs[:, qslice(q)], pb[4 + q][:, :]),
              reads=[f"pb{4 + q}"], writes=[K + "br"], mark=(q == NQ - 1))
    for t in range(NT):
        tr.op("vector", lambda e, t=t: e.tensor_add(sq[:], in_tiles[t][:], bnm[:]),
              reads=[in_keys[t], K + "bm"], writes=[K + "sq"])
        if w_sb is None:
            tr.op("vector", lambda e, t=t: e.tensor_mul(out_tiles[t][:], sq[:], brs[:]),
                  reads=[K + "sq", K + "br"], writes=[out_keys[t]])
        else:
            tr.op("vector", lambda e, t=t: e.tensor_mul(sq[:], sq[:], brs[:]),
                  reads=[K + "sq", K + "br"], writes=[K + "sq"])
            tr.op("vector", lambda e, t=t: e.tensor_scalar(
                out_tiles[t][:], sq[:], w_sb[:, t:t + 1], b_sb[:, t:t + 1],
                ALU.mult, ALU.add),
                reads=[K + "sq", wkey], writes=[out_keys[t]])
        if post is not None:
            post(t)


def emit_mamba_A(tr, g, pb, cpool, bc_sb, xn, w_minT, w_mxT_sb, w_mdtT_sb,
                 sm_convw, sm_convb, sm_bdt, s_x, s_zs, s_dt):
    wbuf = [g.sb(f"maw{i}", [128, NT, 128]) for i in range(2)]
    xpad = g.sb("maxp", [128, L + 3])
    cb0 = g.sb("mac0", [128, L])
    cb1 = g.sb("mac1", [128, L])
    cb2 = g.sb("mac2", [128, L])
    xfin = [g.sb(f"maxf{i}", [128, L]) for i in range(2)]
    zbuf = cb0
    zs2 = [cb1, cb2]
    dtr_sb = g.sb("ma_dtr", [64, L])
    sp_sb = g.sb("masp", [128, L])
    dt_sb = [g.sb(f"madt{i}", [128, L]) for i in range(2)]

    tr.op("vector", lambda e: e.memset(xpad[:, 0:3], 0.0), writes=["maxpad_z"])

    for e_ in range(32):
        wk = f"maw{e_ % 2}"
        for k in range(NT):
            tr.dma("sync", out=wbuf[e_ % 2][:, k, :],
                   in_=w_minT[k * 128:(k + 1) * 128, e_ * 128:(e_ + 1) * 128],
                   writes=[wk] if k == NT - 1 else [wk + "part"])
        for q in range(NQ):
            for k in range(NT):
                tr.op("tensor", lambda e, e_=e_, q=q, k=k: e.matmul(
                    pb[q][:, :], wbuf[e_ % 2][:, k, :], xn[k][:, qslice(q)],
                    start=(k == 0), stop=(k == NT - 1)),
                    reads=[wk, wk + "part", f"xn{k}"], writes=[f"pb{q}"],
                    mark=(q == NQ - 1 and k == NT - 1))
        if e_ < 16:
            for q in range(NQ):
                tr.op("vector", lambda e, q=q: e.tensor_scalar(
                    xpad[:, 3 + q * QC:3 + (q + 1) * QC], pb[q][:, :],
                    0.0, 1.0, ALU.max, ALU.min),
                    reads=[f"pb{q}", "maxpad_z"], writes=["maxp"], mark=(q == NQ - 1))
            tr.op("vector", lambda e, e_=e_: e.tensor_scalar(
                cb0[:], xpad[:, 0:L], sm_convw[:, 4 * e_:4 * e_ + 1], None, ALU.mult),
                reads=["maxp", "smalls"], writes=["mac0"])
            tr.op("vector", lambda e, e_=e_: e.scalar_tensor_tensor(
                cb1[:], xpad[:, 1:1 + L], sm_convw[:, 4 * e_ + 1:4 * e_ + 2], cb0[:],
                ALU.mult, ALU.add), reads=["maxp", "mac0", "smalls"], writes=["mac1"])
            tr.op("vector", lambda e, e_=e_: e.scalar_tensor_tensor(
                cb0[:], xpad[:, 2:2 + L], sm_convw[:, 4 * e_ + 2:4 * e_ + 3], cb1[:],
                ALU.mult, ALU.add), reads=["maxp", "mac1", "smalls"], writes=["mac0"])
            tr.op("vector", lambda e, e_=e_: e.scalar_tensor_tensor(
                cb1[:], xpad[:, 3:3 + L], sm_convw[:, 4 * e_ + 3:4 * e_ + 4], cb0[:],
                ALU.mult, ALU.add), reads=["maxp", "mac0", "smalls"], writes=["mac1"])
            tr.op("scalar", lambda e, e_=e_: e.activation(
                cb2[:], cb1[:], AF.Silu, bias=sm_convb[:, e_:e_ + 1], scale=1.0),
                reads=["mac1", "smalls"], writes=["mac2"])
            xk = f"maxf{e_ % 2}"
            tr.op("vector", lambda e, e_=e_: e.tensor_scalar(
                xfin[e_ % 2][:], cb2[:], 0.0, 1.0, ALU.max, ALU.min),
                reads=["mac2"], writes=[xk])
            tr.dma("gpsimd", out=s_x[e_, :, :], in_=xfin[e_ % 2][:],
                   reads=[xk], writes=[f"s_x{e_}"])
            for q in range(NQ):
                tr.op("tensor", lambda e, e_=e_, q=q: e.matmul(
                    pb[4 + q][0:96, :], w_mxT_sb[:, e_, :], xfin[e_ % 2][:, qslice(q)],
                    start=(e_ == 0), stop=(e_ == 15)),
                    reads=[xk, "w_mxT"], writes=[f"pb{4 + q}"],
                    mark=(q == NQ - 1))
        else:
            zk = "mac1" if e_ % 2 == 0 else "mac2"
            for q in range(NQ):
                tr.op("vector", lambda e, q=q: e.tensor_scalar(
                    zbuf[:, qslice(q)], pb[q][:, :], 0.0, 1.0, ALU.max, ALU.min),
                    reads=[f"pb{q}"], writes=["mac0"], mark=(q == NQ - 1))
            tr.op("scalar", lambda e, e_=e_: e.activation(
                zs2[e_ % 2][:], zbuf[:], AF.Silu, bias=0.0, scale=1.0),
                reads=["mac0"], writes=[zk])
            tr.dma("gpsimd", out=s_zs[e_ - 16, :, :], in_=zs2[e_ % 2][:],
                   reads=[zk], writes=[f"s_zs{e_ - 16}"])

    for q in range(NQ):
        tr.op("vector", lambda e, q=q: e.tensor_scalar(
            dtr_sb[:, qslice(q)], pb[4 + q][0:64, :], 0.0, 1.0, ALU.max, ALU.min),
            reads=[f"pb{4 + q}"], writes=["ma_dtr"], mark=(q == NQ - 1))
        tr.op("vector", lambda e, q=q: e.tensor_copy(
            bc_sb[:, qslice(q)], pb[4 + q][64:96, :]),
            reads=[f"pb{4 + q}"], writes=["ma_bc"], mark=(q == NQ - 1))

    for t in range(NTI):
        for q in range(NQ):
            bank = 4 + ((t * NQ + q) % 4)
            tr.op("tensor", lambda e, t=t, q=q, bank=bank: e.matmul(
                pb[bank][:, :], w_mdtT_sb[:, t * 128:(t + 1) * 128],
                dtr_sb[:, qslice(q)], start=True, stop=True),
                reads=["ma_dtr", "w_mdtT"], writes=[f"pb{bank}"])
            tr.op("vector", lambda e, t=t, q=q, bank=bank: e.tensor_scalar(
                sp_sb[:, qslice(q)], pb[bank][:, :], sm_bdt[:, t:t + 1], 21.0,
                ALU.add, ALU.min),
                reads=[f"pb{bank}", "smalls"], writes=["masp"], mark=(q == NQ - 1))
        tr.op("scalar", lambda e, t=t: e.activation(cb0[:], sp_sb[:], AF.Exp,
                                                    bias=0.0, scale=1.0),
              reads=["masp"], writes=["mac0"])
        tr.op("scalar", lambda e, t=t: e.activation(cb1[:], cb0[:], AF.Ln,
                                                    bias=cpool[:, 1:2], scale=1.0),
              reads=["mac0", "cpool"], writes=["mac1"])
        dk = f"madt{t % 2}"
        tr.op("vector", lambda e, t=t: e.tensor_scalar(
            dt_sb[t % 2][:], cb1[:], 1e-4, 20.0, ALU.max, ALU.min),
            reads=["mac1"], writes=[dk])
        tr.dma("gpsimd", out=s_dt[t, :, :], in_=dt_sb[t % 2][:],
               reads=[dk], writes=[f"s_dt{t}"])
    return dtr_sb


def emit_mamba_scan(tr, g, pb, A, Bar, e32_sb, bc_sb, negA, sm_d, w_moutT,
                    s_x, s_dt, s_zs, s_s1, hcar_ref=None):
    def BCv(j):
        return A[j // 4][:, (j % 4) * QC:(j % 4 + 1) * QC]

    def YCv(t):
        return Bar[t // 4][:, (t % 4) * QC:(t % 4 + 1) * QC]

    def YAv(t):
        return Bar[4 + t // 4][:, (t % 4) * QC:(t % 4 + 1) * QC]

    dtb = [g.sb(f"scdt{i}", [128, QC]) for i in range(2)]
    xb = [g.sb(f"scx{i}", [128, QC]) for i in range(2)]
    dtu = g.sb("scdu", [128, QC])
    dAb = [g.sb(f"scda{i}", [128, QC]) for i in range(4)]
    bfb = g.sb("scbf", [128, QC])
    hsq = [g.sb(f"sch{i}", [128, QC]) for i in range(4)]
    qb = g.sb("scq", [128, QC])
    qb2 = g.sb("scq2", [128, QC])
    hcar = g.sb("schc", [128, NTI * DST])
    if hcar_ref is not None:
        hcar_ref.append(hcar)
    zsb = [g.sb(f"sczs{i}", [128, QC]) for i in range(2)]
    wob = [g.sb("scwo0", [128, NTI, 128])]
    ssb = [g.sb(f"scs{i}", [128, QC]) for i in range(2)]

    tr.op("vector", lambda e: e.memset(hcar[:], 0.0),
          writes=["schc"] + [f"shc{t}_{n}" for t in range(NTI) for n in range(DST)])

    for c in range(NQ):
        for j in range(32):
            bank = j % 2
            tr.op("tensor", lambda e, c=c, j=j, bank=bank: e.matmul(
                pb[bank][:, :], e32_sb[:, j * 128:(j + 1) * 128],
                bc_sb[:, qslice(c)], start=True, stop=True),
                reads=["ma_bc", "c_e32"], writes=[f"pb{bank}"])
            tr.op("scalar", lambda e, j=j, bank=bank: e.activation(
                BCv(j), pb[bank][:, :], AF.Copy, bias=0.0, scale=1.0),
                reads=[f"pb{bank}"], writes=[f"scb{j}"])
        for t in range(NTI):
            kdt, kx, kzs = f"scdt{t % 2}", f"scx{t % 2}", f"sczs{t % 2}"
            tr.dma("sync", out=dtb[t % 2][:], in_=s_dt[t, :, qslice(c)],
                   reads=[f"s_dt{t}"], writes=[kdt])
            tr.dma("scalar", out=xb[t % 2][:], in_=s_x[t, :, qslice(c)],
                   reads=[f"s_x{t}"], writes=[kx])
            tr.dma("scalar", out=zsb[t % 2][:], in_=s_zs[t, :, qslice(c)],
                   reads=[f"s_zs{t}"], writes=[kzs])
            tr.op("vector", lambda e, t=t: e.tensor_mul(dtu[:], dtb[t % 2][:],
                                                        xb[t % 2][:]),
                  reads=[kdt, kx], writes=["scdu"])
            for n in range(DST):
                ka, kh = f"scda{n % 4}", f"sch{n % 4}"
                tr.op("scalar", lambda e, t=t, n=n: e.activation(
                    dAb[n % 4][:], dtb[t % 2][:], AF.Exp, bias=0.0,
                    scale=negA[:, t * DST + n:t * DST + n + 1]),
                    reads=[kdt, "negA"], writes=[ka])
                tr.op("vector", lambda e, n=n: e.tensor_mul(
                    bfb[:], dtu[:], BCv(n)),
                    reads=["scdu", f"scb{n}"], writes=["scbf"])
                tr.op("vector", lambda e, t=t, n=n: e.tensor_tensor_scan(
                    hsq[n % 4][:], dAb[n % 4][:], bfb[:],
                    hcar[:, t * DST + n:t * DST + n + 1], ALU.mult, ALU.add),
                    reads=[ka, "scbf", f"shc{t}_{n}"], writes=[kh], fenced=True)
                tr.op("gpsimd", lambda e, t=t, n=n: e.tensor_copy(
                    hcar[:, t * DST + n:t * DST + n + 1], hsq[n % 4][:, QC - 1:QC]),
                    reads=[kh], writes=[f"shc{t}_{n}"])
                if n == 0:
                    tr.op("gpsimd", lambda e, t=t, n=n: e.tensor_mul(
                        YCv(t), hsq[n % 4][:], BCv(16 + n)),
                        reads=[kh, f"scb{16 + n}"], writes=[f"scy{t}"])
                elif n == 1:
                    tr.op("vector", lambda e, t=t, n=n: e.tensor_mul(
                        YAv(t), hsq[n % 4][:], BCv(16 + n)),
                        reads=[kh, f"scb{16 + n}"], writes=[f"sca{t}"])
                elif n % 2 == 0:
                    tr.op("gpsimd", lambda e, t=t, n=n: e.tensor_mul(
                        qb[:], hsq[n % 4][:], BCv(16 + n)),
                        reads=[kh, f"scb{16 + n}"], writes=["scq"])
                    tr.op("gpsimd", lambda e, t=t: e.tensor_add(
                        YCv(t), YCv(t), qb[:]),
                        reads=["scq", f"scy{t}"], writes=[f"scy{t}"])
                else:
                    tr.op("vector", lambda e, t=t, n=n: e.tensor_mul(
                        qb2[:], hsq[n % 4][:], BCv(16 + n)),
                        reads=[kh, f"scb{16 + n}"], writes=["scq2"])
                    tr.op("vector", lambda e, t=t: e.tensor_add(
                        YAv(t), YAv(t), qb2[:]),
                        reads=["scq2", f"sca{t}"], writes=[f"sca{t}"])
            if "nofin" in getattr(tr, "flags", ()):
                continue
            tr.op("vector", lambda e, t=t: e.tensor_add(
                YCv(t), YCv(t), YAv(t)),
                reads=[f"sca{t}", f"scy{t}"], writes=[f"scy{t}"])
            tr.op("vector", lambda e, t=t: e.scalar_tensor_tensor(
                YCv(t), xb[t % 2][:], sm_d[:, t:t + 1], YCv(t),
                ALU.mult, ALU.add), reads=[kx, f"scy{t}", "smalls"], writes=[f"scy{t}"])
            tr.op("vector", lambda e, t=t: e.tensor_scalar(
                YCv(t), YCv(t), 0.0, 1.0, ALU.max, ALU.min),
                reads=[f"scy{t}"], writes=[f"scy{t}"])
            tr.op("vector", lambda e, t=t: e.tensor_mul(
                YCv(t), YCv(t), zsb[t % 2][:]),
                reads=[f"scy{t}", kzs], writes=[f"scy{t}"])
            tr.op("vector", lambda e, t=t: e.tensor_scalar(
                YCv(t), YCv(t), 0.0, 1.0, ALU.max, ALU.min),
                reads=[f"scy{t}"], writes=[f"scy{t}"])
        for dm in range(NT):
            kw = "scwo0"
            for t in range(NTI):
                tr.dma("sync", out=wob[0][:, t, :],
                       in_=w_moutT[t * 128:(t + 1) * 128, dm * 128:(dm + 1) * 128],
                       writes=[kw] if t == NTI - 1 else [kw + "p"])
            bank = 2 + (dm % 2)
            for t in range(NTI):
                tr.op("tensor", lambda e, dm=dm, t=t, bank=bank: e.matmul(
                    pb[bank][:, :], wob[0][:, t, :], YCv(t),
                    start=(t == 0), stop=(t == NTI - 1)),
                    reads=[kw, kw + "p", f"scy{t}"], writes=[f"pb{bank}"],
                    mark=(t == NTI - 1))
            ks = f"scs{dm % 2}"
            tr.op("vector", lambda e, bank=bank, dm=dm: e.tensor_copy(
                ssb[dm % 2][:], pb[bank][:, :]), reads=[f"pb{bank}"], writes=[ks])
            tr.dma("gpsimd", out=s_s1[dm, :, qslice(c)], in_=ssb[dm % 2][:],
                   reads=[ks], writes=[f"s_s1{dm}"])



def emit_ffn_prologue(tr, g, pb, ones, cpool, A, B, s_xn, w_ln1w, w_ln1b, w_bo,
                      w_bqkvh, bq8):
    for t in range(NT):
        tr.dma("sync", out=B[t][:], in_=s_xn[t, :, :],
               reads=[f"s_xn{t}"], writes=[f"xB{t}"])
    emit_layernorm(tr, g, pb, ones, cpool, B, [f"xB{t}" for t in range(NT)],
                   A, [f"nx{t}" for t in range(NT)],
                   w_sb=w_ln1w, b_sb=w_ln1b, wkey="smallsF", tag="ln1")
    for t in range(NT):
        tr.op("vector", lambda e, t=t: e.tensor_scalar(
            B[t][:], B[t][:], w_bo[:, t:t + 1], None, ALU.add),
            reads=[f"xB{t}", "smallsF"], writes=[f"x1_{t}", f"xB{t}"])
    tr.op("vector", lambda e: e.tensor_scalar_mul(bq8[:], w_bqkvh[:, 0:48:3], 0.125),
          reads=["smallsF"], writes=["bq8"])


def emit_attn_l0(tr, g, pb, ones, A, B, w_qkv, w_wo, w_bqkvh, bq8, masks_sb):
    wq = g.sb("a0wq", [128, NT, 128])
    wk = g.sb("a0wk", [128, NT, 128])
    wv = g.sb("a0wv", [128, NT, 128])
    wo = g.sb("a0wo", [64, 2, D])
    Qa = g.sb("a0qa", [65, L])
    Ka = g.sb("a0ka", [65, L])
    V65 = g.sb("a0v", [128, NTI, 65])
    att = g.sb("a0att", [64, L])
    sq64 = g.sb("a0sq", [64, QC])
    crow = g.sb("a0cr", [65, 8])
    rz = g.sb("a0rz", [65, QC])
    zbs = g.sb("a0zb", [64, QC])
    Pb = [g.sb(f"a0p{i}", [128, QC]) for i in range(2)]

    tr.op("vector", lambda e: e.memset(Ka[64:65, :], 1.0), writes=["ka_ones"])

    for gi in range(8):
        for k in range(NT):
            tr.dma("sync", out=wq[:, k, :], in_=w_qkv[k * 128:(k + 1) * 128,
                                                      gi * 128:(gi + 1) * 128],
                   writes=["a0wq"] if k == NT - 1 else ["a0wq_p"])
            tr.dma("scalar", out=wk[:, k, :], in_=w_qkv[k * 128:(k + 1) * 128,
                                                        D + gi * 128:D + (gi + 1) * 128],
                   writes=["a0wk"] if k == NT - 1 else ["a0wk_p"])
            tr.dma("gpsimd", out=wv[:, k, :], in_=w_qkv[k * 128:(k + 1) * 128,
                                                        2 * D + gi * 128:2 * D + (gi + 1) * 128],
                   writes=["a0wv"] if k == NT - 1 else ["a0wv_p"])
        for h2 in range(2):
            tr.dma("sync", out=wo[:, h2, :],
                   in_=w_wo[(2 * gi + h2) * 64:(2 * gi + h2 + 1) * 64, :],
                   writes=["a0wo"] if h2 == 1 else ["a0wo_p"])
        for h2 in range(2):
            h = 2 * gi + h2
            hs = slice(h2 * 64, h2 * 64 + 64)
            for qn, (wbuf, wkkey, dst, dkey, scl, bias_ap) in enumerate([
                    (wq, ["a0wq", "a0wq_p"], Qa, "qa", 0.125, bq8[:, h:h + 1]),
                    (wk, ["a0wk", "a0wk_p"], Ka, "ka", 1.0, w_bqkvh[:, 3 * h + 1:3 * h + 2])]):
                for q in range(NQ):
                    for k in range(NT):
                        tr.op("tensor", lambda e, wbuf=wbuf, hs=hs, k=k, q=q: e.matmul(
                            pb[0][0:64, :], wbuf[:, k, hs], A[k][:, qslice(q)],
                            start=(k == 0), stop=(k == NT - 1)),
                            reads=wkkey + [f"nx{k}"], writes=["pb0"],
                            mark=(k == NT - 1))
                    tr.op("scalar", lambda e, dst=dst, q=q, scl=scl, bias_ap=bias_ap:
                          e.activation(dst[0:64, qslice(q)], pb[0][0:64, :], AF.Copy,
                                       bias=0.0, scale=scl),
                          reads=["pb0"], writes=[dkey + "_s"])
                    tr.op("vector", lambda e, dst=dst, q=q, bias_ap=bias_ap:
                          e.tensor_scalar(dst[0:64, qslice(q)], dst[0:64, qslice(q)],
                                          bias_ap, None, ALU.add),
                          reads=[dkey + "_s", "bq8", "smallsF"], writes=[dkey])
            for tt in range(NTI):
                for k in range(NT):
                    tr.op("tensor", lambda e, tt=tt, k=k, hs=hs: e.matmul(
                        pb[1][:, 0:64], A[k][:, tt * 128:(tt + 1) * 128],
                        wv[:, k, hs], start=(k == 0), stop=(k == NT - 1)),
                        reads=["a0wv", "a0wv_p", f"nx{k}"], writes=["pb1"],
                        mark=(k == NT - 1))
                tr.op("vector", lambda e, tt=tt: e.tensor_copy(
                    V65[:, tt, 0:64], pb[1][:, 0:64]),
                    reads=["pb1"], writes=["v65"])
            tr.op("vector", lambda e: e.memset(V65[:, :, 64:65], 1.0),
                  writes=["v65"])
            for q in range(NQ):
                tr.op("scalar", lambda e, q=q: e.activation(
                    sq64[:, :], Ka[0:64, qslice(q)], AF.Square, bias=0.0, scale=1.0),
                    reads=["ka"], writes=["sq64"])
                tr.op("tensor", lambda e, q=q: e.matmul(
                    pb[2][0:65, :], ones[0:64, 0:65], sq64[:, :],
                    start=True, stop=True),
                    reads=["sq64", "c_ones"], writes=["pb2"])
                tr.op("vector", lambda e, q=q: e.tensor_reduce(
                    crow[64:65, q:q + 1], pb[2][64:65, :], mybir.AxisListType.X,
                    ALU.max),
                    reads=["pb2"], writes=["crow"])
            tr.op("vector", lambda e: e.tensor_reduce(
                crow[64:65, 4:5], crow[64:65, 0:4], mybir.AxisListType.X, ALU.max),
                reads=["crow"], writes=["rk2"])
            for q in range(NQ):
                tr.op("scalar", lambda e, q=q: e.activation(
                    sq64[:, :], Qa[0:64, qslice(q)], AF.Square, bias=0.0, scale=1.0),
                    reads=["qa"], writes=["sq64"])
                tr.op("tensor", lambda e, q=q: e.matmul(
                    pb[2][0:65, :], ones[0:64, 0:65], sq64[:, :],
                    start=True, stop=True),
                    reads=["sq64", "c_ones"], writes=["pb2"])
                tr.op("scalar", lambda e, q=q: e.activation(
                    Qa[64:65, qslice(q)], pb[2][64:65, :], AF.Sqrt, bias=0.0,
                    scale=crow[64:65, 4:5]),
                    reads=["pb2", "rk2", "cpool"], writes=["qa64"])
            tr.op("vector", lambda e: e.tensor_scalar_mul(Qa[64:65, :], Qa[64:65, :], -1.0),
                  reads=["qa64"], writes=["qa64"])
            for q in range(NQ):
                nkc = 4 * q + 4
                pvb = 4 + (q % 2)

                def emit_S(kc, q=q):
                    sb = 2 + (kc % 2)
                    tr.op("tensor", lambda e, kc=kc, q=q, sb=sb: e.matmul(
                        pb[sb][:, :], Ka[:, kc * 128:(kc + 1) * 128], Qa[:, qslice(q)],
                        start=True, stop=True),
                        reads=["ka", "ka_ones", "qa", "qa64"], writes=[f"pb{sb}"])
                    pk = f"a0p{kc % 2}"
                    tr.op("scalar", lambda e, kc=kc, sb=sb: e.activation(
                        Pb[kc % 2][:], pb[sb][:, :], AF.Exp, bias=0.0, scale=1.0),
                        reads=[f"pb{sb}"], writes=[pk])
                    if kc >= 4 * q:
                        j = kc - 4 * q
                        tr.op("vector", lambda e, kc=kc, j=j: e.tensor_mul(
                            Pb[kc % 2][:], Pb[kc % 2][:], masks_sb[:, j, :]),
                            reads=[pk, "masks"], writes=[pk])

                emit_S(0)
                for kc in range(nkc):
                    if kc + 1 < nkc:
                        emit_S(kc + 1)
                    pk = f"a0p{kc % 2}"
                    tr.op("tensor", lambda e, kc=kc, q=q, pvb=pvb: e.matmul(
                        pb[pvb][0:65, :], V65[:, kc, :], Pb[kc % 2][:],
                        start=(kc == 0), stop=(kc == nkc - 1)),
                        reads=[pk, "v65"], writes=[f"pb{pvb}"])
                tr.op("vector", lambda e, q=q, pvb=pvb: e.reciprocal(
                    rz[64:65, :], pb[pvb][64:65, :]), reads=[f"pb{pvb}"], writes=["rz"])
                tr.op("tensor", lambda e, q=q: e.matmul(
                    pb[6][0:64, :], ones[64:65, 0:64], rz[64:65, :],
                    start=True, stop=True),
                    reads=["rz", "c_ones"], writes=["pb6"])
                tr.op("vector", lambda e, q=q: e.tensor_copy(zbs[:, :], pb[6][0:64, :]),
                      reads=["pb6"], writes=["zbs"])
                tr.op("vector", lambda e, q=q, pvb=pvb: e.tensor_mul(
                    att[:, qslice(q)], pb[pvb][0:64, :], zbs[:, :]),
                    reads=[f"pb{pvb}", "zbs"], writes=["att"])
                tr.op("vector", lambda e, q=q, h=h: e.tensor_scalar(
                    att[:, qslice(q)], att[:, qslice(q)],
                    w_bqkvh[:, 3 * h + 2:3 * h + 3], None, ALU.add),
                    reads=["att", "smallsF"], writes=["att"])
            for dm in range(NT):
                for q in range(NQ):
                    wb = 6 + ((dm * NQ + q) % 2)
                    tr.op("tensor", lambda e, dm=dm, q=q, h2=h2, wb=wb: e.matmul(
                        pb[wb][:, :], wo[:, h2, dm * 128:(dm + 1) * 128],
                        att[:, qslice(q)], start=True, stop=True),
                        reads=["att", "a0wo", "a0wo_p"], writes=[f"pb{wb}"])
                    tr.op("vector", lambda e, dm=dm, q=q, wb=wb: e.tensor_add(
                        B[dm][:, qslice(q)], B[dm][:, qslice(q)], pb[wb][:, :]),
                        reads=[f"pb{wb}", f"x1_{dm}"], writes=[f"x1_{dm}"],
                        mark=True)


def emit_ln2_store(tr, g, pb, ones, cpool, B, w_ln2w, w_ln2b, s_nx2):
    tmp = [g.sb(f"l2t{i}", [128, L]) for i in range(2)]
    def post(t):
        tr.dma("gpsimd", out=s_nx2[t, :, :], in_=tmp[t % 2][:],
               reads=[f"nx2t{t % 2}"], writes=[f"s_nx2{t}"])

    emit_layernorm(tr, g, pb, ones, cpool, B, [f"x1_{t}" for t in range(NT)],
                   [tmp[t % 2] for t in range(NT)],
                   [f"nx2t{t % 2}" for t in range(NT)],
                   w_sb=w_ln2w, b_sb=w_ln2b, wkey="smallsF", tag="ln2", post=post)



def emit_moe(tr, g, pb, ones, ident, e8_sb, A, B, w_fc1T, w_fc2T, fc1b, fc2b,
             l0gw, s_nx2):
    nxq_bufs = [g.sb(f"monx{i}", [128, NT, QC]) for i in range(2)]
    wf1 = [g.sb(f"mow1{i}", [128, NT, 256]) for i in range(2)]
    wf2 = [g.sb(f"mow2{i}", [128, D]) for i in range(2)]
    sT8 = g.sb("most", [8, L])
    sbs = g.sb("mosb", [128, QC])
    tmp = g.sb("motmp", [128, QC])
    gsb = g.sb("mogs", [128, 64])
    g8 = gsb[:, 0:8]
    i1 = gsb[:, 8:16]
    mk8 = gsb[:, 16:24]
    i2 = gsb[:, 24:32]
    s8a = gsb[:, 32:40]
    s8f = gsb[:, 40:48]
    gsc = gsb[:, 48:52]

    def H1(ft):
        return A[ft // 4][:, (ft % 4) * QC:(ft % 4 + 1) * QC]

    for e_ in range(NE):
        for qc in range(NQ):
            it = e_ * NQ + qc
            nxq = nxq_bufs[it % 2]
            nk = f"monx{it % 2}"
            for k in range(NT):
                tr.dma("sync", out=nxq[:, k, :], in_=s_nx2[k, :, qslice(qc)],
                       reads=[f"s_nx2{k}"],
                       writes=[nk] if k == NT - 1 else [nk + "_p"])
            if e_ == 0:
                for tt2 in range(4):
                    for k in range(NT):
                        tr.op("tensor", lambda e, k=k, tt2=tt2: e.matmul(
                            pb[0][:, 0:8], nxq[:, k, tt2 * 128:(tt2 + 1) * 128],
                            l0gw[:, k, :], start=(k == 0), stop=(k == NT - 1)),
                            reads=[nk, nk + "_p", "l0gw"], writes=["pb0"],
                            mark=(k == NT - 1))
                    tr.op("vector", lambda e: e.tensor_copy(g8, pb[0][:, 0:8]),
                          reads=["pb0"], writes=["mog"])
                    tr.op("vector", lambda e: e.tensor_reduce(
                        gsc[:, 0:1], g8, mybir.AxisListType.X, ALU.max),
                        reads=["mog"], writes=["mog"])
                    tr.op("vector", lambda e: e.tensor_scalar(
                        i1, g8, gsc[:, 0:1], 0.0, ALU.subtract, ALU.is_ge),
                        reads=["mog"], writes=["mog"])
                    tr.op("vector", lambda e: e.scalar_tensor_tensor(
                        mk8, i1, -1e30, g8, ALU.mult, ALU.add),
                        reads=["mog"], writes=["mog"])
                    tr.op("vector", lambda e: e.tensor_reduce(
                        gsc[:, 1:2], mk8, mybir.AxisListType.X, ALU.max, negate=True),
                        reads=["mog"], writes=["mog"])
                    tr.op("scalar", lambda e: e.activation(
                        gsc[:, 2:3], gsc[:, 0:1], AF.Sigmoid, bias=gsc[:, 1:2],
                        scale=1.0), reads=["mog"], writes=["mog"])
                    tr.op("vector", lambda e: e.tensor_scalar(
                        gsc[:, 3:4], gsc[:, 2:3], -1.0, 1.0, ALU.mult, ALU.add),
                        reads=["mog"], writes=["mog"])
                    tr.op("vector", lambda e: e.tensor_scalar(
                        i2, mk8, gsc[:, 1:2], 0.0, ALU.add, ALU.is_ge),
                        reads=["mog"], writes=["mog"])
                    tr.op("vector", lambda e: e.tensor_scalar(
                        s8a, i1, gsc[:, 2:3], None, ALU.mult),
                        reads=["mog"], writes=["mog"])
                    tr.op("vector", lambda e: e.scalar_tensor_tensor(
                        s8f, i2, gsc[:, 3:4], s8a, ALU.mult, ALU.add),
                        reads=["mog"], writes=["mog2"])
                    tr.op("tensor", lambda e: e.transpose(
                        pb[1][0:8, 0:128], s8f, ident[:]),
                        reads=["mog2", "c_ident"], writes=["pb1"])
                    tr.op("vector", lambda e, qc=qc, tt2=tt2: e.tensor_copy(
                        sT8[0:8, qc * QC + tt2 * 128: qc * QC + (tt2 + 1) * 128],
                        pb[1][0:8, 0:128]),
                        reads=["pb1"], writes=["mosT"])
            tr.op("tensor", lambda e, e_=e_, qc=qc: e.matmul(
                pb[1][:, :], e8_sb[:, e_ * 128:(e_ + 1) * 128], sT8[0:8, qslice(qc)],
                start=True, stop=True),
                reads=["mosT", "c_e8"], writes=["pb1"])
            tr.op("vector", lambda e: e.tensor_copy(sbs[:], pb[1][:, :]),
                  reads=["pb1"], writes=["mosb"])
            for fg in range(16):
                wk = f"mow1{fg % 2}"
                for k in range(NT):
                    tr.dma("sync", out=wf1[fg % 2][:, k, :],
                           in_=w_fc1T[e_, k * 128:(k + 1) * 128,
                                      fg * 256:(fg + 1) * 256],
                           writes=[wk] if k == NT - 1 else [wk + "p"])
                for f2 in range(2):
                    ft = fg * 2 + f2
                    bank = 2 + (ft % 6)
                    for k in range(NT):
                        tr.op("tensor", lambda e, fg=fg, f2=f2, k=k, bank=bank: e.matmul(
                            pb[bank][:, :], wf1[fg % 2][:, k, f2 * 128:(f2 + 1) * 128],
                            nxq[:, k, :], start=(k == 0), stop=(k == NT - 1)),
                            reads=[wk, wk + "p", nk, nk + "_p"],
                            writes=[f"pb{bank}"], mark=(k == NT - 1))
                    tr.op("scalar", lambda e, e_=e_, ft=ft, bank=bank: e.activation(
                        H1(ft), pb[bank][:, :], AF.Gelu,
                        bias=fc1b[:, e_ * 32 + ft:e_ * 32 + ft + 1], scale=1.0),
                        reads=[f"pb{bank}", "smallsF"], writes=[f"h1_{ft}"])
            for ft in range(32):
                wk2 = f"mow2{ft % 2}"
                tr.dma("sync", out=wf2[ft % 2][:],
                       in_=w_fc2T[e_, ft * 128:(ft + 1) * 128, :], writes=[wk2])
                for dm in range(NT):
                    tr.op("tensor", lambda e, ft=ft, dm=dm: e.matmul(
                        pb[dm][:, :], wf2[ft % 2][:, dm * 128:(dm + 1) * 128],
                        H1(ft), start=(ft == 0), stop=(ft == 31)),
                        reads=[wk2, f"h1_{ft}"], writes=[f"pb{dm}"],
                        mark=(dm == NT - 1))
            for dm in range(NT):
                tr.op("vector", lambda e, e_=e_, dm=dm: e.scalar_tensor_tensor(
                    tmp[:], pb[dm][:, :], fc2b[:, e_ * NT + dm:e_ * NT + dm + 1],
                    sbs[:], ALU.add, ALU.mult),
                    reads=[f"pb{dm}", "mosb", "smallsF"], writes=["motmp"])
                tr.op("vector", lambda e, qc=qc, dm=dm: e.tensor_add(
                    B[dm][:, qslice(qc)], B[dm][:, qslice(qc)], tmp[:]),
                    reads=["motmp", f"x1_{dm}"], writes=[f"x1_{dm}"])



def emit_combine_l0(tr, g, pb, B, x_in, s_s1, s_x1r):
    gb = g.sb("cbg", [128, L])
    sb1 = [g.sb(f"cbs{i}", [128, QC]) for i in range(2)]
    xb0 = [g.sb(f"cbx{i}", [128, QC]) for i in range(2)]
    for t in range(NT):
        tr.op("scalar", lambda e, t=t: e.activation(gb[:], B[t][:], AF.Silu,
                                                    bias=0.0, scale=1.0),
              reads=[f"x1_{t}"], writes=["cbg"])
        for q in range(NQ):
            i = (t * NQ + q) % 2
            tr.dma("sync", out=sb1[i][:], in_=s_s1[t, :, qslice(q)],
                   reads=[f"s_s1{t}"], writes=[f"cbs{i}"])
            tr.dma("sync", out=xb0[i][:], in_=x_in[t, :, qslice(q)],
                   writes=[f"cbx{i}"])
            tr.op("vector", lambda e, t=t, q=q, i=i: e.tensor_mul(
                B[t][:, qslice(q)], gb[:, qslice(q)], sb1[i][:]),
                reads=["cbg", f"cbs{i}", f"x1_{t}"], writes=[f"xr{t}", f"x1_{t}"])
            tr.op("vector", lambda e, t=t, q=q, i=i: e.tensor_add(
                B[t][:, qslice(q)], B[t][:, qslice(q)], xb0[i][:]),
                reads=[f"cbx{i}"], writes=[f"xr{t}"])
        tr.dma("gpsimd", out=s_x1r[t, :, :], in_=B[t][:],
               reads=[f"xr{t}"], writes=[f"s_x1r{t}"])


def emit_l1ffn(tr, g, pb, ones, ident, e8_sb, A, B, w_e0T, w_e1T, w_owT,
               l1gw, s_g1):
    l1g = g.sb("l1gw_sb", [128, NT, NE])
    owt = [g.sb(f"l1owt{i}", [128, NT, 128]) for i in range(2)]
    hacc = g.sb("l1h", [128, NT, QC])
    sT2 = g.sb("l1st", [2, L])
    sbs0 = g.sb("l1s0", [128, QC])
    sbs1 = g.sb("l1s1", [128, QC])
    tmp = g.sb("l1tmp", [128, QC])
    g1b = [g.sb(f"l1g1{i}", [128, QC]) for i in range(2)]
    gsb = g.sb("l1gs", [128, 32])
    g8, p8, ge = gsb[:, 0:8], gsb[:, 8:16], gsb[:, 16:24]
    gsc = gsb[:, 24:30]

    def E0v(k):
        return A[k // 2][:, (k % 2) * D:(k % 2 + 1) * D]

    def E1v(k):
        return A[4 + k // 2][:, (k % 2) * D:(k % 2 + 1) * D]

    for k in range(NT):
        tr.dma("sync", out=l1g[:, k, :], in_=l1gw[k, :, :], writes=["l1gw"])
        tr.dma("sync", out=E0v(k), in_=w_e0T[k * 128:(k + 1) * 128, :],
               writes=[f"e0w{k}"])
        tr.dma("sync", out=E1v(k), in_=w_e1T[k * 128:(k + 1) * 128, :],
               writes=[f"e1w{k}"])
    for tt in range(NTI):
        for k in range(NT):
            tr.op("tensor", lambda e, tt=tt, k=k: e.matmul(
                pb[0][:, 0:8], B[k][:, tt * 128:(tt + 1) * 128], l1g[:, k, :],
                start=(k == 0), stop=(k == NT - 1)),
                reads=["l1gw", f"xn1{k}"], writes=["pb0"], mark=(k == NT - 1))
        tr.op("vector", lambda e: e.tensor_copy(g8, pb[0][:, 0:8]),
              reads=["pb0"], writes=["l1g8"])
        tr.op("vector", lambda e: e.tensor_reduce(
            gsc[:, 0:1], g8, mybir.AxisListType.X, ALU.max, negate=True),
            reads=["l1g8"], writes=["l1g8"])
        tr.op("scalar", lambda e: e.activation(
            p8, g8, AF.Exp, bias=gsc[:, 0:1], scale=1.0,
            accum_out=gsc[:, 1:2]), reads=["l1g8"], writes=["l1g8"])
        tr.op("vector", lambda e: e.reciprocal(gsc[:, 2:3], gsc[:, 1:2]),
              reads=["l1g8"], writes=["l1g8"])
        tr.op("vector", lambda e: e.tensor_scalar(
            p8, p8, gsc[:, 2:3], None, ALU.mult),
            reads=["l1g8"], writes=["l1g8"])
        tr.op("vector", lambda e: e.tensor_scalar(
            ge, p8, 0.3, None, ALU.is_gt), reads=["l1g8"], writes=["l1g8"])
        tr.op("vector", lambda e: e.tensor_reduce(
            gsc[:, 3:4], ge, mybir.AxisListType.X, ALU.add),
            reads=["l1g8"], writes=["l1g8"])
        tr.op("vector", lambda e: e.tensor_scalar(
            gsc[:, 4:5], gsc[:, 3:4], -1.0, 8.0, ALU.mult, ALU.add),
            reads=["l1g8"], writes=["l1g8"])
        tr.op("vector", lambda e: e.tensor_mul(
            gsc[:, 4:5], gsc[:, 4:5], p8[:, 0:1]),
            reads=["l1g8"], writes=["l1g8"])
        tr.op("vector", lambda e: e.tensor_mul(
            gsc[:, 5:6], gsc[:, 3:4], p8[:, 1:2]),
            reads=["l1g8"], writes=["l1s01"])
        tr.op("tensor", lambda e: e.transpose(
            pb[1][0:2, 0:128], gsc[:, 4:6], ident[:]),
            reads=["l1s01", "c_ident"], writes=["pb1"])
        tr.op("vector", lambda e, tt=tt: e.tensor_copy(
            sT2[0:2, tt * 128:(tt + 1) * 128], pb[1][0:2, 0:128]),
            reads=["pb1"], writes=["l1sT"])
    for qc in range(NQ):
        tr.op("tensor", lambda e, qc=qc: e.matmul(
            pb[1][:, :], e8_sb[0:2, 0:128], sT2[0:2, qslice(qc)],
            start=True, stop=True), reads=["l1sT", "c_e8"], writes=["pb1"])
        tr.op("vector", lambda e: e.tensor_copy(sbs0[:], pb[1][:, :]),
              reads=["pb1"], writes=["l1sb0"])
        tr.op("tensor", lambda e, qc=qc: e.matmul(
            pb[1][:, :], e8_sb[0:2, 128:256], sT2[0:2, qslice(qc)],
            start=True, stop=True), reads=["l1sT", "c_e8"], writes=["pb1"])
        tr.op("vector", lambda e: e.tensor_copy(sbs1[:], pb[1][:, :]),
              reads=["pb1"], writes=["l1sb1"])
        for ft in range(NT):
            b0 = 2 + (ft % 2)
            for k in range(NT):
                tr.op("tensor", lambda e, ft=ft, k=k, qc=qc, b0=b0: e.matmul(
                    pb[b0][:, :], E0v(k)[:, ft * 128:(ft + 1) * 128],
                    B[k][:, qslice(qc)], start=(k == 0), stop=(k == NT - 1)),
                    reads=[f"e0w{k}", f"xn1{k}"], writes=[f"pb{b0}"],
                    mark=(k == NT - 1))
            tr.op("vector", lambda e, ft=ft, b0=b0: e.tensor_mul(
                hacc[:, ft, :], pb[b0][:, :], sbs0[:]),
                reads=[f"pb{b0}", "l1sb0"], writes=[f"l1h{ft}"])
            b1 = 4 + (ft % 2)
            for k in range(NT):
                tr.op("tensor", lambda e, ft=ft, k=k, qc=qc, b1=b1: e.matmul(
                    pb[b1][:, :], E1v(k)[:, ft * 128:(ft + 1) * 128],
                    B[k][:, qslice(qc)], start=(k == 0), stop=(k == NT - 1)),
                    reads=[f"e1w{k}", f"xn1{k}"], writes=[f"pb{b1}"],
                    mark=(k == NT - 1))
            tr.op("vector", lambda e, ft=ft, b1=b1: e.tensor_mul(
                tmp[:], pb[b1][:, :], sbs1[:]),
                reads=[f"pb{b1}", "l1sb1"], writes=["l1tmp"])
            tr.op("vector", lambda e, ft=ft: e.tensor_add(
                hacc[:, ft, :], hacc[:, ft, :], tmp[:]),
                reads=["l1tmp", f"l1h{ft}"], writes=[f"l1h{ft}"])
        for dm in range(NT):
            b2 = 6 + (dm % 2)
            wkow = f"l1owt{dm % 2}"
            for k in range(NT):
                tr.dma("sync", out=owt[dm % 2][:, k, :],
                       in_=w_owT[k * 128:(k + 1) * 128, dm * 128:(dm + 1) * 128],
                       writes=[wkow] if k == NT - 1 else [wkow + "p"])
            for k in range(NT):
                tr.op("tensor", lambda e, dm=dm, k=k, b2=b2: e.matmul(
                    pb[b2][:, :], owt[dm % 2][:, k, :],
                    hacc[:, k, :], start=(k == 0), stop=(k == NT - 1)),
                    reads=[wkow, wkow + "p", f"l1h{k}"], writes=[f"pb{b2}"],
                    mark=(k == NT - 1))
            gk = f"l1g1{dm % 2}"
            tr.op("scalar", lambda e, dm=dm, b2=b2: e.activation(
                g1b[dm % 2][:], pb[b2][:, :], AF.Silu, bias=0.0, scale=1.0),
                reads=[f"pb{b2}"], writes=[gk])
            tr.dma("gpsimd", out=s_g1[dm, :, qslice(qc)], in_=g1b[dm % 2][:],
                   reads=[gk], writes=[f"s_g1{dm}"])


def emit_attn_l1(tr, g, pb, ones, ident, cpool, A, B, w_dqT, w_dkvT, w_uqT,
                 w_ukT, w_uvT, w_l1wo, smf2, ropec, ropes, s_xn1):
    for t in range(NT):
        tr.dma("gpsimd", out=s_xn1[t, :, :], in_=B[t][:],
               reads=[f"xn1{t}"], writes=[f"s_xn1{t}"])
    Gln = Grp(tr.nc)
    emit_layernorm(tr, Gln, pb, ones, cpool, B,
                   [f"xn1{t}" for t in range(NT)], B,
                   [f"hi{t}" for t in range(NT)],
                   w_sb=smf2[:, 0:NT], b_sb=smf2[:, NT:2 * NT],
                   wkey="smallsF", tag="l1ln")
    tr.barrier()
    Gln.close()
    Gdq = Grp(tr.nc)
    wdq = Gdq.sb("l1wdq", [128, NT, QC])
    wdk = Gdq.sb("l1wdk", [128, NT, QC])
    for k in range(NT):
        tr.dma("sync", out=wdq[:, k, :], in_=w_dqT[k * 128:(k + 1) * 128, :],
               writes=["l1wdq"])
        tr.dma("sync", out=wdk[:, k, :], in_=w_dkvT[k * 128:(k + 1) * 128, :],
               writes=["l1wdk"])
    for ct in range(4):
        for q in range(NQ):
            bk = (ct * NQ + q) % 2
            for k in range(NT):
                tr.op("tensor", lambda e, ct=ct, q=q, k=k, bk=bk: e.matmul(
                    pb[bk][:, :], wdq[:, k, ct * 128:(ct + 1) * 128],
                    B[k][:, qslice(q)], start=(k == 0), stop=(k == NT - 1)),
                    reads=["l1wdq", f"hi{k}"], writes=[f"pb{bk}"],
                    mark=(k == NT - 1))
            tr.op("scalar", lambda e, ct=ct, q=q, bk=bk: e.activation(
                A[ct][:, qslice(q)], pb[bk][:, :], AF.Copy, bias=0.0,
                scale=0.125), reads=[f"pb{bk}"], writes=[f"cq{ct}"])
            bk2 = 2 + (ct * NQ + q) % 2
            for k in range(NT):
                tr.op("tensor", lambda e, ct=ct, q=q, k=k, bk2=bk2: e.matmul(
                    pb[bk2][:, :], wdk[:, k, ct * 128:(ct + 1) * 128],
                    B[k][:, qslice(q)], start=(k == 0), stop=(k == NT - 1)),
                    reads=["l1wdk", f"hi{k}"], writes=[f"pb{bk2}"],
                    mark=(k == NT - 1))
            tr.op("vector", lambda e, ct=ct, q=q, bk2=bk2: e.tensor_copy(
                A[4 + ct][:, qslice(q)], pb[bk2][:, :]),
                reads=[f"pb{bk2}"], writes=[f"ckv{ct}"])
    for t in range(NT):
        tr.dma("sync", out=B[t][:], in_=s_xn1[t, :, :],
               reads=[f"s_xn1{t}"], writes=[f"s1a{t}", f"hi{t}"])
    tr.barrier()
    Gdq.close()
    wuq = g.sb("l1wuq", [128, 4, 128])
    wuk = g.sb("l1wuk", [128, 4, 128])
    wuv = g.sb("l1wuv", [128, 4, 128])
    wo = g.sb("l1wo", [64, D])
    QT = g.sb("l1qt", [128, NTI, 128])
    KT = g.sb("l1kt", [128, NTI, 128])
    rtmp = g.sb("l1rt", [128, 2, 64])
    Qa = g.sb("l1qa", [65, L])
    Ka = g.sb("l1ka", [65, L])
    V65 = g.sb("l1v", [128, NTI, 65])
    att = g.sb("l1att", [64, QC])
    sq64 = g.sb("l1scr64", [64, QC])
    crow = g.sb("l1cr", [65, 8 + QC])
    rz = crow[:, 8:8 + QC]
    zbs = sq64
    Pb = [g.sb(f"l1p{i}", [128, QC]) for i in range(2)]
    tr.op("vector", lambda e: e.memset(Ka[64:65, :], 1.0), writes=["l1kaone"])

    for gi in range(8):
        for c in range(4):
            tr.dma("sync", out=wuq[:, c, :], in_=w_uqT[c * 128:(c + 1) * 128,
                                                       gi * 128:(gi + 1) * 128],
                   writes=["l1wuq"])
            tr.dma("scalar", out=wuk[:, c, :], in_=w_ukT[c * 128:(c + 1) * 128,
                                                         gi * 128:(gi + 1) * 128],
                   writes=["l1wuk"])
            tr.dma("gpsimd", out=wuv[:, c, :], in_=w_uvT[c * 128:(c + 1) * 128,
                                                         gi * 128:(gi + 1) * 128],
                   writes=["l1wuv"])
        for (wbuf, wkkey, dst, dkey, srck) in [
                (wuq, "l1wuq", QT, "l1qt", "cq"), (wuk, "l1wuk", KT, "l1kt", "ckv")]:
            srcA = A if srck == "cq" else None
            for tt in range(NTI):
                for c in range(4):
                    base = 0 if srck == "cq" else 4
                    tr.op("tensor", lambda e, tt=tt, c=c, wbuf=wbuf, base=base: e.matmul(
                        pb[6][:, 0:128], A[base + c][:, tt * 128:(tt + 1) * 128],
                        wbuf[:, c, :], start=(c == 0), stop=(c == 3)),
                        reads=[wkkey, f"{srck}{c}"], writes=["pb6"],
                        mark=(c == 3))
                tr.op("vector", lambda e, tt=tt: e.tensor_mul(
                    rtmp[:, 0, :], pb[6][:, 0:128:2], ropec[:, tt, :]),
                    reads=["pb6", "rope"], writes=["l1r0"])
                tr.op("vector", lambda e, tt=tt: e.tensor_mul(
                    rtmp[:, 1, :], pb[6][:, 1:128:2], ropes[:, tt, :]),
                    reads=["pb6", "rope"], writes=["l1r1"])
                tr.op("vector", lambda e, tt=tt, dst=dst: e.tensor_sub(
                    dst[:, tt, 0:128:2], rtmp[:, 0, :], rtmp[:, 1, :]),
                    reads=["l1r0", "l1r1"], writes=[dkey])
                tr.op("vector", lambda e, tt=tt: e.tensor_mul(
                    rtmp[:, 0, :], pb[6][:, 0:128:2], ropes[:, tt, :]),
                    reads=["pb6", "rope", "l1r0"], writes=["l1r0"])
                tr.op("vector", lambda e, tt=tt: e.tensor_mul(
                    rtmp[:, 1, :], pb[6][:, 1:128:2], ropec[:, tt, :]),
                    reads=["pb6", "rope", "l1r1"], writes=["l1r1"])
                tr.op("vector", lambda e, tt=tt, dst=dst: e.tensor_add(
                    dst[:, tt, 1:128:2], rtmp[:, 0, :], rtmp[:, 1, :]),
                    reads=["l1r0", "l1r1"], writes=[dkey])
        for h2 in range(2):
            h = 2 * gi + h2
            hs = slice(h2 * 64, h2 * 64 + 64)
            tr.dma("scalar", out=wo[:, :],
                   in_=w_l1wo[h * 64:(h + 1) * 64, :], writes=["l1wo"])
            for tt in range(NTI):
                tr.op("tensor", lambda e, tt=tt, hs=hs: e.transpose(
                    pb[6][0:64, 0:128], QT[:, tt, hs], ident[:]),
                    reads=["l1qt", "c_ident"], writes=["pb6"])
                tr.op("vector", lambda e, tt=tt: e.tensor_copy(
                    Qa[0:64, tt * 128:(tt + 1) * 128], pb[6][0:64, 0:128]),
                    reads=["pb6"], writes=["l1qa"])
                tr.op("tensor", lambda e, tt=tt, hs=hs: e.transpose(
                    pb[7][0:64, 0:128], KT[:, tt, hs], ident[:]),
                    reads=["l1kt", "c_ident"], writes=["pb7"])
                tr.op("vector", lambda e, tt=tt: e.tensor_copy(
                    Ka[0:64, tt * 128:(tt + 1) * 128], pb[7][0:64, 0:128]),
                    reads=["pb7"], writes=["l1kad"])
            for tt in range(NTI):
                for c in range(4):
                    tr.op("tensor", lambda e, tt=tt, c=c, hs=hs: e.matmul(
                        pb[1][:, 0:64], A[4 + c][:, tt * 128:(tt + 1) * 128],
                        wuv[:, c, hs], start=(c == 0), stop=(c == 3)),
                        reads=["l1wuv", f"ckv{c}"], writes=["pb1"],
                        mark=(c == 3))
                tr.op("vector", lambda e, tt=tt: e.tensor_copy(
                    V65[:, tt, 0:64], pb[1][:, 0:64]),
                    reads=["pb1"], writes=["l1v65"])
            tr.op("vector", lambda e: e.memset(V65[:, :, 64:65], 1.0),
                  writes=["l1v65"])
            for q in range(NQ):
                tr.op("scalar", lambda e, q=q: e.activation(
                    sq64[:, :], Ka[0:64, qslice(q)], AF.Square, bias=0.0, scale=1.0),
                    reads=["l1kad"], writes=["l1scr"])
                tr.op("tensor", lambda e, q=q: e.matmul(
                    pb[2][0:65, :], ones[0:64, 0:65], sq64[:, :],
                    start=True, stop=True),
                    reads=["l1scr", "c_ones"], writes=["pb2"])
                tr.op("vector", lambda e, q=q: e.tensor_reduce(
                    crow[64:65, q:q + 1], pb[2][64:65, :], mybir.AxisListType.X,
                    ALU.max), reads=["pb2"], writes=["l1crow"])
            tr.op("vector", lambda e: e.tensor_reduce(
                crow[64:65, 4:5], crow[64:65, 0:4], mybir.AxisListType.X, ALU.max),
                reads=["l1crow"], writes=["l1rk2"])
            for q in range(NQ):
                tr.op("scalar", lambda e, q=q: e.activation(
                    sq64[:, :], Qa[0:64, qslice(q)], AF.Square, bias=0.0, scale=1.0),
                    reads=["l1qa"], writes=["l1scr"])
                tr.op("tensor", lambda e, q=q: e.matmul(
                    pb[2][0:65, :], ones[0:64, 0:65], sq64[:, :],
                    start=True, stop=True),
                    reads=["l1scr", "c_ones"], writes=["pb2"])
                tr.op("scalar", lambda e, q=q: e.activation(
                    Qa[64:65, qslice(q)], pb[2][64:65, :], AF.Sqrt, bias=0.0,
                    scale=crow[64:65, 4:5]),
                    reads=["pb2", "l1rk2", "cpool"], writes=["l1qa64"])
            tr.op("vector", lambda e: e.tensor_scalar_mul(
                Qa[64:65, :], Qa[64:65, :], -1.0),
                reads=["l1qa64"], writes=["l1qa64"])
            for q in range(NQ):
                pvb = 4 + (q % 2)

                def emit_S1(kc, q=q):
                    sb = 2 + (kc % 2)
                    tr.op("tensor", lambda e, kc=kc, q=q, sb=sb: e.matmul(
                        pb[sb][:, :], Ka[:, kc * 128:(kc + 1) * 128],
                        Qa[:, qslice(q)], start=True, stop=True),
                        reads=["l1kad", "l1kaone", "l1qa", "l1qa64"],
                        writes=[f"pb{sb}"])
                    pk = f"l1p{kc % 2}"
                    tr.op("scalar", lambda e, kc=kc, sb=sb: e.activation(
                        Pb[kc % 2][:], pb[sb][:, :], AF.Exp, bias=0.0, scale=1.0),
                        reads=[f"pb{sb}"], writes=[pk])

                emit_S1(0)
                for kc in range(NTI):
                    if kc + 1 < NTI:
                        emit_S1(kc + 1)
                    pk = f"l1p{kc % 2}"
                    tr.op("tensor", lambda e, kc=kc, q=q, pvb=pvb: e.matmul(
                        pb[pvb][0:65, :], V65[:, kc, :], Pb[kc % 2][:],
                        start=(kc == 0), stop=(kc == NTI - 1)),
                        reads=[pk, "l1v65"], writes=[f"pb{pvb}"])
                tr.op("vector", lambda e, q=q, pvb=pvb: e.reciprocal(
                    rz[64:65, :], pb[pvb][64:65, :]),
                    reads=[f"pb{pvb}"], writes=["l1rz"])
                tr.op("tensor", lambda e, q=q: e.matmul(
                    pb[6][0:64, :], ones[64:65, 0:64], rz[64:65, :],
                    start=True, stop=True),
                    reads=["l1rz", "c_ones"], writes=["pb6"])
                tr.op("vector", lambda e, q=q: e.tensor_copy(zbs[:, :], pb[6][0:64, :]),
                      reads=["pb6"], writes=["l1scr"])
                tr.op("vector", lambda e, q=q, pvb=pvb: e.tensor_mul(
                    att[:, :], pb[pvb][0:64, :], zbs[:, :]),
                    reads=[f"pb{pvb}", "l1scr"], writes=["l1att"])
                for dm in range(NT):
                    wb = 6 + (dm % 2)
                    tr.op("tensor", lambda e, dm=dm, q=q, wb=wb: e.matmul(
                        pb[wb][:, :], wo[:, dm * 128:(dm + 1) * 128],
                        att[:, :], start=True, stop=True),
                        reads=["l1att", "l1wo"], writes=[f"pb{wb}"])
                    tr.op("vector", lambda e, dm=dm, q=q, wb=wb: e.tensor_add(
                        B[dm][:, qslice(q)], B[dm][:, qslice(q)], pb[wb][:, :]),
                        reads=[f"pb{wb}", f"s1a{dm}"], writes=[f"s1a{dm}"])


def emit_final(tr, g, B, s_x1r, s_g1, y_out):
    xb = [g.sb(f"fnx{i}", [128, QC]) for i in range(2)]
    gb = [g.sb(f"fng{i}", [128, QC]) for i in range(2)]
    ob = [g.sb(f"fno{i}", [128, QC]) for i in range(2)]
    for t in range(NT):
        for q in range(NQ):
            i = (t * NQ + q) % 2
            tr.dma("sync", out=xb[i][:], in_=s_x1r[t, :, qslice(q)],
                   reads=[f"s_x1r{t}"], writes=[f"fnx{i}"])
            tr.dma("sync", out=gb[i][:], in_=s_g1[t, :, qslice(q)],
                   reads=[f"s_g1{t}"], writes=[f"fng{i}"])
            tr.op("vector", lambda e, t=t, q=q, i=i: e.tensor_mul(
                ob[i][:], B[t][:, qslice(q)], gb[i][:]),
                reads=[f"s1a{t}", f"fng{i}"], writes=[f"fno{i}"])
            tr.op("vector", lambda e, t=t, q=q, i=i: e.tensor_add(
                ob[i][:], ob[i][:], xb[i][:]),
                reads=[f"fnx{i}", f"fno{i}"], writes=[f"fno{i}"])
            tr.dma("gpsimd", out=y_out[t, :, qslice(q)], in_=ob[i][:],
                   reads=[f"fno{i}"], writes=[f"y{t}"])


def build_core(upto=99, taps=()):
    nc = bass.Bass("TRN2", target_bir_lowering=False)
    es = contextlib.ExitStack()

    def din(name, shape):
        return nc.dram_tensor(name, shape, F32, kind="ExternalInput").ap()

    x_in = din("x_in", [NT, 128, L])
    w_minT = din("w_minT", [D, 2 * DI])
    w_mxT = din("w_mxT", [NTI, 128, DTR + 2 * DST])
    w_mdtT = din("w_mdtT", [DTR, DI])
    w_moutT = din("w_moutT", [DI, D])
    c_ones = din("c_ones", [128, 128])
    c_ident = din("c_ident", [128, 128])
    c_e32 = din("c_e32", [32, 32 * 128])
    sm_alog = din("sm_alog", [128, NTI * DST])
    sm_d_in = din("sm_d", [128, NTI])
    sm_convb_in = din("sm_convb", [128, NTI])
    sm_convw_in = din("sm_convw", [128, NTI * 4])
    sm_bdt_in = din("sm_bdt", [128, NTI])

    w_qkv = din("w_qkv", [D, 3 * D])
    w_wo = din("w_wo", [D, D])
    c_masks = din("c_masks", [128, 4, QC])
    smf_in = din("smf", [128, 5 * NT])
    bqkvh_in = din("bqkvh", [64, 48])
    l0gw_in = din("l0gw", [NT, 128, NE])
    fc1b_in = din("fc1b", [128, NE * 32])
    fc2b_in = din("fc2b", [128, NE * NT])
    c_e8 = din("c_e8", [8, 8 * 128])
    l1gw_in = din("l1gw", [NT, 128, NE])
    w_e0T = din("w_e0T", [D, D])
    w_e1T = din("w_e1T", [D, D])
    w_owT = din("w_owT", [D, D])
    w_dqT = din("w_dqT", [D, 512])
    w_dkvT = din("w_dkvT", [D, 512])
    w_uqT = din("w_uqT", [512, D])
    w_ukT = din("w_ukT", [512, D])
    w_uvT = din("w_uvT", [512, D])
    w_l1wo = din("w_l1wo", [D, D])
    smf2_in = din("smf2", [128, 2 * NT])
    ropec_in = din("ropec", [NTI, 128, 64])
    ropes_in = din("ropes", [NTI, 128, 64])
    w_fc1T = din("w_fc1T", [NE, D, DFF])
    w_fc2T = din("w_fc2T", [NE, DFF, D])

    s_x = nc.dram_tensor("s_x", [NTI, 128, L], F32).ap()
    s_zs = nc.dram_tensor("s_zs", [NTI, 128, L], F32).ap()
    s_dt = nc.dram_tensor("s_dt", [NTI, 128, L], F32).ap()
    s_s1 = nc.dram_tensor("s_s1", [NT, 128, L], F32).ap()
    s_xn = nc.dram_tensor("s_xn", [NT, 128, L], F32).ap()
    s_bc = nc.dram_tensor("s_bc", [32, L], F32).ap()
    s_nx2 = nc.dram_tensor("s_nx2", [NT, 128, L], F32).ap()
    s_x1r = nc.dram_tensor("s_x1r", [NT, 128, L], F32).ap()
    s_g1 = nc.dram_tensor("s_g1", [NT, 128, L], F32).ap()
    s_xn1 = nc.dram_tensor("s_xn1", [NT, 128, L], F32).ap()

    y_out = nc.dram_tensor("y", [NT, 128, L], F32, kind="ExternalOutput").ap()
    tap_outs = {}

    with es:
        block = es.enter_context(nc.Block())
        tr = Tr(nc, block, es)
        tr.flags = taps
        pb = [es.enter_context(nc.psum_tensor(f"pb{i}", [128, QC], F32))
              for i in range(8)]

        P = Grp(nc)
        ones = P.sb("ones", [128, 128])
        ident = P.sb("ident", [128, 128])
        negA = P.sb("negA", [128, NTI * DST])
        sm_d = P.sb("smd", [128, NTI])
        sm_convb = P.sb("smcb", [128, NTI])
        sm_convw = P.sb("smcw", [128, NTI * 4])
        sm_bdt = P.sb("smbd", [128, NTI])
        cpool = P.sb("cpool", [128, 4])
        A = [P.sb(f"arA{t}", [128, L]) for t in range(NT)]
        PB_holder = {}

        tr.op("vector", lambda e: e.memset(cpool[:, 0:1], EPS), writes=["cpool_a"])
        tr.op("vector", lambda e: e.memset(cpool[:, 1:2], 1.0), writes=["cpool_b"])
        tr.op("vector", lambda e: e.memset(cpool[:, 2:4], 0.0), writes=["cpool"],
              reads=["cpool_a", "cpool_b"])
        tr.dma("sync", out=ones[:], in_=c_ones, writes=["c_ones"])
        tr.dma("sync", out=ident[:], in_=c_ident, writes=["c_ident"])
        tr.dma("sync", out=negA[:], in_=sm_alog, writes=["negA_raw"])
        tr.dma("sync", out=sm_d[:], in_=sm_d_in, writes=["smalls_d"])
        tr.dma("sync", out=sm_convb[:], in_=sm_convb_in, writes=["smalls_cb"])
        tr.dma("sync", out=sm_convw[:], in_=sm_convw_in, writes=["smalls_cw"])
        tr.dma("sync", out=sm_bdt[:], in_=sm_bdt_in, writes=["smalls_bd"])
        tr.op("scalar", lambda e: e.activation(negA[:], negA[:], AF.Exp,
                                               bias=0.0, scale=1.0),
              reads=["negA_raw"], writes=["negA"])
        tr.op("vector", lambda e: e.tensor_scalar_mul(negA[:], negA[:], -1.0),
              reads=["negA"], writes=["negA"])
        tr.op("gpsimd", lambda e: e.tensor_copy(sm_d[:, 0:1], sm_d[:, 0:1]),
              reads=["smalls_d", "smalls_cb", "smalls_cw", "smalls_bd"],
              writes=["smalls"])

        xn = A
        for t in range(NT):
            tr.dma("sync", out=xn[t][:], in_=x_in[t, :, :], writes=[f"xn{t}"])

        G1a = Grp(nc)
        emit_layernorm(tr, G1a, pb, ones, cpool, xn, [f"xn{t}" for t in range(NT)],
                       xn, [f"xn{t}" for t in range(NT)], tag="ln0")
        tr.barrier()
        G1a.close()
        for t in range(NT):
            tr.dma("gpsimd", out=s_xn[t, :, :], in_=xn[t][:],
                   reads=[f"xn{t}"], writes=[f"s_xn{t}"])

        def tap(name, sb, key):
            if name in taps:
                out = nc.dram_tensor("tap_" + name, list(sb.shape), F32,
                                     kind="ExternalOutput").ap()
                tap_outs[name] = True
                tr.dma("gpsimd", out=out, in_=sb[:], reads=[key])

        for t in range(NT):
            tap(f"xn{t}", xn[t], f"xn{t}")

        if upto >= 2:
            G2 = Grp(nc)
            bc_sb = G2.sb("ma_bc", [32, L])
            w_mxT_sb = G2.sb("wmx", [128, NTI, DTR + 2 * DST])
            w_mdtT_sb = G2.sb("wmdt", [DTR, DI])
            for t in range(NTI):
                tr.dma("sync", out=w_mxT_sb[:, t, :], in_=w_mxT[t, :, :],
                       writes=["w_mxT"] if t == NTI - 1 else ["w_mxT_p"])
            tr.dma("sync", out=w_mdtT_sb[:], in_=w_mdtT, writes=["w_mdtT"])
            dtr_sb = emit_mamba_A(tr, G2, pb, cpool, bc_sb, xn, w_minT,
                                  w_mxT_sb, w_mdtT_sb, sm_convw, sm_convb,
                                  sm_bdt, s_x, s_zs, s_dt)
            tr.dma("gpsimd", out=s_bc, in_=bc_sb[:], reads=["ma_bc"],
                   writes=["s_bc"])
            tap("dtr", dtr_sb, "ma_dtr")
            tap("bc", bc_sb, "ma_bc")

        if upto >= 3:
            tr.barrier()
            G2.close()
            PB = Grp(nc)
            B = [PB.sb(f"arB{t}", [128, L]) for t in range(NT)]
            PB_holder["B"] = B
            G3 = Grp(nc)
            e32_sb = G3.sb("e32", [32, 32 * 128])
            bc_sb = G3.sb("ma_bc3", [32, L])
            tr.dma("sync", out=e32_sb[:], in_=c_e32, writes=["c_e32"])
            tr.dma("sync", out=bc_sb[:], in_=s_bc, reads=["s_bc"],
                   writes=["ma_bc"])
            hcar_ref = []
            emit_mamba_scan(tr, G3, pb, A, B, e32_sb, bc_sb, negA, sm_d,
                            w_moutT, s_x, s_dt, s_zs, s_s1, hcar_ref)
            stp_holder = {}

            def tap_hbm(base, dram, n, keyf):
                if base not in taps:
                    return
                if "stp" not in stp_holder:
                    stp_holder["stp"] = G3.sb("stapx", [128, QC])
                stp = stp_holder["stp"]
                for j in range(n):
                    out = nc.dram_tensor(f"tap_{base}_{j}", [128, L], F32,
                                         kind="ExternalOutput").ap()
                    for q in range(NQ):
                        tr.dma("sync", out=stp[:], in_=dram[j, :, qslice(q)],
                               reads=[keyf(j)], writes=["stap" + base])
                        tr.dma("gpsimd", out=out[:, qslice(q)], in_=stp[:],
                               reads=["stap" + base], writes=["stapr" + base])
            tap_hbm("mx", s_x, NTI, lambda j: f"s_x{j}")
            tap_hbm("mzs", s_zs, NTI, lambda j: f"s_zs{j}")
            tap_hbm("mdt", s_dt, NTI, lambda j: f"s_dt{j}")
            tap_hbm("s1", s_s1, NT, lambda j: f"s_s1{j}")
            if "hcar" in taps:
                outh = nc.dram_tensor("tap_hcar", [128, NTI * DST], F32,
                                      kind="ExternalOutput").ap()
                tr.dma("gpsimd", out=outh, in_=hcar_ref[0][:], reads=["schc"])
            if "ylast" in taps:
                for j in range(4):
                    out = nc.dram_tensor(f"tap_ylast_{j}", [128, L], F32,
                                         kind="ExternalOutput").ap()
                    tr.dma("gpsimd", out=out, in_=B[j][:],
                           reads=[f"scy{4*j}", f"scy{4*j+1}", f"scy{4*j+2}", f"scy{4*j+3}"])
            if "bclast" in taps:
                for j in range(4):
                    out = nc.dram_tensor(f"tap_bclast_{j}", [128, L], F32,
                                         kind="ExternalOutput").ap()
                    tr.dma("gpsimd", out=out, in_=A[j][:],
                           reads=[f"scb{4*j}", f"scb{4*j+1}", f"scb{4*j+2}", f"scb{4*j+3}"])

        if upto >= 4:
            tr.barrier()
            G3.close()
            B = PB_holder["B"]
            PC = Grp(nc)
            smf = PC.sb("smf_sb", [128, 5 * NT])
            bqkvh = PC.sb("bqkvh_sb", [64, 48])
            bq8 = PC.sb("bq8", [64, 16])

            l0gw = PC.sb("l0gw_sb", [128, NT, NE])
            fc1b = PC.sb("fc1b_sb", [128, NE * 32])
            fc2b = PC.sb("fc2b_sb", [128, NE * NT])
            smf2 = PC.sb("smf2_sb", [128, 2 * NT])
            tr.dma("sync", out=smf[:], in_=smf_in, writes=["smallsF"])
            tr.dma("sync", out=bqkvh[:], in_=bqkvh_in, writes=["smallsF"])

            for k in range(NT):
                tr.dma("sync", out=l0gw[:, k, :], in_=l0gw_in[k, :, :],
                       writes=["l0gw"])
            tr.dma("sync", out=fc1b[:], in_=fc1b_in, writes=["smallsF"])
            tr.dma("sync", out=fc2b[:], in_=fc2b_in, writes=["smallsF"])
            tr.dma("sync", out=smf2[:], in_=smf2_in, writes=["smallsF"])
            w_ln1w, w_ln1b = smf[:, 0:NT], smf[:, NT:2 * NT]
            w_ln2w, w_ln2b = smf[:, 2 * NT:3 * NT], smf[:, 3 * NT:4 * NT]
            w_bo = smf[:, 4 * NT:5 * NT]
            G4 = Grp(nc)
            emit_ffn_prologue(tr, G4, pb, ones, cpool, A, B, s_xn,
                              w_ln1w, w_ln1b, w_bo, bqkvh, bq8)
            for t in range(NT):
                tap(f"nxf{t}", A[t], f"nx{t}")

        if upto >= 5:
            tr.barrier()
            G4.close()
            G5 = Grp(nc)
            masks_sb = G5.sb("masks", [128, 4, QC])
            tr.dma("sync", out=masks_sb[:], in_=c_masks, writes=["masks"])
            emit_attn_l0(tr, G5, pb, ones, A, B, w_qkv, w_wo, bqkvh, bq8,
                         masks_sb)
            for t in range(NT):
                tap(f"x1_{t}", B[t], f"x1_{t}")

        if upto >= 6:
            tr.barrier()
            G5.close()
            G6 = Grp(nc)
            emit_ln2_store(tr, G6, pb, ones, cpool, B, w_ln2w, w_ln2b, s_nx2)
            tr.barrier()
            G6.close()

        if upto >= 7:
            G7 = Grp(nc)
            e8_sb = G7.sb("e8_sb", [8, 8 * 128])
            tr.dma("sync", out=e8_sb[:], in_=c_e8, writes=["c_e8"])
            emit_moe(tr, G7, pb, ones, ident, e8_sb, A, B, w_fc1T, w_fc2T,
                     fc1b, fc2b, l0gw, s_nx2)
            for t in range(NT):
                tap(f"ffn{t}", B[t], f"x1_{t}")

        if upto >= 8:
            tr.barrier()
            G7.close()
            G8 = Grp(nc)
            emit_combine_l0(tr, G8, pb, B, x_in, s_s1, s_x1r)
            for t in range(NT):
                tap(f"xr{t}", B[t], f"xr{t}")

        if upto >= 9:
            tr.barrier()
            G8.close()
            G9 = Grp(nc)
            emit_layernorm(tr, G9, pb, ones, cpool, B,
                           [f"xr{t}" for t in range(NT)], B,
                           [f"xn1{t}" for t in range(NT)], tag="ln9")
            tr.barrier()
            G9.close()

        if upto >= 10:
            G10 = Grp(nc)
            e8b = G10.sb("e8b", [8, 8 * 128])
            tr.dma("sync", out=e8b[:], in_=c_e8, writes=["c_e8"])
            emit_l1ffn(tr, G10, pb, ones, ident, e8b, A, B, w_e0T, w_e1T,
                       w_owT, l1gw_in, s_g1)

        if upto >= 11:
            tr.barrier()
            G10.close()
            G11 = Grp(nc)
            ropec_sb = G11.sb("ropec_sb", [128, NTI, 64])
            ropes_sb = G11.sb("ropes_sb", [128, NTI, 64])
            for t in range(NTI):
                tr.dma("sync", out=ropec_sb[:, t, :], in_=ropec_in[t, :, :],
                       writes=["rope"])
                tr.dma("sync", out=ropes_sb[:, t, :], in_=ropes_in[t, :, :],
                       writes=["rope"])
            emit_attn_l1(tr, G11, pb, ones, ident, cpool, A, B, w_dqT, w_dkvT,
                         w_uqT, w_ukT, w_uvT, w_l1wo, smf2, ropec_sb, ropes_sb,
                         s_xn1)
            for t in range(NT):
                tap(f"s1a{t}", B[t], f"s1a{t}")

        if upto >= 12:
            tr.barrier()
            G11.close()
            G12 = Grp(nc)
            emit_final(tr, G12, B, s_x1r, s_g1, y_out)

        tr.finish()
    return nc


def _prep_common(i):
    c = {}
    A = np.ascontiguousarray
    c["w_minT"] = A(i["L0_attn_W_in"].T)
    c["w_mxT"] = A(i["L0_attn_W_x"].T.reshape(NTI, 128, 96))
    c["w_mdtT"] = A(i["L0_attn_W_dt"].T)
    c["w_moutT"] = A(i["L0_attn_W_out"].T)
    c["c_ones"] = np.ones((128, 128), np.float32)
    c["c_ident"] = np.eye(128, dtype=np.float32)
    e32 = np.zeros((32, 32, 128), np.float32)
    for j in range(32):
        e32[j, j, :] = 1.0
    c["c_e32"] = A(e32.reshape(32, 32 * 128))
    c["sm_alog"] = A(i["L0_attn_A_log"].reshape(NTI, 128, DST)
                     .transpose(1, 0, 2).reshape(128, NTI * DST))
    c["sm_d"] = A(i["L0_attn_D"].reshape(NTI, 128).T)
    c["sm_convb"] = A(i["L0_attn_conv_b"].reshape(NTI, 128).T)
    c["sm_convw"] = A(i["L0_attn_conv_w"][:, 0, :].reshape(NTI, 128, 4)
                      .transpose(1, 0, 2).reshape(128, NTI * 4))
    c["sm_bdt"] = A(i["L0_attn_b_dt"].reshape(NTI, 128).T)
    c["w_qkv"] = A(i["L0_ffn_Wqkv"])
    c["w_wo"] = A(i["L0_ffn_Wo"])
    m = np.zeros((128, 4, QC), np.float32)
    for j in range(4):
        for kl in range(128):
            m[kl, j, :] = (np.arange(QC) >= kl + 128 * j).astype(np.float32)
    c["c_masks"] = m
    smf = np.zeros((128, 5 * NT), np.float32)
    for idx, nm in enumerate(["L0_ffn_ln1_w", "L0_ffn_ln1_b", "L0_ffn_ln2_w",
                              "L0_ffn_ln2_b", "L0_ffn_bo"]):
        smf[:, idx * NT:(idx + 1) * NT] = i[nm].reshape(NT, 128).T
    c["smf"] = smf
    c["bqkvh"] = A(i["L0_ffn_bqkv"].reshape(3, 16, 64).transpose(2, 1, 0)
                   .reshape(64, 48))
    c["l0gw"] = A(i["L0_ffn_gate_W"].reshape(NT, 128, NE))
    c["fc1b"] = A(i["L0_ffn_fc1_b"].reshape(NE, 32, 128).transpose(2, 0, 1)
                  .reshape(128, NE * 32))
    c["fc2b"] = A(i["L0_ffn_fc2_b"].reshape(NE, NT, 128).transpose(2, 0, 1)
                  .reshape(128, NE * NT))
    e8m = np.zeros((8, 8, 128), np.float32)
    for j in range(8):
        e8m[j, j, :] = 1.0
    c["c_e8"] = A(e8m.reshape(8, 1024))
    c["w_fc1T"] = A(i["L0_ffn_fc1_w"].transpose(0, 2, 1))
    c["w_fc2T"] = A(i["L0_ffn_fc2_w"].transpose(0, 2, 1))
    c["l1gw"] = A(i["L1_ffn_gate_W"].T.reshape(NT, 128, NE))
    c["w_e0T"] = A(i["L1_ffn_expert_W"][0].T)
    c["w_e1T"] = A(i["L1_ffn_expert_W"][1].T)
    c["w_owT"] = A(i["L1_ffn_out_W"].T)
    c["w_dqT"] = A(i["L1_attn_w_down_q"].T)
    c["w_dkvT"] = A(i["L1_attn_w_down_kv"].T)
    c["w_uqT"] = A(i["L1_attn_w_up_q"].T)
    c["w_ukT"] = A(i["L1_attn_w_up_k"].T)
    c["w_uvT"] = A(i["L1_attn_w_up_v"].T)
    c["w_l1wo"] = A(i["L1_attn_w_out"].T)
    smf2 = np.zeros((128, 2 * NT), np.float32)
    smf2[:, 0:NT] = i["L1_attn_ln_scale"].reshape(NT, 128).T
    smf2[:, NT:2 * NT] = i["L1_attn_ln_bias"].reshape(NT, 128).T
    c["smf2"] = smf2
    inv_freq = 1.0 / 10000.0 ** (np.arange(0, 64, 2, dtype=np.float32) / 64)
    tpos = np.arange(L, dtype=np.float32)
    freqs = np.outer(tpos, inv_freq)
    cos_t = np.concatenate([np.cos(freqs), np.cos(freqs)], -1)[:, ::2]
    sin_t = np.concatenate([np.sin(freqs), np.sin(freqs)], -1)[:, ::2]
    c["ropec"] = A(np.concatenate([cos_t, cos_t], 1).reshape(NTI, 128, 64))
    c["ropes"] = A(np.concatenate([sin_t, sin_t], 1).reshape(NTI, 128, 64))
    return {k: np.asarray(v, np.float32) for k, v in c.items()}


def _per_core(i, b):
    x = np.asarray(i["L0_attn_hidden_states"][b], np.float32)
    return {"x_in": np.ascontiguousarray(x.T.reshape(NT, 128, L))}


def kernel(**inputs):
    common = _prep_common(inputs)
    nc = build_core(upto=99)
    in_maps = []
    for b in range(8):
        m = dict(common)
        m.update(_per_core(inputs, b))
        in_maps.append(m)
    res = run_bass_kernel_spmd(nc, in_maps, core_ids=list(range(8)))
    outs = []
    for b in range(8):
        y = res.results[b]["y"].reshape(D, L)
        outs.append(y.T)
    return np.stack(outs).astype(np.float32)


# version 05
# speedup vs baseline: 1.2574x; 1.2574x over previous
import contextlib
import numpy as np
import concourse.bass as bass
import concourse.mybir as mybir
from concourse.bass_utils import run_bass_kernel_spmd
from concourse.bass import IndirectOffsetOnAxis

F32 = mybir.dt.float32
AF = mybir.ActivationFunctionType
ALU = mybir.AluOpType

L = 2048
D = 1024
DI = 2048
DST = 16
DTR = 64
NH, HD = 16, 64
DFF = 4096
NE = 8
NT = D // 128
NTI = DI // 128
QC = 512
NQ = L // QC
EPS = 1e-5

ENGS = ("tensor", "vector", "scalar", "sync", "gpsimd")
DMAQ = {"sync": "syncq", "gpsimd": "gpsq", "scalar": "scalq", "vector": "vecq"}
QLANES = frozenset(DMAQ.values())


class Tr:

    def __init__(self, nc, block, es):
        self.nc = nc
        self.block = block
        self.es = es
        self.sems = {}
        self.cnt = {}
        self.wm = {}
        self.wr = {}
        self.rd = {}
        self.pend = {}
        self.nwaits = 0
        self.nops = 0
        for e in ENGS:
            self.pend[e] = []

    def _sem(self, lane):
        if lane not in self.sems:
            self.sems[lane] = self.es.enter_context(self.nc.semaphore(lane))
            self.cnt[lane] = 0
        return self.sems[lane]

    def _wait(self, eng_name, eng, lane, val):
        if val <= 0:
            return
        if lane.startswith("F_"):
            lane = "P_" + lane[2:]
        key = (eng_name, lane)
        if self.wm.get(key, 0) >= val:
            return
        eng.wait_ge(self._sem(lane), val)
        self.wm[key] = val
        self.nwaits += 1

    def _lane_wait(self, eng_name, eng, lane, val, kind, k):
        if lane in QLANES:
            val = self.cnt[lane]
        elif val is None:
            raise RuntimeError(f"{kind} on {k!r} pending unmarked event")
        self._wait(eng_name, eng, lane, val)

    def _deps(self, eng_name, eng, reads, writes, own_p=True):
        own = ("P_" + eng_name,) if own_p else ()
        for k in reads:
            w = self.wr.get(k)
            if w is not None and w[0] not in own:
                self._lane_wait(eng_name, eng, w[0], w[1], "read", k)
        for k in writes:
            w = self.wr.get(k)
            if w is not None and w[0] not in own:
                self._lane_wait(eng_name, eng, w[0], w[1], "WAW", k)
            for lane, val in self.rd.get(k, ()):
                if lane in own:
                    continue
                self._lane_wait(eng_name, eng, lane, val, "WAR", k)

    def op(self, eng_name, fn, reads=(), writes=(), mark=True, fenced=False):
        if fenced:
            assert mark
        sec = getattr(self.block, eng_name)
        lane = "P_" + eng_name
        self._sem(lane)
        tr = self

        @sec
        def _(eng):
            tr._deps(eng_name, eng, reads, writes)
            inst = fn(eng)
            if mark:
                inst.then_inc(tr._sem(lane), 1)
        if mark:
            self.cnt[lane] += 1
            val = self.cnt[lane]
            for k in self.pend[eng_name]:
                lw = self.wr.get(k)
                if lw is not None and lw[0] == lane and lw[1] is None:
                    self.wr[k] = (lane, val)
                rl = self.rd.get(k)
                if rl:
                    self.rd[k] = [(ln, val if (ln == lane and v is None) else v)
                                  for ln, v in rl]
            self.pend[eng_name] = []
            ev = (("F_" + eng_name) if fenced else lane, val)
        else:
            ev = (lane, None)
        for k in writes:
            self.wr[k] = ev
            self.rd[k] = []
            if not mark:
                self.pend[eng_name].append(k)
        for k in reads:
            self.rd.setdefault(k, []).append(ev)
            if not mark and k not in self.pend[eng_name]:
                self.pend[eng_name].append(k)
        self.nops += 1

    def dma(self, eng_name, out, in_, reads=(), writes=()):
        sec = getattr(self.block, eng_name)
        qlane = DMAQ[eng_name]
        self._sem(qlane)
        tr = self

        @sec
        def _(eng):
            tr._deps(eng_name, eng, reads, writes, own_p=False)
            inst = eng.dma_start(out=out, in_=in_)
            inst.then_inc(tr._sem(qlane), 16)
        self.cnt[qlane] += 16
        ev = (qlane, self.cnt[qlane])
        for k in writes:
            self.wr[k] = ev
            self.rd[k] = []
        for k in reads:
            self.rd.setdefault(k, []).append(ev)
        self.nops += 1

    def idma(self, fn, reads=(), writes=()):
        qlane = "gpsq"
        self._sem(qlane)
        tr = self

        @self.block.gpsimd
        def _(eng):
            tr._deps("gpsimd", eng, reads, writes, own_p=False)
            inst = fn(eng)
            inst.then_inc(tr._sem(qlane), 16)
        self.cnt[qlane] += 16
        ev = (qlane, self.cnt[qlane])
        for k in writes:
            self.wr[k] = ev
            self.rd[k] = []
        for k in reads:
            self.rd.setdefault(k, []).append(ev)
        self.nops += 1

    def barrier(self):
        vals = {ln: self.cnt.get(ln, 0) for ln in list(self.cnt)}
        for e in ENGS:
            lane = "P_" + e
            self._sem(lane)
            sec = getattr(self.block, e)
            tr = self
            myv = dict(vals)

            @sec
            def _(eng, e=e, myv=myv):
                for ln, v in myv.items():
                    if ln != "P_" + e:
                        tr._wait(e, eng, ln, v)
                eng.sem_inc(tr._sem("P_" + e), 1)
            self.cnt[lane] += 1
        for k in list(self.wr):
            lane, val = self.wr[k]
            if val is None:
                raise RuntimeError(f"barrier with pending unmarked write on {k!r}")
        for e in ENGS:
            for ln in list(self.cnt):
                if ln != "P_" + e:
                    self.wm[(e, ln)] = max(self.wm.get((e, ln), 0), vals.get(ln, 0))

    def finish(self):
        tr = self

        @self.block.gpsimd
        def _(gp):
            for ln in ("syncq", "gpsq", "scalq", "vecq"):
                if ln in tr.sems:
                    tr._wait("gpsimd", gp, ln, tr.cnt[ln])
            for e in ENGS:
                ln = "P_" + e
                if ln in tr.sems and e != "gpsimd":
                    tr._wait("gpsimd", gp, ln, tr.cnt[ln])


class Grp:
    def __init__(self, nc):
        self.nc = nc
        self.es = contextlib.ExitStack()

    def sb(self, name, shape, dtype=F32):
        return self.es.enter_context(self.nc.sbuf_tensor(name, shape, dtype))

    def close(self):
        self.es.close()


def qslice(q):
    return slice(q * QC, (q + 1) * QC)


def emit_layernorm(tr, g, pb, ones, cpool, in_tiles, in_keys, out_tiles, out_keys,
                   w_sb=None, b_sb=None, wkey=None, tag="ln", post=None):
    sq = g.sb(tag + "sq", [128, L])
    rowm = g.sb(tag + "rm", [1, L])
    rowr = g.sb(tag + "rr", [1, L])
    bnm = g.sb(tag + "bm", [128, L])
    brs = g.sb(tag + "br", [128, L])
    K = tag
    for t in range(NT):
        tr.op("scalar", lambda e, t=t: e.activation(sq[:], in_tiles[t][:], AF.Square,
                                                    bias=0.0, scale=1.0),
              reads=[in_keys[t]], writes=[K + "sq"])
        for q in range(NQ):
            tr.op("tensor", lambda e, t=t, q=q: e.matmul(
                pb[q][0:1, :], ones[:, 0:1], in_tiles[t][:, qslice(q)],
                start=(t == 0), stop=(t == NT - 1)),
                reads=[in_keys[t], "c_ones"], writes=[f"pb{q}"], mark=False)
            tr.op("tensor", lambda e, t=t, q=q: e.matmul(
                pb[4 + q][0:1, :], ones[:, 0:1], sq[:, qslice(q)],
                start=(t == 0), stop=(t == NT - 1)),
                reads=[K + "sq", "c_ones"], writes=[f"pb{4 + q}"],
                mark=(q == NQ - 1))
    for q in range(NQ):
        tr.op("vector", lambda e, q=q: e.tensor_scalar_mul(
            rowm[:, qslice(q)], pb[q][0:1, :], -1.0 / D),
            reads=[f"pb{q}"], writes=[K + "rm"], mark=(q == NQ - 1))
        tr.op("vector", lambda e, q=q: e.tensor_scalar_mul(
            rowr[:, qslice(q)], pb[4 + q][0:1, :], 1.0 / D),
            reads=[f"pb{4 + q}"], writes=[K + "rr"], mark=(q == NQ - 1))
    tr.op("vector", lambda e: e.tensor_mul(sq[0:1, 0:L], rowm[:], rowm[:]),
          reads=[K + "rm"], writes=[K + "sq"])
    tr.op("vector", lambda e: e.tensor_sub(rowr[:], rowr[:], sq[0:1, 0:L]),
          reads=[K + "sq", K + "rr"], writes=[K + "rr"])
    tr.op("scalar", lambda e: e.activation(rowr[:], rowr[:], AF.Sqrt, bias=cpool[0:1, 0:1], scale=1.0),
          reads=[K + "rr", "cpool"], writes=[K + "rr"])
    tr.op("vector", lambda e: e.reciprocal(rowr[:], rowr[:]),
          reads=[K + "rr"], writes=[K + "rr"])
    for q in range(NQ):
        tr.op("tensor", lambda e, q=q: e.matmul(pb[q][:, :], ones[0:1, :],
                                                rowm[:, qslice(q)], start=True, stop=True),
              reads=[K + "rm", "c_ones"], writes=[f"pb{q}"])
        tr.op("tensor", lambda e, q=q: e.matmul(pb[4 + q][:, :], ones[0:1, :],
                                                rowr[:, qslice(q)], start=True, stop=True),
              reads=[K + "rr", "c_ones"], writes=[f"pb{4 + q}"])
        tr.op("vector", lambda e, q=q: e.tensor_copy(bnm[:, qslice(q)], pb[q][:, :]),
              reads=[f"pb{q}"], writes=[K + "bm"], mark=(q == NQ - 1))
        tr.op("vector", lambda e, q=q: e.tensor_copy(brs[:, qslice(q)], pb[4 + q][:, :]),
              reads=[f"pb{4 + q}"], writes=[K + "br"], mark=(q == NQ - 1))
    for t in range(NT):
        tr.op("vector", lambda e, t=t: e.tensor_add(sq[:], in_tiles[t][:], bnm[:]),
              reads=[in_keys[t], K + "bm"], writes=[K + "sq"])
        if w_sb is None:
            tr.op("vector", lambda e, t=t: e.tensor_mul(out_tiles[t][:], sq[:], brs[:]),
                  reads=[K + "sq", K + "br"], writes=[out_keys[t]])
        else:
            tr.op("vector", lambda e, t=t: e.tensor_mul(sq[:], sq[:], brs[:]),
                  reads=[K + "sq", K + "br"], writes=[K + "sq"])
            tr.op("vector", lambda e, t=t: e.tensor_scalar(
                out_tiles[t][:], sq[:], w_sb[:, t:t + 1], b_sb[:, t:t + 1],
                ALU.mult, ALU.add),
                reads=[K + "sq", wkey], writes=[out_keys[t]])
        if post is not None:
            post(t)


def emit_mamba_A(tr, g, pb, cpool, bc_sb, xn, w_minT, w_mxT_sb, w_mdtT_sb,
                 sm_convw, sm_convb, sm_bdt, s_x, s_zs, s_dt):
    wbuf = [g.sb(f"maw{i}", [128, NT, 128]) for i in range(2)]
    xpad = g.sb("maxp", [128, L + 3])
    cb0 = g.sb("mac0", [128, L])
    cb1 = g.sb("mac1", [128, L])
    cb2 = g.sb("mac2", [128, L])
    xfin = [g.sb(f"maxf{i}", [128, L]) for i in range(2)]
    zbuf = cb0
    zs2 = [cb1, cb2]
    dtr_sb = g.sb("ma_dtr", [64, L])
    sp_sb = g.sb("masp", [128, L])
    dt_sb = [g.sb(f"madt{i}", [128, L]) for i in range(2)]

    tr.op("vector", lambda e: e.memset(xpad[:, 0:3], 0.0), writes=["maxpad_z"])

    for e_ in range(32):
        wk = f"maw{e_ % 2}"
        for k in range(NT):
            tr.dma("sync", out=wbuf[e_ % 2][:, k, :],
                   in_=w_minT[k * 128:(k + 1) * 128, e_ * 128:(e_ + 1) * 128],
                   writes=[wk] if k == NT - 1 else [wk + "part"])
        for q in range(NQ):
            for k in range(NT):
                tr.op("tensor", lambda e, e_=e_, q=q, k=k: e.matmul(
                    pb[q][:, :], wbuf[e_ % 2][:, k, :], xn[k][:, qslice(q)],
                    start=(k == 0), stop=(k == NT - 1)),
                    reads=[wk, wk + "part", f"xn{k}"], writes=[f"pb{q}"],
                    mark=(q == NQ - 1 and k == NT - 1))
        if e_ < 16:
            for q in range(NQ):
                tr.op("vector", lambda e, q=q: e.tensor_scalar(
                    xpad[:, 3 + q * QC:3 + (q + 1) * QC], pb[q][:, :],
                    0.0, 1.0, ALU.max, ALU.min),
                    reads=[f"pb{q}", "maxpad_z"], writes=["maxp"], mark=(q == NQ - 1))
            tr.op("vector", lambda e, e_=e_: e.tensor_scalar(
                cb0[:], xpad[:, 0:L], sm_convw[:, 4 * e_:4 * e_ + 1], None, ALU.mult),
                reads=["maxp", "smalls"], writes=["mac0"])
            tr.op("vector", lambda e, e_=e_: e.scalar_tensor_tensor(
                cb1[:], xpad[:, 1:1 + L], sm_convw[:, 4 * e_ + 1:4 * e_ + 2], cb0[:],
                ALU.mult, ALU.add), reads=["maxp", "mac0", "smalls"], writes=["mac1"])
            tr.op("vector", lambda e, e_=e_: e.scalar_tensor_tensor(
                cb0[:], xpad[:, 2:2 + L], sm_convw[:, 4 * e_ + 2:4 * e_ + 3], cb1[:],
                ALU.mult, ALU.add), reads=["maxp", "mac1", "smalls"], writes=["mac0"])
            tr.op("vector", lambda e, e_=e_: e.scalar_tensor_tensor(
                cb1[:], xpad[:, 3:3 + L], sm_convw[:, 4 * e_ + 3:4 * e_ + 4], cb0[:],
                ALU.mult, ALU.add), reads=["maxp", "mac0", "smalls"], writes=["mac1"])
            tr.op("scalar", lambda e, e_=e_: e.activation(
                cb2[:], cb1[:], AF.Silu, bias=sm_convb[:, e_:e_ + 1], scale=1.0),
                reads=["mac1", "smalls"], writes=["mac2"])
            xk = f"maxf{e_ % 2}"
            tr.op("vector", lambda e, e_=e_: e.tensor_scalar(
                xfin[e_ % 2][:], cb2[:], 0.0, 1.0, ALU.max, ALU.min),
                reads=["mac2"], writes=[xk])
            tr.dma("gpsimd", out=s_x[e_, :, :], in_=xfin[e_ % 2][:],
                   reads=[xk], writes=[f"s_x{e_}"])
            for q in range(NQ):
                tr.op("tensor", lambda e, e_=e_, q=q: e.matmul(
                    pb[4 + q][0:96, :], w_mxT_sb[:, e_, :], xfin[e_ % 2][:, qslice(q)],
                    start=(e_ == 0), stop=(e_ == 15)),
                    reads=[xk, "w_mxT"], writes=[f"pb{4 + q}"],
                    mark=(q == NQ - 1))
        else:
            zk = "mac1" if e_ % 2 == 0 else "mac2"
            for q in range(NQ):
                tr.op("vector", lambda e, q=q: e.tensor_scalar(
                    zbuf[:, qslice(q)], pb[q][:, :], 0.0, 1.0, ALU.max, ALU.min),
                    reads=[f"pb{q}"], writes=["mac0"], mark=(q == NQ - 1))
            tr.op("scalar", lambda e, e_=e_: e.activation(
                zs2[e_ % 2][:], zbuf[:], AF.Silu, bias=0.0, scale=1.0),
                reads=["mac0"], writes=[zk])
            tr.dma("gpsimd", out=s_zs[e_ - 16, :, :], in_=zs2[e_ % 2][:],
                   reads=[zk], writes=[f"s_zs{e_ - 16}"])

    for q in range(NQ):
        tr.op("vector", lambda e, q=q: e.tensor_scalar(
            dtr_sb[:, qslice(q)], pb[4 + q][0:64, :], 0.0, 1.0, ALU.max, ALU.min),
            reads=[f"pb{4 + q}"], writes=["ma_dtr"], mark=(q == NQ - 1))
        tr.op("vector", lambda e, q=q: e.tensor_copy(
            bc_sb[:, qslice(q)], pb[4 + q][64:96, :]),
            reads=[f"pb{4 + q}"], writes=["ma_bc"], mark=(q == NQ - 1))

    for t in range(NTI):
        for q in range(NQ):
            bank = 4 + ((t * NQ + q) % 4)
            tr.op("tensor", lambda e, t=t, q=q, bank=bank: e.matmul(
                pb[bank][:, :], w_mdtT_sb[:, t * 128:(t + 1) * 128],
                dtr_sb[:, qslice(q)], start=True, stop=True),
                reads=["ma_dtr", "w_mdtT"], writes=[f"pb{bank}"])
            tr.op("vector", lambda e, t=t, q=q, bank=bank: e.tensor_scalar(
                sp_sb[:, qslice(q)], pb[bank][:, :], sm_bdt[:, t:t + 1], 21.0,
                ALU.add, ALU.min),
                reads=[f"pb{bank}", "smalls"], writes=["masp"], mark=(q == NQ - 1))
        tr.op("scalar", lambda e, t=t: e.activation(cb0[:], sp_sb[:], AF.Exp,
                                                    bias=0.0, scale=1.0),
              reads=["masp"], writes=["mac0"])
        tr.op("scalar", lambda e, t=t: e.activation(cb1[:], cb0[:], AF.Ln,
                                                    bias=cpool[:, 1:2], scale=1.0),
              reads=["mac0", "cpool"], writes=["mac1"])
        dk = f"madt{t % 2}"
        tr.op("vector", lambda e, t=t: e.tensor_scalar(
            dt_sb[t % 2][:], cb1[:], 1e-4, 20.0, ALU.max, ALU.min),
            reads=["mac1"], writes=[dk])
        tr.dma("gpsimd", out=s_dt[t, :, :], in_=dt_sb[t % 2][:],
               reads=[dk], writes=[f"s_dt{t}"])
    return dtr_sb


def emit_mamba_scan(tr, g, pb, A, Bar, e32_sb, bc_sb, negA, sm_d, w_moutT,
                    s_x, s_dt, s_zs, s_s1, hcar_ref=None):
    def BCv(j):
        return A[j // 4][:, (j % 4) * QC:(j % 4 + 1) * QC]

    def YCv(t):
        return Bar[t // 4][:, (t % 4) * QC:(t % 4 + 1) * QC]

    def YAv(t):
        return Bar[4 + t // 4][:, (t % 4) * QC:(t % 4 + 1) * QC]

    dtb = [g.sb(f"scdt{i}", [128, QC]) for i in range(2)]
    xb = [g.sb(f"scx{i}", [128, QC]) for i in range(2)]
    dtu = g.sb("scdu", [128, QC])
    dAb = [g.sb(f"scda{i}", [128, QC]) for i in range(4)]
    bfb = g.sb("scbf", [128, QC])
    hsq = [g.sb(f"sch{i}", [128, QC]) for i in range(4)]
    qb = g.sb("scq", [128, QC])
    qb2 = g.sb("scq2", [128, QC])
    hcar = g.sb("schc", [128, NTI * DST])
    if hcar_ref is not None:
        hcar_ref.append(hcar)
    zsb = [g.sb(f"sczs{i}", [128, QC]) for i in range(2)]
    wob = [g.sb("scwo0", [128, NTI, 128])]
    ssb = [g.sb(f"scs{i}", [128, QC]) for i in range(2)]

    tr.op("vector", lambda e: e.memset(hcar[:], 0.0),
          writes=["schc"] + [f"shc{t}_{n}" for t in range(NTI) for n in range(DST)])

    for c in range(NQ):
        for j in range(32):
            bank = j % 2
            tr.op("tensor", lambda e, c=c, j=j, bank=bank: e.matmul(
                pb[bank][:, :], e32_sb[:, j * 128:(j + 1) * 128],
                bc_sb[:, qslice(c)], start=True, stop=True),
                reads=["ma_bc", "c_e32"], writes=[f"pb{bank}"])
            tr.op("scalar", lambda e, j=j, bank=bank: e.activation(
                BCv(j), pb[bank][:, :], AF.Copy, bias=0.0, scale=1.0),
                reads=[f"pb{bank}"], writes=[f"scb{j}"])
        for t in range(NTI):
            kdt, kx, kzs = f"scdt{t % 2}", f"scx{t % 2}", f"sczs{t % 2}"
            tr.dma("sync", out=dtb[t % 2][:], in_=s_dt[t, :, qslice(c)],
                   reads=[f"s_dt{t}"], writes=[kdt])
            tr.dma("scalar", out=xb[t % 2][:], in_=s_x[t, :, qslice(c)],
                   reads=[f"s_x{t}"], writes=[kx])
            tr.dma("scalar", out=zsb[t % 2][:], in_=s_zs[t, :, qslice(c)],
                   reads=[f"s_zs{t}"], writes=[kzs])
            tr.op("vector", lambda e, t=t: e.tensor_mul(dtu[:], dtb[t % 2][:],
                                                        xb[t % 2][:]),
                  reads=[kdt, kx], writes=["scdu"])
            for n in range(DST):
                ka, kh = f"scda{n % 4}", f"sch{n % 4}"
                tr.op("scalar", lambda e, t=t, n=n: e.activation(
                    dAb[n % 4][:], dtb[t % 2][:], AF.Exp, bias=0.0,
                    scale=negA[:, t * DST + n:t * DST + n + 1]),
                    reads=[kdt, "negA"], writes=[ka])
                tr.op("vector", lambda e, n=n: e.tensor_mul(
                    bfb[:], dtu[:], BCv(n)),
                    reads=["scdu", f"scb{n}"], writes=["scbf"])
                tr.op("vector", lambda e, t=t, n=n: e.tensor_tensor_scan(
                    hsq[n % 4][:], dAb[n % 4][:], bfb[:],
                    hcar[:, t * DST + n:t * DST + n + 1], ALU.mult, ALU.add),
                    reads=[ka, "scbf", f"shc{t}_{n}"], writes=[kh], fenced=True)
                tr.op("gpsimd", lambda e, t=t, n=n: e.tensor_copy(
                    hcar[:, t * DST + n:t * DST + n + 1], hsq[n % 4][:, QC - 1:QC]),
                    reads=[kh], writes=[f"shc{t}_{n}"])
                if n == 0:
                    tr.op("gpsimd", lambda e, t=t, n=n: e.tensor_mul(
                        YCv(t), hsq[n % 4][:], BCv(16 + n)),
                        reads=[kh, f"scb{16 + n}"], writes=[f"scy{t}"])
                elif n == 1:
                    tr.op("vector", lambda e, t=t, n=n: e.tensor_mul(
                        YAv(t), hsq[n % 4][:], BCv(16 + n)),
                        reads=[kh, f"scb{16 + n}"], writes=[f"sca{t}"])
                elif n % 2 == 0:
                    tr.op("gpsimd", lambda e, t=t, n=n: e.tensor_mul(
                        qb[:], hsq[n % 4][:], BCv(16 + n)),
                        reads=[kh, f"scb{16 + n}"], writes=["scq"])
                    tr.op("gpsimd", lambda e, t=t: e.tensor_add(
                        YCv(t), YCv(t), qb[:]),
                        reads=["scq", f"scy{t}"], writes=[f"scy{t}"])
                else:
                    tr.op("vector", lambda e, t=t, n=n: e.tensor_mul(
                        qb2[:], hsq[n % 4][:], BCv(16 + n)),
                        reads=[kh, f"scb{16 + n}"], writes=["scq2"])
                    tr.op("vector", lambda e, t=t: e.tensor_add(
                        YAv(t), YAv(t), qb2[:]),
                        reads=["scq2", f"sca{t}"], writes=[f"sca{t}"])
            if "nofin" in getattr(tr, "flags", ()):
                continue
            tr.op("vector", lambda e, t=t: e.tensor_add(
                YCv(t), YCv(t), YAv(t)),
                reads=[f"sca{t}", f"scy{t}"], writes=[f"scy{t}"])
            tr.op("vector", lambda e, t=t: e.scalar_tensor_tensor(
                YCv(t), xb[t % 2][:], sm_d[:, t:t + 1], YCv(t),
                ALU.mult, ALU.add), reads=[kx, f"scy{t}", "smalls"], writes=[f"scy{t}"])
            tr.op("vector", lambda e, t=t: e.tensor_scalar(
                YCv(t), YCv(t), 0.0, 1.0, ALU.max, ALU.min),
                reads=[f"scy{t}"], writes=[f"scy{t}"])
            tr.op("vector", lambda e, t=t: e.tensor_mul(
                YCv(t), YCv(t), zsb[t % 2][:]),
                reads=[f"scy{t}", kzs], writes=[f"scy{t}"])
            tr.op("vector", lambda e, t=t: e.tensor_scalar(
                YCv(t), YCv(t), 0.0, 1.0, ALU.max, ALU.min),
                reads=[f"scy{t}"], writes=[f"scy{t}"])
        for dm in range(NT):
            kw = "scwo0"
            for t in range(NTI):
                tr.dma("sync", out=wob[0][:, t, :],
                       in_=w_moutT[t * 128:(t + 1) * 128, dm * 128:(dm + 1) * 128],
                       writes=[kw] if t == NTI - 1 else [kw + "p"])
            bank = 2 + (dm % 2)
            for t in range(NTI):
                tr.op("tensor", lambda e, dm=dm, t=t, bank=bank: e.matmul(
                    pb[bank][:, :], wob[0][:, t, :], YCv(t),
                    start=(t == 0), stop=(t == NTI - 1)),
                    reads=[kw, kw + "p", f"scy{t}"], writes=[f"pb{bank}"],
                    mark=(t == NTI - 1))
            ks = f"scs{dm % 2}"
            tr.op("vector", lambda e, bank=bank, dm=dm: e.tensor_copy(
                ssb[dm % 2][:], pb[bank][:, :]), reads=[f"pb{bank}"], writes=[ks])
            tr.dma("gpsimd", out=s_s1[dm, :, qslice(c)], in_=ssb[dm % 2][:],
                   reads=[ks], writes=[f"s_s1{dm}"])



def emit_ffn_prologue(tr, g, pb, ones, cpool, A, B, s_xn, w_ln1w, w_ln1b, w_bo,
                      w_bqkvh, bq8):
    for t in range(NT):
        tr.dma("sync", out=B[t][:], in_=s_xn[t, :, :],
               reads=[f"s_xn{t}"], writes=[f"xB{t}"])
    emit_layernorm(tr, g, pb, ones, cpool, B, [f"xB{t}" for t in range(NT)],
                   A, [f"nx{t}" for t in range(NT)],
                   w_sb=w_ln1w, b_sb=w_ln1b, wkey="smallsF", tag="ln1")
    for t in range(NT):
        tr.op("vector", lambda e, t=t: e.tensor_scalar(
            B[t][:], B[t][:], w_bo[:, t:t + 1], None, ALU.add),
            reads=[f"xB{t}", "smallsF"], writes=[f"x1_{t}", f"xB{t}"])
    tr.op("vector", lambda e: e.tensor_scalar_mul(bq8[:], w_bqkvh[:, 0:48:3], 0.125),
          reads=["smallsF"], writes=["bq8"])


def emit_attn_l0(tr, g, pb, ones, A, B, w_qkv, w_wo, w_bqkvh, bq8, masks_sb):
    wq = g.sb("a0wq", [128, NT, 128])
    wk = g.sb("a0wk", [128, NT, 128])
    wv = g.sb("a0wv", [128, NT, 128])
    wo = g.sb("a0wo", [64, 2, D])
    Qa = g.sb("a0qa", [65, L])
    Ka = g.sb("a0ka", [65, L])
    V65 = g.sb("a0v", [128, NTI, 65])
    att = g.sb("a0att", [64, L])
    sq64 = g.sb("a0sq", [64, QC])
    crow = g.sb("a0cr", [65, 8])
    rz = g.sb("a0rz", [65, QC])
    zbs = g.sb("a0zb", [64, QC])
    Pb = [g.sb(f"a0p{i}", [128, QC]) for i in range(2)]

    tr.op("vector", lambda e: e.memset(Ka[64:65, :], 1.0), writes=["ka_ones"])

    for gi in range(8):
        for k in range(NT):
            tr.dma("sync", out=wq[:, k, :], in_=w_qkv[k * 128:(k + 1) * 128,
                                                      gi * 128:(gi + 1) * 128],
                   writes=["a0wq"] if k == NT - 1 else ["a0wq_p"])
            tr.dma("scalar", out=wk[:, k, :], in_=w_qkv[k * 128:(k + 1) * 128,
                                                        D + gi * 128:D + (gi + 1) * 128],
                   writes=["a0wk"] if k == NT - 1 else ["a0wk_p"])
            tr.dma("gpsimd", out=wv[:, k, :], in_=w_qkv[k * 128:(k + 1) * 128,
                                                        2 * D + gi * 128:2 * D + (gi + 1) * 128],
                   writes=["a0wv"] if k == NT - 1 else ["a0wv_p"])
        for h2 in range(2):
            tr.dma("sync", out=wo[:, h2, :],
                   in_=w_wo[(2 * gi + h2) * 64:(2 * gi + h2 + 1) * 64, :],
                   writes=["a0wo"] if h2 == 1 else ["a0wo_p"])
        for h2 in range(2):
            h = 2 * gi + h2
            hs = slice(h2 * 64, h2 * 64 + 64)
            for qn, (wbuf, wkkey, dst, dkey, scl, bias_ap) in enumerate([
                    (wq, ["a0wq", "a0wq_p"], Qa, "qa", 0.125, bq8[:, h:h + 1]),
                    (wk, ["a0wk", "a0wk_p"], Ka, "ka", 1.0, w_bqkvh[:, 3 * h + 1:3 * h + 2])]):
                for q in range(NQ):
                    for k in range(NT):
                        tr.op("tensor", lambda e, wbuf=wbuf, hs=hs, k=k, q=q: e.matmul(
                            pb[0][0:64, :], wbuf[:, k, hs], A[k][:, qslice(q)],
                            start=(k == 0), stop=(k == NT - 1)),
                            reads=wkkey + [f"nx{k}"], writes=["pb0"],
                            mark=(k == NT - 1))
                    tr.op("scalar", lambda e, dst=dst, q=q, scl=scl, bias_ap=bias_ap:
                          e.activation(dst[0:64, qslice(q)], pb[0][0:64, :], AF.Copy,
                                       bias=0.0, scale=scl),
                          reads=["pb0"], writes=[dkey + "_s"])
                    tr.op("vector", lambda e, dst=dst, q=q, bias_ap=bias_ap:
                          e.tensor_scalar(dst[0:64, qslice(q)], dst[0:64, qslice(q)],
                                          bias_ap, None, ALU.add),
                          reads=[dkey + "_s", "bq8", "smallsF"], writes=[dkey])
            for tt in range(NTI):
                for k in range(NT):
                    tr.op("tensor", lambda e, tt=tt, k=k, hs=hs: e.matmul(
                        pb[1][:, 0:64], A[k][:, tt * 128:(tt + 1) * 128],
                        wv[:, k, hs], start=(k == 0), stop=(k == NT - 1)),
                        reads=["a0wv", "a0wv_p", f"nx{k}"], writes=["pb1"],
                        mark=(k == NT - 1))
                tr.op("vector", lambda e, tt=tt: e.tensor_copy(
                    V65[:, tt, 0:64], pb[1][:, 0:64]),
                    reads=["pb1"], writes=["v65"])
            tr.op("vector", lambda e: e.memset(V65[:, :, 64:65], 1.0),
                  writes=["v65"])
            for q in range(NQ):
                tr.op("scalar", lambda e, q=q: e.activation(
                    sq64[:, :], Ka[0:64, qslice(q)], AF.Square, bias=0.0, scale=1.0),
                    reads=["ka"], writes=["sq64"])
                tr.op("tensor", lambda e, q=q: e.matmul(
                    pb[2][0:65, :], ones[0:64, 0:65], sq64[:, :],
                    start=True, stop=True),
                    reads=["sq64", "c_ones"], writes=["pb2"])
                tr.op("vector", lambda e, q=q: e.tensor_reduce(
                    crow[64:65, q:q + 1], pb[2][64:65, :], mybir.AxisListType.X,
                    ALU.max),
                    reads=["pb2"], writes=["crow"])
            tr.op("vector", lambda e: e.tensor_reduce(
                crow[64:65, 4:5], crow[64:65, 0:4], mybir.AxisListType.X, ALU.max),
                reads=["crow"], writes=["rk2"])
            for q in range(NQ):
                tr.op("scalar", lambda e, q=q: e.activation(
                    sq64[:, :], Qa[0:64, qslice(q)], AF.Square, bias=0.0, scale=1.0),
                    reads=["qa"], writes=["sq64"])
                tr.op("tensor", lambda e, q=q: e.matmul(
                    pb[2][0:65, :], ones[0:64, 0:65], sq64[:, :],
                    start=True, stop=True),
                    reads=["sq64", "c_ones"], writes=["pb2"])
                tr.op("scalar", lambda e, q=q: e.activation(
                    Qa[64:65, qslice(q)], pb[2][64:65, :], AF.Sqrt, bias=0.0,
                    scale=crow[64:65, 4:5]),
                    reads=["pb2", "rk2", "cpool"], writes=["qa64"])
            tr.op("vector", lambda e: e.tensor_scalar_mul(Qa[64:65, :], Qa[64:65, :], -1.0),
                  reads=["qa64"], writes=["qa64"])
            for q in range(NQ):
                nkc = 4 * q + 4
                pvb = 4 + (q % 2)

                def emit_S(kc, q=q):
                    sb = 2 + (kc % 2)
                    tr.op("tensor", lambda e, kc=kc, q=q, sb=sb: e.matmul(
                        pb[sb][:, :], Ka[:, kc * 128:(kc + 1) * 128], Qa[:, qslice(q)],
                        start=True, stop=True),
                        reads=["ka", "ka_ones", "qa", "qa64"], writes=[f"pb{sb}"])
                    pk = f"a0p{kc % 2}"
                    tr.op("scalar", lambda e, kc=kc, sb=sb: e.activation(
                        Pb[kc % 2][:], pb[sb][:, :], AF.Exp, bias=0.0, scale=1.0),
                        reads=[f"pb{sb}"], writes=[pk])
                    if kc >= 4 * q:
                        j = kc - 4 * q
                        tr.op("vector", lambda e, kc=kc, j=j: e.tensor_mul(
                            Pb[kc % 2][:], Pb[kc % 2][:], masks_sb[:, j, :]),
                            reads=[pk, "masks"], writes=[pk])

                emit_S(0)
                for kc in range(nkc):
                    if kc + 1 < nkc:
                        emit_S(kc + 1)
                    pk = f"a0p{kc % 2}"
                    tr.op("tensor", lambda e, kc=kc, q=q, pvb=pvb: e.matmul(
                        pb[pvb][0:65, :], V65[:, kc, :], Pb[kc % 2][:],
                        start=(kc == 0), stop=(kc == nkc - 1)),
                        reads=[pk, "v65"], writes=[f"pb{pvb}"])
                tr.op("vector", lambda e, q=q, pvb=pvb: e.reciprocal(
                    rz[64:65, :], pb[pvb][64:65, :]), reads=[f"pb{pvb}"], writes=["rz"])
                tr.op("tensor", lambda e, q=q: e.matmul(
                    pb[6][0:64, :], ones[64:65, 0:64], rz[64:65, :],
                    start=True, stop=True),
                    reads=["rz", "c_ones"], writes=["pb6"])
                tr.op("vector", lambda e, q=q: e.tensor_copy(zbs[:, :], pb[6][0:64, :]),
                      reads=["pb6"], writes=["zbs"])
                tr.op("vector", lambda e, q=q, pvb=pvb: e.tensor_mul(
                    att[:, qslice(q)], pb[pvb][0:64, :], zbs[:, :]),
                    reads=[f"pb{pvb}", "zbs"], writes=["att"])
                tr.op("vector", lambda e, q=q, h=h: e.tensor_scalar(
                    att[:, qslice(q)], att[:, qslice(q)],
                    w_bqkvh[:, 3 * h + 2:3 * h + 3], None, ALU.add),
                    reads=["att", "smallsF"], writes=["att"])
            for dm in range(NT):
                for q in range(NQ):
                    wb = 6 + ((dm * NQ + q) % 2)
                    tr.op("tensor", lambda e, dm=dm, q=q, h2=h2, wb=wb: e.matmul(
                        pb[wb][:, :], wo[:, h2, dm * 128:(dm + 1) * 128],
                        att[:, qslice(q)], start=True, stop=True),
                        reads=["att", "a0wo", "a0wo_p"], writes=[f"pb{wb}"])
                    tr.op("vector", lambda e, dm=dm, q=q, wb=wb: e.tensor_add(
                        B[dm][:, qslice(q)], B[dm][:, qslice(q)], pb[wb][:, :]),
                        reads=[f"pb{wb}", f"x1_{dm}"], writes=[f"x1_{dm}"],
                        mark=True)


def emit_ln2_store(tr, g, pb, ones, cpool, B, w_ln2w, w_ln2b, s_nx2):
    tmp = [g.sb(f"l2t{i}", [128, L]) for i in range(2)]
    def post(t):
        tr.dma("gpsimd", out=s_nx2[t, :, :], in_=tmp[t % 2][:],
               reads=[f"nx2t{t % 2}"], writes=[f"s_nx2{t}"])

    emit_layernorm(tr, g, pb, ones, cpool, B, [f"x1_{t}" for t in range(NT)],
                   [tmp[t % 2] for t in range(NT)],
                   [f"nx2t{t % 2}" for t in range(NT)],
                   w_sb=w_ln2w, b_sb=w_ln2b, wkey="smallsF", tag="ln2", post=post)



def emit_moe(tr, g, pb, ones, ident, e8_sb, A, B, w_fc1T, w_fc2T, fc1b, fc2b,
             l0gw, s_nx2):
    nxq_bufs = [g.sb(f"monx{i}", [128, NT, QC]) for i in range(2)]
    wf1 = [g.sb(f"mow1{i}", [128, NT, 256]) for i in range(2)]
    wf2 = [g.sb(f"mow2{i}", [128, D]) for i in range(2)]
    sT8 = g.sb("most", [8, L])
    sbs = g.sb("mosb", [128, QC])
    tmp = g.sb("motmp", [128, QC])
    gsb = g.sb("mogs", [128, 64])
    g8 = gsb[:, 0:8]
    i1 = gsb[:, 8:16]
    mk8 = gsb[:, 16:24]
    i2 = gsb[:, 24:32]
    s8a = gsb[:, 32:40]
    s8f = gsb[:, 40:48]
    gsc = gsb[:, 48:52]

    def H1(ft):
        return A[ft // 4][:, (ft % 4) * QC:(ft % 4 + 1) * QC]

    for e_ in range(NE):
        for qc in range(NQ):
            it = e_ * NQ + qc
            nxq = nxq_bufs[it % 2]
            nk = f"monx{it % 2}"
            for k in range(NT):
                tr.dma("sync", out=nxq[:, k, :], in_=s_nx2[k, :, qslice(qc)],
                       reads=[f"s_nx2{k}"],
                       writes=[nk] if k == NT - 1 else [nk + "_p"])
            if e_ == 0:
                for tt2 in range(4):
                    for k in range(NT):
                        tr.op("tensor", lambda e, k=k, tt2=tt2: e.matmul(
                            pb[0][:, 0:8], nxq[:, k, tt2 * 128:(tt2 + 1) * 128],
                            l0gw[:, k, :], start=(k == 0), stop=(k == NT - 1)),
                            reads=[nk, nk + "_p", "l0gw"], writes=["pb0"],
                            mark=(k == NT - 1))
                    tr.op("vector", lambda e: e.tensor_copy(g8, pb[0][:, 0:8]),
                          reads=["pb0"], writes=["mog"])
                    tr.op("vector", lambda e: e.tensor_reduce(
                        gsc[:, 0:1], g8, mybir.AxisListType.X, ALU.max),
                        reads=["mog"], writes=["mog"])
                    tr.op("vector", lambda e: e.tensor_scalar(
                        i1, g8, gsc[:, 0:1], 0.0, ALU.subtract, ALU.is_ge),
                        reads=["mog"], writes=["mog"])
                    tr.op("vector", lambda e: e.scalar_tensor_tensor(
                        mk8, i1, -1e30, g8, ALU.mult, ALU.add),
                        reads=["mog"], writes=["mog"])
                    tr.op("vector", lambda e: e.tensor_reduce(
                        gsc[:, 1:2], mk8, mybir.AxisListType.X, ALU.max, negate=True),
                        reads=["mog"], writes=["mog"])
                    tr.op("scalar", lambda e: e.activation(
                        gsc[:, 2:3], gsc[:, 0:1], AF.Sigmoid, bias=gsc[:, 1:2],
                        scale=1.0), reads=["mog"], writes=["mog"])
                    tr.op("vector", lambda e: e.tensor_scalar(
                        gsc[:, 3:4], gsc[:, 2:3], -1.0, 1.0, ALU.mult, ALU.add),
                        reads=["mog"], writes=["mog"])
                    tr.op("vector", lambda e: e.tensor_scalar(
                        i2, mk8, gsc[:, 1:2], 0.0, ALU.add, ALU.is_ge),
                        reads=["mog"], writes=["mog"])
                    tr.op("vector", lambda e: e.tensor_scalar(
                        s8a, i1, gsc[:, 2:3], None, ALU.mult),
                        reads=["mog"], writes=["mog"])
                    tr.op("vector", lambda e: e.scalar_tensor_tensor(
                        s8f, i2, gsc[:, 3:4], s8a, ALU.mult, ALU.add),
                        reads=["mog"], writes=["mog2"])
                    tr.op("tensor", lambda e: e.transpose(
                        pb[1][0:8, 0:128], s8f, ident[:]),
                        reads=["mog2", "c_ident"], writes=["pb1"])
                    tr.op("vector", lambda e, qc=qc, tt2=tt2: e.tensor_copy(
                        sT8[0:8, qc * QC + tt2 * 128: qc * QC + (tt2 + 1) * 128],
                        pb[1][0:8, 0:128]),
                        reads=["pb1"], writes=["mosT"])
            tr.op("tensor", lambda e, e_=e_, qc=qc: e.matmul(
                pb[1][:, :], e8_sb[:, e_ * 128:(e_ + 1) * 128], sT8[0:8, qslice(qc)],
                start=True, stop=True),
                reads=["mosT", "c_e8"], writes=["pb1"])
            tr.op("vector", lambda e: e.tensor_copy(sbs[:], pb[1][:, :]),
                  reads=["pb1"], writes=["mosb"])
            for fg in range(16):
                wk = f"mow1{fg % 2}"
                for k in range(NT):
                    tr.dma("sync", out=wf1[fg % 2][:, k, :],
                           in_=w_fc1T[e_, k * 128:(k + 1) * 128,
                                      fg * 256:(fg + 1) * 256],
                           writes=[wk] if k == NT - 1 else [wk + "p"])
                for f2 in range(2):
                    ft = fg * 2 + f2
                    bank = 2 + (ft % 6)
                    for k in range(NT):
                        tr.op("tensor", lambda e, fg=fg, f2=f2, k=k, bank=bank: e.matmul(
                            pb[bank][:, :], wf1[fg % 2][:, k, f2 * 128:(f2 + 1) * 128],
                            nxq[:, k, :], start=(k == 0), stop=(k == NT - 1)),
                            reads=[wk, wk + "p", nk, nk + "_p"],
                            writes=[f"pb{bank}"], mark=(k == NT - 1))
                    tr.op("scalar", lambda e, e_=e_, ft=ft, bank=bank: e.activation(
                        H1(ft), pb[bank][:, :], AF.Gelu,
                        bias=fc1b[:, e_ * 32 + ft:e_ * 32 + ft + 1], scale=1.0),
                        reads=[f"pb{bank}", "smallsF"], writes=[f"h1_{ft}"])
            for ft in range(32):
                wk2 = f"mow2{ft % 2}"
                tr.dma("sync", out=wf2[ft % 2][:],
                       in_=w_fc2T[e_, ft * 128:(ft + 1) * 128, :], writes=[wk2])
                for dm in range(NT):
                    tr.op("tensor", lambda e, ft=ft, dm=dm: e.matmul(
                        pb[dm][:, :], wf2[ft % 2][:, dm * 128:(dm + 1) * 128],
                        H1(ft), start=(ft == 0), stop=(ft == 31)),
                        reads=[wk2, f"h1_{ft}"], writes=[f"pb{dm}"],
                        mark=(dm == NT - 1))
            for dm in range(NT):
                tr.op("vector", lambda e, e_=e_, dm=dm: e.scalar_tensor_tensor(
                    tmp[:], pb[dm][:, :], fc2b[:, e_ * NT + dm:e_ * NT + dm + 1],
                    sbs[:], ALU.add, ALU.mult),
                    reads=[f"pb{dm}", "mosb", "smallsF"], writes=["motmp"])
                tr.op("vector", lambda e, qc=qc, dm=dm: e.tensor_add(
                    B[dm][:, qslice(qc)], B[dm][:, qslice(qc)], tmp[:]),
                    reads=["motmp", f"x1_{dm}"], writes=[f"x1_{dm}"])



def emit_combine_l0(tr, g, pb, B, x_in, s_s1, s_x1r):
    gb = g.sb("cbg", [128, L])
    sb1 = [g.sb(f"cbs{i}", [128, QC]) for i in range(2)]
    xb0 = [g.sb(f"cbx{i}", [128, QC]) for i in range(2)]
    for t in range(NT):
        tr.op("scalar", lambda e, t=t: e.activation(gb[:], B[t][:], AF.Silu,
                                                    bias=0.0, scale=1.0),
              reads=[f"x1_{t}"], writes=["cbg"])
        for q in range(NQ):
            i = (t * NQ + q) % 2
            tr.dma("sync", out=sb1[i][:], in_=s_s1[t, :, qslice(q)],
                   reads=[f"s_s1{t}"], writes=[f"cbs{i}"])
            tr.dma("sync", out=xb0[i][:], in_=x_in[t, :, qslice(q)],
                   writes=[f"cbx{i}"])
            tr.op("vector", lambda e, t=t, q=q, i=i: e.tensor_mul(
                B[t][:, qslice(q)], gb[:, qslice(q)], sb1[i][:]),
                reads=["cbg", f"cbs{i}", f"x1_{t}"], writes=[f"xr{t}", f"x1_{t}"])
            tr.op("vector", lambda e, t=t, q=q, i=i: e.tensor_add(
                B[t][:, qslice(q)], B[t][:, qslice(q)], xb0[i][:]),
                reads=[f"cbx{i}"], writes=[f"xr{t}"])
        tr.dma("gpsimd", out=s_x1r[t, :, :], in_=B[t][:],
               reads=[f"xr{t}"], writes=[f"s_x1r{t}"])


def emit_l1ffn(tr, g, pb, ones, ident, e8_sb, A, B, w_e0T, w_e1T, w_owT,
               l1gw, s_g1):
    l1g = g.sb("l1gw_sb", [128, NT, NE])
    owt = [g.sb(f"l1owt{i}", [128, NT, 128]) for i in range(2)]
    hacc = g.sb("l1h", [128, NT, QC])
    sT2 = g.sb("l1st", [2, L])
    sbs0 = g.sb("l1s0", [128, QC])
    sbs1 = g.sb("l1s1", [128, QC])
    tmp = g.sb("l1tmp", [128, QC])
    g1b = [g.sb(f"l1g1{i}", [128, QC]) for i in range(2)]
    gsb = g.sb("l1gs", [128, 32])
    g8, p8, ge = gsb[:, 0:8], gsb[:, 8:16], gsb[:, 16:24]
    gsc = gsb[:, 24:30]

    def E0v(k):
        return A[k // 2][:, (k % 2) * D:(k % 2 + 1) * D]

    def E1v(k):
        return A[4 + k // 2][:, (k % 2) * D:(k % 2 + 1) * D]

    for k in range(NT):
        tr.dma("sync", out=l1g[:, k, :], in_=l1gw[k, :, :], writes=["l1gw"])
        tr.dma("sync", out=E0v(k), in_=w_e0T[k * 128:(k + 1) * 128, :],
               writes=[f"e0w{k}"])
        tr.dma("sync", out=E1v(k), in_=w_e1T[k * 128:(k + 1) * 128, :],
               writes=[f"e1w{k}"])
    for tt in range(NTI):
        for k in range(NT):
            tr.op("tensor", lambda e, tt=tt, k=k: e.matmul(
                pb[0][:, 0:8], B[k][:, tt * 128:(tt + 1) * 128], l1g[:, k, :],
                start=(k == 0), stop=(k == NT - 1)),
                reads=["l1gw", f"xn1{k}"], writes=["pb0"], mark=(k == NT - 1))
        tr.op("vector", lambda e: e.tensor_copy(g8, pb[0][:, 0:8]),
              reads=["pb0"], writes=["l1g8"])
        tr.op("vector", lambda e: e.tensor_reduce(
            gsc[:, 0:1], g8, mybir.AxisListType.X, ALU.max, negate=True),
            reads=["l1g8"], writes=["l1g8"])
        tr.op("scalar", lambda e: e.activation(
            p8, g8, AF.Exp, bias=gsc[:, 0:1], scale=1.0,
            accum_out=gsc[:, 1:2]), reads=["l1g8"], writes=["l1g8"])
        tr.op("vector", lambda e: e.reciprocal(gsc[:, 2:3], gsc[:, 1:2]),
              reads=["l1g8"], writes=["l1g8"])
        tr.op("vector", lambda e: e.tensor_scalar(
            p8, p8, gsc[:, 2:3], None, ALU.mult),
            reads=["l1g8"], writes=["l1g8"])
        tr.op("vector", lambda e: e.tensor_scalar(
            ge, p8, 0.3, None, ALU.is_gt), reads=["l1g8"], writes=["l1g8"])
        tr.op("vector", lambda e: e.tensor_reduce(
            gsc[:, 3:4], ge, mybir.AxisListType.X, ALU.add),
            reads=["l1g8"], writes=["l1g8"])
        tr.op("vector", lambda e: e.tensor_scalar(
            gsc[:, 4:5], gsc[:, 3:4], -1.0, 8.0, ALU.mult, ALU.add),
            reads=["l1g8"], writes=["l1g8"])
        tr.op("vector", lambda e: e.tensor_mul(
            gsc[:, 4:5], gsc[:, 4:5], p8[:, 0:1]),
            reads=["l1g8"], writes=["l1g8"])
        tr.op("vector", lambda e: e.tensor_mul(
            gsc[:, 5:6], gsc[:, 3:4], p8[:, 1:2]),
            reads=["l1g8"], writes=["l1s01"])
        tr.op("tensor", lambda e: e.transpose(
            pb[1][0:2, 0:128], gsc[:, 4:6], ident[:]),
            reads=["l1s01", "c_ident"], writes=["pb1"])
        tr.op("vector", lambda e, tt=tt: e.tensor_copy(
            sT2[0:2, tt * 128:(tt + 1) * 128], pb[1][0:2, 0:128]),
            reads=["pb1"], writes=["l1sT"])
    for qc in range(NQ):
        tr.op("tensor", lambda e, qc=qc: e.matmul(
            pb[1][:, :], e8_sb[0:2, 0:128], sT2[0:2, qslice(qc)],
            start=True, stop=True), reads=["l1sT", "c_e8"], writes=["pb1"])
        tr.op("vector", lambda e: e.tensor_copy(sbs0[:], pb[1][:, :]),
              reads=["pb1"], writes=["l1sb0"])
        tr.op("tensor", lambda e, qc=qc: e.matmul(
            pb[1][:, :], e8_sb[0:2, 128:256], sT2[0:2, qslice(qc)],
            start=True, stop=True), reads=["l1sT", "c_e8"], writes=["pb1"])
        tr.op("vector", lambda e: e.tensor_copy(sbs1[:], pb[1][:, :]),
              reads=["pb1"], writes=["l1sb1"])
        for ft in range(NT):
            b0 = 2 + (ft % 2)
            for k in range(NT):
                tr.op("tensor", lambda e, ft=ft, k=k, qc=qc, b0=b0: e.matmul(
                    pb[b0][:, :], E0v(k)[:, ft * 128:(ft + 1) * 128],
                    B[k][:, qslice(qc)], start=(k == 0), stop=(k == NT - 1)),
                    reads=[f"e0w{k}", f"xn1{k}"], writes=[f"pb{b0}"],
                    mark=(k == NT - 1))
            tr.op("vector", lambda e, ft=ft, b0=b0: e.tensor_mul(
                hacc[:, ft, :], pb[b0][:, :], sbs0[:]),
                reads=[f"pb{b0}", "l1sb0"], writes=[f"l1h{ft}"])
            b1 = 4 + (ft % 2)
            for k in range(NT):
                tr.op("tensor", lambda e, ft=ft, k=k, qc=qc, b1=b1: e.matmul(
                    pb[b1][:, :], E1v(k)[:, ft * 128:(ft + 1) * 128],
                    B[k][:, qslice(qc)], start=(k == 0), stop=(k == NT - 1)),
                    reads=[f"e1w{k}", f"xn1{k}"], writes=[f"pb{b1}"],
                    mark=(k == NT - 1))
            tr.op("vector", lambda e, ft=ft, b1=b1: e.tensor_mul(
                tmp[:], pb[b1][:, :], sbs1[:]),
                reads=[f"pb{b1}", "l1sb1"], writes=["l1tmp"])
            tr.op("vector", lambda e, ft=ft: e.tensor_add(
                hacc[:, ft, :], hacc[:, ft, :], tmp[:]),
                reads=["l1tmp", f"l1h{ft}"], writes=[f"l1h{ft}"])
        for dm in range(NT):
            b2 = 6 + (dm % 2)
            wkow = f"l1owt{dm % 2}"
            for k in range(NT):
                tr.dma("sync", out=owt[dm % 2][:, k, :],
                       in_=w_owT[k * 128:(k + 1) * 128, dm * 128:(dm + 1) * 128],
                       writes=[wkow] if k == NT - 1 else [wkow + "p"])
            for k in range(NT):
                tr.op("tensor", lambda e, dm=dm, k=k, b2=b2: e.matmul(
                    pb[b2][:, :], owt[dm % 2][:, k, :],
                    hacc[:, k, :], start=(k == 0), stop=(k == NT - 1)),
                    reads=[wkow, wkow + "p", f"l1h{k}"], writes=[f"pb{b2}"],
                    mark=(k == NT - 1))
            gk = f"l1g1{dm % 2}"
            tr.op("scalar", lambda e, dm=dm, b2=b2: e.activation(
                g1b[dm % 2][:], pb[b2][:, :], AF.Silu, bias=0.0, scale=1.0),
                reads=[f"pb{b2}"], writes=[gk])
            tr.dma("gpsimd", out=s_g1[dm, :, qslice(qc)], in_=g1b[dm % 2][:],
                   reads=[gk], writes=[f"s_g1{dm}"])


def emit_attn_l1(tr, g, pb, ones, ident, cpool, A, B, w_dqT, w_dkvT, w_uqT,
                 w_ukT, w_uvT, w_l1wo, smf2, ropec, ropes, s_xn1):
    for t in range(NT):
        tr.dma("gpsimd", out=s_xn1[t, :, :], in_=B[t][:],
               reads=[f"xn1{t}"], writes=[f"s_xn1{t}"])
    Gln = Grp(tr.nc)
    emit_layernorm(tr, Gln, pb, ones, cpool, B,
                   [f"xn1{t}" for t in range(NT)], B,
                   [f"hi{t}" for t in range(NT)],
                   w_sb=smf2[:, 0:NT], b_sb=smf2[:, NT:2 * NT],
                   wkey="smallsF", tag="l1ln")
    tr.barrier()
    Gln.close()
    Gdq = Grp(tr.nc)
    wdq = Gdq.sb("l1wdq", [128, NT, QC])
    wdk = Gdq.sb("l1wdk", [128, NT, QC])
    for k in range(NT):
        tr.dma("sync", out=wdq[:, k, :], in_=w_dqT[k * 128:(k + 1) * 128, :],
               writes=["l1wdq"])
        tr.dma("sync", out=wdk[:, k, :], in_=w_dkvT[k * 128:(k + 1) * 128, :],
               writes=["l1wdk"])
    for ct in range(4):
        for q in range(NQ):
            bk = (ct * NQ + q) % 2
            for k in range(NT):
                tr.op("tensor", lambda e, ct=ct, q=q, k=k, bk=bk: e.matmul(
                    pb[bk][:, :], wdq[:, k, ct * 128:(ct + 1) * 128],
                    B[k][:, qslice(q)], start=(k == 0), stop=(k == NT - 1)),
                    reads=["l1wdq", f"hi{k}"], writes=[f"pb{bk}"],
                    mark=(k == NT - 1))
            tr.op("scalar", lambda e, ct=ct, q=q, bk=bk: e.activation(
                A[ct][:, qslice(q)], pb[bk][:, :], AF.Copy, bias=0.0,
                scale=0.125), reads=[f"pb{bk}"], writes=[f"cq{ct}"])
            bk2 = 2 + (ct * NQ + q) % 2
            for k in range(NT):
                tr.op("tensor", lambda e, ct=ct, q=q, k=k, bk2=bk2: e.matmul(
                    pb[bk2][:, :], wdk[:, k, ct * 128:(ct + 1) * 128],
                    B[k][:, qslice(q)], start=(k == 0), stop=(k == NT - 1)),
                    reads=["l1wdk", f"hi{k}"], writes=[f"pb{bk2}"],
                    mark=(k == NT - 1))
            tr.op("vector", lambda e, ct=ct, q=q, bk2=bk2: e.tensor_copy(
                A[4 + ct][:, qslice(q)], pb[bk2][:, :]),
                reads=[f"pb{bk2}"], writes=[f"ckv{ct}"])
    for t in range(NT):
        tr.dma("sync", out=B[t][:], in_=s_xn1[t, :, :],
               reads=[f"s_xn1{t}"], writes=[f"s1a{t}", f"hi{t}"])
    tr.barrier()
    Gdq.close()
    wuq = g.sb("l1wuq", [128, 4, 128])
    wuk = g.sb("l1wuk", [128, 4, 128])
    wuv = g.sb("l1wuv", [128, 4, 128])
    wo = g.sb("l1wo", [64, D])
    QT = g.sb("l1qt", [128, NTI, 128])
    KT = g.sb("l1kt", [128, NTI, 128])
    rtmp = g.sb("l1rt", [128, 2, 64])
    Qa = g.sb("l1qa", [65, L])
    Ka = g.sb("l1ka", [65, L])
    V65 = g.sb("l1v", [128, NTI, 65])
    att = g.sb("l1att", [64, QC])
    sq64 = g.sb("l1scr64", [64, QC])
    crow = g.sb("l1cr", [65, 8 + QC])
    rz = crow[:, 8:8 + QC]
    zbs = sq64
    Pb = [g.sb(f"l1p{i}", [128, QC]) for i in range(2)]
    tr.op("vector", lambda e: e.memset(Ka[64:65, :], 1.0), writes=["l1kaone"])

    for gi in range(8):
        for c in range(4):
            tr.dma("sync", out=wuq[:, c, :], in_=w_uqT[c * 128:(c + 1) * 128,
                                                       gi * 128:(gi + 1) * 128],
                   writes=["l1wuq"])
            tr.dma("scalar", out=wuk[:, c, :], in_=w_ukT[c * 128:(c + 1) * 128,
                                                         gi * 128:(gi + 1) * 128],
                   writes=["l1wuk"])
            tr.dma("gpsimd", out=wuv[:, c, :], in_=w_uvT[c * 128:(c + 1) * 128,
                                                         gi * 128:(gi + 1) * 128],
                   writes=["l1wuv"])
        for (wbuf, wkkey, dst, dkey, srck) in [
                (wuq, "l1wuq", QT, "l1qt", "cq"), (wuk, "l1wuk", KT, "l1kt", "ckv")]:
            srcA = A if srck == "cq" else None
            for tt in range(NTI):
                for c in range(4):
                    base = 0 if srck == "cq" else 4
                    tr.op("tensor", lambda e, tt=tt, c=c, wbuf=wbuf, base=base: e.matmul(
                        pb[6][:, 0:128], A[base + c][:, tt * 128:(tt + 1) * 128],
                        wbuf[:, c, :], start=(c == 0), stop=(c == 3)),
                        reads=[wkkey, f"{srck}{c}"], writes=["pb6"],
                        mark=(c == 3))
                tr.op("vector", lambda e, tt=tt: e.tensor_mul(
                    rtmp[:, 0, :], pb[6][:, 0:128:2], ropec[:, tt, :]),
                    reads=["pb6", "rope"], writes=["l1r0"])
                tr.op("vector", lambda e, tt=tt: e.tensor_mul(
                    rtmp[:, 1, :], pb[6][:, 1:128:2], ropes[:, tt, :]),
                    reads=["pb6", "rope"], writes=["l1r1"])
                tr.op("vector", lambda e, tt=tt, dst=dst: e.tensor_sub(
                    dst[:, tt, 0:128:2], rtmp[:, 0, :], rtmp[:, 1, :]),
                    reads=["l1r0", "l1r1"], writes=[dkey])
                tr.op("vector", lambda e, tt=tt: e.tensor_mul(
                    rtmp[:, 0, :], pb[6][:, 0:128:2], ropes[:, tt, :]),
                    reads=["pb6", "rope", "l1r0"], writes=["l1r0"])
                tr.op("vector", lambda e, tt=tt: e.tensor_mul(
                    rtmp[:, 1, :], pb[6][:, 1:128:2], ropec[:, tt, :]),
                    reads=["pb6", "rope", "l1r1"], writes=["l1r1"])
                tr.op("vector", lambda e, tt=tt, dst=dst: e.tensor_add(
                    dst[:, tt, 1:128:2], rtmp[:, 0, :], rtmp[:, 1, :]),
                    reads=["l1r0", "l1r1"], writes=[dkey])
        for h2 in range(2):
            h = 2 * gi + h2
            hs = slice(h2 * 64, h2 * 64 + 64)
            tr.dma("scalar", out=wo[:, :],
                   in_=w_l1wo[h * 64:(h + 1) * 64, :], writes=["l1wo"])
            for tt in range(NTI):
                tr.op("tensor", lambda e, tt=tt, hs=hs: e.transpose(
                    pb[6][0:64, 0:128], QT[:, tt, hs], ident[:]),
                    reads=["l1qt", "c_ident"], writes=["pb6"])
                tr.op("vector", lambda e, tt=tt: e.tensor_copy(
                    Qa[0:64, tt * 128:(tt + 1) * 128], pb[6][0:64, 0:128]),
                    reads=["pb6"], writes=["l1qa"])
                tr.op("tensor", lambda e, tt=tt, hs=hs: e.transpose(
                    pb[7][0:64, 0:128], KT[:, tt, hs], ident[:]),
                    reads=["l1kt", "c_ident"], writes=["pb7"])
                tr.op("vector", lambda e, tt=tt: e.tensor_copy(
                    Ka[0:64, tt * 128:(tt + 1) * 128], pb[7][0:64, 0:128]),
                    reads=["pb7"], writes=["l1kad"])
            for tt in range(NTI):
                for c in range(4):
                    tr.op("tensor", lambda e, tt=tt, c=c, hs=hs: e.matmul(
                        pb[1][:, 0:64], A[4 + c][:, tt * 128:(tt + 1) * 128],
                        wuv[:, c, hs], start=(c == 0), stop=(c == 3)),
                        reads=["l1wuv", f"ckv{c}"], writes=["pb1"],
                        mark=(c == 3))
                tr.op("vector", lambda e, tt=tt: e.tensor_copy(
                    V65[:, tt, 0:64], pb[1][:, 0:64]),
                    reads=["pb1"], writes=["l1v65"])
            tr.op("vector", lambda e: e.memset(V65[:, :, 64:65], 1.0),
                  writes=["l1v65"])
            for q in range(NQ):
                tr.op("scalar", lambda e, q=q: e.activation(
                    sq64[:, :], Ka[0:64, qslice(q)], AF.Square, bias=0.0, scale=1.0),
                    reads=["l1kad"], writes=["l1scr"])
                tr.op("tensor", lambda e, q=q: e.matmul(
                    pb[2][0:65, :], ones[0:64, 0:65], sq64[:, :],
                    start=True, stop=True),
                    reads=["l1scr", "c_ones"], writes=["pb2"])
                tr.op("vector", lambda e, q=q: e.tensor_reduce(
                    crow[64:65, q:q + 1], pb[2][64:65, :], mybir.AxisListType.X,
                    ALU.max), reads=["pb2"], writes=["l1crow"])
            tr.op("vector", lambda e: e.tensor_reduce(
                crow[64:65, 4:5], crow[64:65, 0:4], mybir.AxisListType.X, ALU.max),
                reads=["l1crow"], writes=["l1rk2"])
            for q in range(NQ):
                tr.op("scalar", lambda e, q=q: e.activation(
                    sq64[:, :], Qa[0:64, qslice(q)], AF.Square, bias=0.0, scale=1.0),
                    reads=["l1qa"], writes=["l1scr"])
                tr.op("tensor", lambda e, q=q: e.matmul(
                    pb[2][0:65, :], ones[0:64, 0:65], sq64[:, :],
                    start=True, stop=True),
                    reads=["l1scr", "c_ones"], writes=["pb2"])
                tr.op("scalar", lambda e, q=q: e.activation(
                    Qa[64:65, qslice(q)], pb[2][64:65, :], AF.Sqrt, bias=0.0,
                    scale=crow[64:65, 4:5]),
                    reads=["pb2", "l1rk2", "cpool"], writes=["l1qa64"])
            tr.op("vector", lambda e: e.tensor_scalar_mul(
                Qa[64:65, :], Qa[64:65, :], -1.0),
                reads=["l1qa64"], writes=["l1qa64"])
            for q in range(NQ):
                pvb = 4 + (q % 2)

                def emit_S1(kc, q=q):
                    sb = 2 + (kc % 2)
                    tr.op("tensor", lambda e, kc=kc, q=q, sb=sb: e.matmul(
                        pb[sb][:, :], Ka[:, kc * 128:(kc + 1) * 128],
                        Qa[:, qslice(q)], start=True, stop=True),
                        reads=["l1kad", "l1kaone", "l1qa", "l1qa64"],
                        writes=[f"pb{sb}"])
                    pk = f"l1p{kc % 2}"
                    tr.op("scalar", lambda e, kc=kc, sb=sb: e.activation(
                        Pb[kc % 2][:], pb[sb][:, :], AF.Exp, bias=0.0, scale=1.0),
                        reads=[f"pb{sb}"], writes=[pk])

                emit_S1(0)
                for kc in range(NTI):
                    if kc + 1 < NTI:
                        emit_S1(kc + 1)
                    pk = f"l1p{kc % 2}"
                    tr.op("tensor", lambda e, kc=kc, q=q, pvb=pvb: e.matmul(
                        pb[pvb][0:65, :], V65[:, kc, :], Pb[kc % 2][:],
                        start=(kc == 0), stop=(kc == NTI - 1)),
                        reads=[pk, "l1v65"], writes=[f"pb{pvb}"])
                tr.op("vector", lambda e, q=q, pvb=pvb: e.reciprocal(
                    rz[64:65, :], pb[pvb][64:65, :]),
                    reads=[f"pb{pvb}"], writes=["l1rz"])
                tr.op("tensor", lambda e, q=q: e.matmul(
                    pb[6][0:64, :], ones[64:65, 0:64], rz[64:65, :],
                    start=True, stop=True),
                    reads=["l1rz", "c_ones"], writes=["pb6"])
                tr.op("vector", lambda e, q=q: e.tensor_copy(zbs[:, :], pb[6][0:64, :]),
                      reads=["pb6"], writes=["l1scr"])
                tr.op("vector", lambda e, q=q, pvb=pvb: e.tensor_mul(
                    att[:, :], pb[pvb][0:64, :], zbs[:, :]),
                    reads=[f"pb{pvb}", "l1scr"], writes=["l1att"])
                for dm in range(NT):
                    wb = 6 + (dm % 2)
                    tr.op("tensor", lambda e, dm=dm, q=q, wb=wb: e.matmul(
                        pb[wb][:, :], wo[:, dm * 128:(dm + 1) * 128],
                        att[:, :], start=True, stop=True),
                        reads=["l1att", "l1wo"], writes=[f"pb{wb}"])
                    tr.op("vector", lambda e, dm=dm, q=q, wb=wb: e.tensor_add(
                        B[dm][:, qslice(q)], B[dm][:, qslice(q)], pb[wb][:, :]),
                        reads=[f"pb{wb}", f"s1a{dm}"], writes=[f"s1a{dm}"])


def emit_final(tr, g, B, s_x1r, s_g1, y_out):
    xb = [g.sb(f"fnx{i}", [128, QC]) for i in range(2)]
    gb = [g.sb(f"fng{i}", [128, QC]) for i in range(2)]
    ob = [g.sb(f"fno{i}", [128, QC]) for i in range(2)]
    for t in range(NT):
        for q in range(NQ):
            i = (t * NQ + q) % 2
            tr.dma("sync", out=xb[i][:], in_=s_x1r[t, :, qslice(q)],
                   reads=[f"s_x1r{t}"], writes=[f"fnx{i}"])
            tr.dma("sync", out=gb[i][:], in_=s_g1[t, :, qslice(q)],
                   reads=[f"s_g1{t}"], writes=[f"fng{i}"])
            tr.op("vector", lambda e, t=t, q=q, i=i: e.tensor_mul(
                ob[i][:], B[t][:, qslice(q)], gb[i][:]),
                reads=[f"s1a{t}", f"fng{i}"], writes=[f"fno{i}"])
            tr.op("vector", lambda e, t=t, q=q, i=i: e.tensor_add(
                ob[i][:], ob[i][:], xb[i][:]),
                reads=[f"fnx{i}", f"fno{i}"], writes=[f"fno{i}"])
            tr.dma("gpsimd", out=y_out[t, :, qslice(q)], in_=ob[i][:],
                   reads=[f"fno{i}"], writes=[f"y{t}"])



CAP = 1024


def emit_moe_routed(tr, g, pb, ones, ident, A, B, w_fc1T, w_fc2T, fc1b, fc2b,
                    l0gw, c_iotaf, c_rowc, c_idxfill, s_nx2, s_nx2T, s_idx,
                    s_moeT):
    U32 = mybir.dt.uint32
    nxq = g.sb("monx", [128, NT, QC])
    stage = g.sb("mostg", [128, D])
    xof = g.sb("moxof", [128, NT, QC])
    gsb = g.sb("mogs", [128, 64])
    g8, i1, mk8, i2 = gsb[:, 0:8], gsb[:, 8:16], gsb[:, 16:24], gsb[:, 24:32]
    s8f, u8, d8 = gsb[:, 32:40], gsb[:, 40:48], gsb[:, 48:56]
    gsc = gsb[:, 56:62]
    destf = g.sb("modst", [128, NTI, NE])
    svf = g.sb("mosvf", [128, NTI, NE])
    slotu = g.sb("moslu", [128, NTI, NE], U32)
    ci = g.sb("moci", [128, NTI])
    rowc = g.sb("morc", [8, 1])
    sT8 = g.sb("most", [8, L])
    scat = g.sb("mosc", [128, 3])
    ixb = g.sb("moix", [128, 3])
    giu = g.sb("mogiu", [128, 4], U32)
    dstu = g.sb("modtu", [128, 4], U32)
    svv = g.sb("mosvv", [128, 4])
    fill = g.sb("mofill", [128, 192])
    g7a = Grp(tr.nc)
    rowA = g7a.sb("morA", [8, L])
    rowB = g7a.sb("morB", [8, L])

    tr.dma("sync", out=ci[:], in_=c_iotaf, writes=["moci"])
    tr.dma("sync", out=rowc[:], in_=c_rowc, writes=["morc"])
    tr.dma("sync", out=fill[:], in_=c_idxfill, writes=["mofill"])
    tr.dma("gpsimd", out=s_idx[0:8192, :], in_=fill[:], reads=["mofill"],
           writes=["s_idx"])
    tr.op("vector", lambda e: e.memset(stage[0:1, :], 0.0), writes=["mostg"])
    tr.dma("gpsimd", out=s_nx2T[2048:2049, :], in_=stage[0:1, :],
           reads=["mostg"], writes=["s_nx2T_z"])

    for qc in range(NQ):
        for k in range(NT):
            tr.dma("sync", out=nxq[:, k, :], in_=s_nx2[k, :, qslice(qc)],
                   reads=[f"s_nx2{k}"],
                   writes=["monx"] if k == NT - 1 else ["monx_p"])
        for tt2 in range(4):
            tt = qc * 4 + tt2
            for k in range(NT):
                tr.op("tensor", lambda e, k=k, tt2=tt2: e.transpose(
                    pb[2 + k % 2][:, 0:128], nxq[:, k, tt2 * 128:(tt2 + 1) * 128],
                    ident[:]),
                    reads=["monx", "monx_p", "c_ident"], writes=[f"pb{2 + k % 2}"])
                tr.op("vector", lambda e, k=k: e.tensor_copy(
                    stage[:, k * 128:(k + 1) * 128], pb[2 + k % 2][:, 0:128]),
                    reads=[f"pb{2 + k % 2}"], writes=["mostg"])
            tr.dma("gpsimd", out=s_nx2T[tt * 128:(tt + 1) * 128, :], in_=stage[:],
                   reads=["mostg"], writes=[f"s_nx2T{tt}"])
            for k in range(NT):
                tr.op("tensor", lambda e, k=k, tt2=tt2: e.matmul(
                    pb[0][:, 0:8], nxq[:, k, tt2 * 128:(tt2 + 1) * 128],
                    l0gw[:, k, :], start=(k == 0), stop=(k == NT - 1)),
                    reads=["monx", "monx_p", "l0gw"], writes=["pb0"],
                    mark=(k == NT - 1))
            tr.op("vector", lambda e: e.tensor_copy(g8, pb[0][:, 0:8]),
                  reads=["pb0"], writes=["mog"])
            tr.op("vector", lambda e: e.tensor_reduce(
                gsc[:, 0:1], g8, mybir.AxisListType.X, ALU.max),
                reads=["mog"], writes=["mog"])
            tr.op("vector", lambda e: e.tensor_scalar(
                i1, g8, gsc[:, 0:1], 0.0, ALU.subtract, ALU.is_ge),
                reads=["mog"], writes=["mog"])
            tr.op("vector", lambda e: e.scalar_tensor_tensor(
                mk8, i1, -1e30, g8, ALU.mult, ALU.add),
                reads=["mog"], writes=["mog"])
            tr.op("vector", lambda e: e.tensor_reduce(
                gsc[:, 1:2], mk8, mybir.AxisListType.X, ALU.max, negate=True),
                reads=["mog"], writes=["mog"])
            tr.op("scalar", lambda e: e.activation(
                gsc[:, 2:3], gsc[:, 0:1], AF.Sigmoid, bias=gsc[:, 1:2],
                scale=1.0), reads=["mog"], writes=["mog"])
            tr.op("vector", lambda e: e.tensor_scalar(
                gsc[:, 3:4], gsc[:, 2:3], -1.0, 1.0, ALU.mult, ALU.add),
                reads=["mog"], writes=["mog"])
            tr.op("vector", lambda e: e.tensor_scalar(
                i2, mk8, gsc[:, 1:2], 0.0, ALU.add, ALU.is_ge),
                reads=["mog"], writes=["mog"])
            tr.op("vector", lambda e: e.tensor_scalar(
                d8, i1, gsc[:, 2:3], None, ALU.mult),
                reads=["mog"], writes=["mog"])
            tr.op("vector", lambda e, tt=tt: e.scalar_tensor_tensor(
                svf[:, tt, :], i2, gsc[:, 3:4], d8, ALU.mult, ALU.add),
                reads=["mog"], writes=["mosvf"])
            tr.op("vector", lambda e: e.tensor_add(u8, i1, i2),
                  reads=["mog"], writes=["mog"])
            tr.op("vector", lambda e: e.tensor_scalar(
                u8, u8, -1.0, 1.0, ALU.mult, ALU.add),
                reads=["mog"], writes=["mog"])
            tr.op("vector", lambda e: e.tensor_scalar(
                d8, i2, 2048.0, None, ALU.mult), reads=["mog"], writes=["mog"])
            tr.op("vector", lambda e: e.scalar_tensor_tensor(
                d8, u8, 8192.0, d8, ALU.mult, ALU.add),
                reads=["mog"], writes=["mog"])
            tr.op("vector", lambda e, tt=tt: e.tensor_scalar(
                destf[:, tt, :], d8, ci[:, tt:tt + 1], None, ALU.add),
                reads=["mog", "moci"], writes=["modst"])
            tr.op("tensor", lambda e, tt=tt: e.transpose(
                pb[1][0:8, 0:128], svf[:, tt, :], ident[:]),
                reads=["mosvf", "c_ident"], writes=["pb1"])
            tr.op("vector", lambda e, tt=tt: e.tensor_copy(
                sT8[0:8, tt * 128:(tt + 1) * 128], pb[1][0:8, 0:128]),
                reads=["pb1"], writes=["mosT"])
    tr.op("vector", lambda e: e.tensor_scalar(
        rowA[0:8, :], sT8[0:8, :], 0.0, None, ALU.is_gt),
        reads=["mosT"], writes=["morA"])
    tr.op("vector", lambda e: e.memset(rowB[0:8, :], 0.0), writes=["morB"])
    tr.op("vector", lambda e: e.tensor_tensor_scan(
        sT8[0:8, :], rowB[0:8, :], rowA[0:8, :], 0.0, ALU.add, ALU.add),
        reads=["morA", "morB"], writes=["mopos", "mosT"], fenced=True)
    tr.op("vector", lambda e: e.tensor_scalar(
        rowB[0:8, :], sT8[0:8, :], float(CAP), None, ALU.is_le),
        reads=["mopos"], writes=["morB"])
    tr.op("vector", lambda e: e.tensor_mul(rowA[0:8, :], rowA[0:8, :], rowB[0:8, :]),
          reads=["morA", "morB"], writes=["morA"])
    tr.op("vector", lambda e: e.tensor_scalar(
        sT8[0:8, :], sT8[0:8, :], rowc[:, 0:1], None, ALU.add),
        reads=["mopos", "morc"], writes=["mopos"])
    tr.op("gpsimd", lambda e: e.iota(rowB[0:8, :], [[1, L]], base=8192,
                                     channel_multiplier=0,
                                     allow_small_or_imprecise_dtypes=True),
          writes=["morB"])
    tr.op("vector", lambda e: e.tensor_sub(sT8[0:8, :], sT8[0:8, :], rowB[0:8, :]),
          reads=["mopos", "morB"], writes=["mopos"])
    tr.op("vector", lambda e: e.tensor_mul(sT8[0:8, :], sT8[0:8, :], rowA[0:8, :]),
          reads=["mopos", "morA"], writes=["mopos"])
    tr.op("vector", lambda e: e.tensor_add(sT8[0:8, :], sT8[0:8, :], rowB[0:8, :]),
          reads=["mopos", "morB"], writes=["mopos"])
    for tt in range(NTI):
        tr.op("tensor", lambda e, tt=tt: e.transpose(
            pb[1][0:128, 0:8], sT8[0:8, tt * 128:(tt + 1) * 128], ident[0:8, 0:8]),
            reads=["mopos", "c_ident"], writes=["pb1"])
        tr.op("vector", lambda e, tt=tt: e.tensor_copy(
            slotu[:, tt, :], pb[1][0:128, 0:8]),
            reads=["pb1"], writes=["moslu"])
    for tt in range(NTI):
        for e_ in range(NE):
            tr.op("vector", lambda e, tt=tt: e.tensor_copy(
                scat[:, 0:1], ci[:, tt:tt + 1]), reads=["moci"], writes=["mosc"])
            tr.op("vector", lambda e, tt=tt, e_=e_: e.tensor_copy(
                scat[:, 1:2], svf[:, tt, e_:e_ + 1]),
                reads=["mosvf"], writes=["mosc"])
            tr.op("vector", lambda e, tt=tt, e_=e_: e.tensor_copy(
                scat[:, 2:3], destf[:, tt, e_:e_ + 1]),
                reads=["modst"], writes=["mosc"])
            tr.idma(lambda e, tt=tt, e_=e_: e.indirect_dma_start(
                out=s_idx, out_offset=IndirectOffsetOnAxis(
                    ap=slotu[:, tt, e_:e_ + 1], axis=0),
                in_=scat[:], in_offset=None),
                reads=["mosc", "moslu", "s_idx"], writes=["s_idx"])
    tr.barrier()
    g7a.close()
    g7b = Grp(tr.nc)
    wf1 = [g7b.sb(f"mow1{i}", [128, NT, 256]) for i in range(2)]
    wf2 = [g7b.sb(f"mow2{i}", [128, D]) for i in range(2)]
    for e_ in range(NE):
        for qd in range(2):
            kxof = "moxof"
            for i in range(4):
                base = e_ * CAP + qd * 512 + i * 128
                tr.dma("sync", out=ixb[:], in_=s_idx[base:base + 128, :],
                       reads=["s_idx"], writes=["moix"])
                tr.op("vector", lambda e, i=i: e.tensor_copy(
                    giu[:, i:i + 1], ixb[:, 0:1]), reads=["moix"], writes=["mogiu"])
                tr.op("vector", lambda e, i=i: e.tensor_copy(
                    svv[:, i:i + 1], ixb[:, 1:2]), reads=["moix"], writes=["mosvv"])
                tr.op("vector", lambda e, i=i: e.tensor_copy(
                    dstu[:, i:i + 1], ixb[:, 2:3]), reads=["moix"], writes=["modtu"])
                tr.idma(lambda e, i=i: e.indirect_dma_start(
                    out=stage[:], out_offset=None, in_=s_nx2T,
                    in_offset=IndirectOffsetOnAxis(ap=giu[:, i:i + 1], axis=0)),
                    reads=["mogiu", "s_nx2T_z"]
                    + [f"s_nx2T{t}" for t in range(NTI)], writes=["mostg"])
                for k in range(NT):
                    tr.op("tensor", lambda e, k=k: e.transpose(
                        pb[6 + k % 2][:, 0:128], stage[:, k * 128:(k + 1) * 128],
                        ident[:]),
                        reads=["mostg", "c_ident"], writes=[f"pb{6 + k % 2}"])
                    tr.op("vector", lambda e, k=k, i=i: e.tensor_copy(
                        xof[:, k, i * 128:(i + 1) * 128], pb[6 + k % 2][:, 0:128]),
                        reads=[f"pb{6 + k % 2}"], writes=[kxof])
            for fg in range(16):
                wk = f"mow1{fg % 2}"
                for k in range(NT):
                    tr.dma("sync", out=wf1[fg % 2][:, k, :],
                           in_=w_fc1T[e_, k * 128:(k + 1) * 128,
                                      fg * 256:(fg + 1) * 256],
                           writes=[wk] if k == NT - 1 else [wk + "p"])
                for f2 in range(2):
                    ft = fg * 2 + f2
                    bank = 2 + (ft % 4)
                    for k in range(NT):
                        tr.op("tensor", lambda e, fg=fg, f2=f2, k=k, bank=bank: e.matmul(
                            pb[bank][:, :], wf1[fg % 2][:, k, f2 * 128:(f2 + 1) * 128],
                            xof[:, k, :], start=(k == 0), stop=(k == NT - 1)),
                            reads=[wk, wk + "p", kxof],
                            writes=[f"pb{bank}"], mark=(k == NT - 1))
                    tr.op("scalar", lambda e, e_=e_, ft=ft, bank=bank: e.activation(
                        A[ft // 4][:, (ft % 4) * QC:(ft % 4 + 1) * QC],
                        pb[bank][:, :], AF.Gelu,
                        bias=fc1b[:, e_ * 32 + ft:e_ * 32 + ft + 1], scale=1.0),
                        reads=[f"pb{bank}", "smallsF"], writes=[f"h1_{ft}"])
            for ft in range(32):
                wk2 = f"mow2{ft % 2}"
                tr.dma("sync", out=wf2[ft % 2][:],
                       in_=w_fc2T[e_, ft * 128:(ft + 1) * 128, :], writes=[wk2])
                for dm in range(NT):
                    tr.op("tensor", lambda e, ft=ft, dm=dm: e.matmul(
                        pb[dm][:, :], wf2[ft % 2][:, dm * 128:(dm + 1) * 128],
                        A[ft // 4][:, (ft % 4) * QC:(ft % 4 + 1) * QC],
                        start=(ft == 0), stop=(ft == 31)),
                        reads=[wk2, f"h1_{ft}"], writes=[f"pb{dm}"],
                        mark=(dm == NT - 1))
            for dm in range(NT):
                tr.op("scalar", lambda e, e_=e_, dm=dm: e.activation(
                    xof[:, dm, :], pb[dm][:, :], AF.Copy, bias=0.0, scale=1.0),
                    reads=[f"pb{dm}"], writes=[kxof])
                tr.op("vector", lambda e, e_=e_, dm=dm: e.tensor_scalar(
                    xof[:, dm, :], xof[:, dm, :],
                    fc2b[:, e_ * NT + dm:e_ * NT + dm + 1], None, ALU.add),
                    reads=[kxof, "smallsF"], writes=[kxof])
            for i in range(4):
                for dm in range(NT):
                    tr.op("tensor", lambda e, dm=dm, i=i: e.transpose(
                        pb[6 + dm % 2][:, 0:128], xof[:, dm, i * 128:(i + 1) * 128],
                        ident[:]),
                        reads=[kxof, "c_ident"], writes=[f"pb{6 + dm % 2}"])
                    tr.op("vector", lambda e, dm=dm, i=i: e.tensor_copy(
                        stage[:, dm * 128:(dm + 1) * 128], pb[6 + dm % 2][:, 0:128]),
                        reads=[f"pb{6 + dm % 2}"], writes=["mostg"])
                tr.op("vector", lambda e, i=i: e.tensor_scalar(
                    stage[:], stage[:], svv[:, i:i + 1], None, ALU.mult),
                    reads=["mostg", "mosvv"], writes=["mostg"])
                tr.idma(lambda e, i=i: e.indirect_dma_start(
                    out=s_moeT, out_offset=IndirectOffsetOnAxis(
                        ap=dstu[:, i:i + 1], axis=0),
                    in_=stage[:], in_offset=None),
                    reads=["mostg", "modtu"], writes=["s_moeT"])
    tr.barrier()
    g7b.close()
    for tt in range(NTI):
        for rank in range(2):
            tr.dma("sync", out=stage[:], in_=s_moeT[rank * 2048 + tt * 128:
                                                    rank * 2048 + (tt + 1) * 128, :],
                   reads=["s_moeT"], writes=["mostg"])
            for dm in range(NT):
                tr.op("tensor", lambda e, dm=dm: e.transpose(
                    pb[6 + dm % 2][:, 0:128], stage[:, dm * 128:(dm + 1) * 128],
                    ident[:]),
                    reads=["mostg", "c_ident"], writes=[f"pb{6 + dm % 2}"])
                tr.op("vector", lambda e, dm=dm, tt=tt: e.tensor_add(
                    B[dm][:, tt * 128:(tt + 1) * 128],
                    B[dm][:, tt * 128:(tt + 1) * 128], pb[6 + dm % 2][:, 0:128]),
                    reads=[f"pb{6 + dm % 2}", f"x1_{dm}"], writes=[f"x1_{dm}"])


def build_core(upto=99, taps=()):
    nc = bass.Bass("TRN2", target_bir_lowering=False)
    es = contextlib.ExitStack()

    def din(name, shape):
        return nc.dram_tensor(name, shape, F32, kind="ExternalInput").ap()

    x_in = din("x_in", [NT, 128, L])
    w_minT = din("w_minT", [D, 2 * DI])
    w_mxT = din("w_mxT", [NTI, 128, DTR + 2 * DST])
    w_mdtT = din("w_mdtT", [DTR, DI])
    w_moutT = din("w_moutT", [DI, D])
    c_ones = din("c_ones", [128, 128])
    c_ident = din("c_ident", [128, 128])
    c_e32 = din("c_e32", [32, 32 * 128])
    sm_alog = din("sm_alog", [128, NTI * DST])
    sm_d_in = din("sm_d", [128, NTI])
    sm_convb_in = din("sm_convb", [128, NTI])
    sm_convw_in = din("sm_convw", [128, NTI * 4])
    sm_bdt_in = din("sm_bdt", [128, NTI])

    w_qkv = din("w_qkv", [D, 3 * D])
    w_wo = din("w_wo", [D, D])
    c_masks = din("c_masks", [128, 4, QC])
    smf_in = din("smf", [128, 5 * NT])
    bqkvh_in = din("bqkvh", [64, 48])
    l0gw_in = din("l0gw", [NT, 128, NE])
    fc1b_in = din("fc1b", [128, NE * 32])
    fc2b_in = din("fc2b", [128, NE * NT])
    c_e8 = din("c_e8", [8, 8 * 128])
    c_iotaf = din("c_iotaf", [128, NTI])
    c_rowc = din("c_rowc", [8, 1])
    c_idxfill = din("c_idxfill", [8192, 3])
    l1gw_in = din("l1gw", [NT, 128, NE])
    w_e0T = din("w_e0T", [D, D])
    w_e1T = din("w_e1T", [D, D])
    w_owT = din("w_owT", [D, D])
    w_dqT = din("w_dqT", [D, 512])
    w_dkvT = din("w_dkvT", [D, 512])
    w_uqT = din("w_uqT", [512, D])
    w_ukT = din("w_ukT", [512, D])
    w_uvT = din("w_uvT", [512, D])
    w_l1wo = din("w_l1wo", [D, D])
    smf2_in = din("smf2", [128, 2 * NT])
    ropec_in = din("ropec", [NTI, 128, 64])
    ropes_in = din("ropes", [NTI, 128, 64])
    w_fc1T = din("w_fc1T", [NE, D, DFF])
    w_fc2T = din("w_fc2T", [NE, DFF, D])

    s_x = nc.dram_tensor("s_x", [NTI, 128, L], F32).ap()
    s_zs = nc.dram_tensor("s_zs", [NTI, 128, L], F32).ap()
    s_dt = nc.dram_tensor("s_dt", [NTI, 128, L], F32).ap()
    s_s1 = nc.dram_tensor("s_s1", [NT, 128, L], F32).ap()
    s_xn = nc.dram_tensor("s_xn", [NT, 128, L], F32).ap()
    s_bc = nc.dram_tensor("s_bc", [32, L], F32).ap()
    s_nx2 = nc.dram_tensor("s_nx2", [NT, 128, L], F32).ap()
    s_x1r = nc.dram_tensor("s_x1r", [NT, 128, L], F32).ap()
    s_nx2T = nc.dram_tensor("s_nx2T", [2049, D], F32).ap()
    s_idx = nc.dram_tensor("s_idx", [10240, 3], F32).ap()
    s_moeT = nc.dram_tensor("s_moeT", [12288, D], F32).ap()
    s_g1 = nc.dram_tensor("s_g1", [NT, 128, L], F32).ap()
    s_xn1 = nc.dram_tensor("s_xn1", [NT, 128, L], F32).ap()

    y_out = nc.dram_tensor("y", [NT, 128, L], F32, kind="ExternalOutput").ap()
    tap_outs = {}

    with es:
        block = es.enter_context(nc.Block())
        tr = Tr(nc, block, es)
        tr.flags = taps
        pb = [es.enter_context(nc.psum_tensor(f"pb{i}", [128, QC], F32))
              for i in range(8)]

        P = Grp(nc)
        ones = P.sb("ones", [128, 128])
        ident = P.sb("ident", [128, 128])
        negA = P.sb("negA", [128, NTI * DST])
        sm_d = P.sb("smd", [128, NTI])
        sm_convb = P.sb("smcb", [128, NTI])
        sm_convw = P.sb("smcw", [128, NTI * 4])
        sm_bdt = P.sb("smbd", [128, NTI])
        cpool = P.sb("cpool", [128, 4])
        A = [P.sb(f"arA{t}", [128, L]) for t in range(NT)]
        PB_holder = {}

        tr.op("vector", lambda e: e.memset(cpool[:, 0:1], EPS), writes=["cpool_a"])
        tr.op("vector", lambda e: e.memset(cpool[:, 1:2], 1.0), writes=["cpool_b"])
        tr.op("vector", lambda e: e.memset(cpool[:, 2:4], 0.0), writes=["cpool"],
              reads=["cpool_a", "cpool_b"])
        tr.dma("sync", out=ones[:], in_=c_ones, writes=["c_ones"])
        tr.dma("sync", out=ident[:], in_=c_ident, writes=["c_ident"])
        tr.dma("sync", out=negA[:], in_=sm_alog, writes=["negA_raw"])
        tr.dma("sync", out=sm_d[:], in_=sm_d_in, writes=["smalls_d"])
        tr.dma("sync", out=sm_convb[:], in_=sm_convb_in, writes=["smalls_cb"])
        tr.dma("sync", out=sm_convw[:], in_=sm_convw_in, writes=["smalls_cw"])
        tr.dma("sync", out=sm_bdt[:], in_=sm_bdt_in, writes=["smalls_bd"])
        tr.op("scalar", lambda e: e.activation(negA[:], negA[:], AF.Exp,
                                               bias=0.0, scale=1.0),
              reads=["negA_raw"], writes=["negA"])
        tr.op("vector", lambda e: e.tensor_scalar_mul(negA[:], negA[:], -1.0),
              reads=["negA"], writes=["negA"])
        tr.op("gpsimd", lambda e: e.tensor_copy(sm_d[:, 0:1], sm_d[:, 0:1]),
              reads=["smalls_d", "smalls_cb", "smalls_cw", "smalls_bd"],
              writes=["smalls"])

        xn = A
        for t in range(NT):
            tr.dma("sync", out=xn[t][:], in_=x_in[t, :, :], writes=[f"xn{t}"])

        G1a = Grp(nc)
        emit_layernorm(tr, G1a, pb, ones, cpool, xn, [f"xn{t}" for t in range(NT)],
                       xn, [f"xn{t}" for t in range(NT)], tag="ln0")
        tr.barrier()
        G1a.close()
        for t in range(NT):
            tr.dma("gpsimd", out=s_xn[t, :, :], in_=xn[t][:],
                   reads=[f"xn{t}"], writes=[f"s_xn{t}"])

        def tap(name, sb, key):
            if name in taps:
                out = nc.dram_tensor("tap_" + name, list(sb.shape), F32,
                                     kind="ExternalOutput").ap()
                tap_outs[name] = True
                tr.dma("gpsimd", out=out, in_=sb[:], reads=[key])

        for t in range(NT):
            tap(f"xn{t}", xn[t], f"xn{t}")

        if upto >= 2:
            G2 = Grp(nc)
            bc_sb = G2.sb("ma_bc", [32, L])
            w_mxT_sb = G2.sb("wmx", [128, NTI, DTR + 2 * DST])
            w_mdtT_sb = G2.sb("wmdt", [DTR, DI])
            for t in range(NTI):
                tr.dma("sync", out=w_mxT_sb[:, t, :], in_=w_mxT[t, :, :],
                       writes=["w_mxT"] if t == NTI - 1 else ["w_mxT_p"])
            tr.dma("sync", out=w_mdtT_sb[:], in_=w_mdtT, writes=["w_mdtT"])
            dtr_sb = emit_mamba_A(tr, G2, pb, cpool, bc_sb, xn, w_minT,
                                  w_mxT_sb, w_mdtT_sb, sm_convw, sm_convb,
                                  sm_bdt, s_x, s_zs, s_dt)
            tr.dma("gpsimd", out=s_bc, in_=bc_sb[:], reads=["ma_bc"],
                   writes=["s_bc"])
            tap("dtr", dtr_sb, "ma_dtr")
            tap("bc", bc_sb, "ma_bc")

        if upto >= 3:
            tr.barrier()
            G2.close()
            PB = Grp(nc)
            B = [PB.sb(f"arB{t}", [128, L]) for t in range(NT)]
            PB_holder["B"] = B
            G3 = Grp(nc)
            e32_sb = G3.sb("e32", [32, 32 * 128])
            bc_sb = G3.sb("ma_bc3", [32, L])
            tr.dma("sync", out=e32_sb[:], in_=c_e32, writes=["c_e32"])
            tr.dma("sync", out=bc_sb[:], in_=s_bc, reads=["s_bc"],
                   writes=["ma_bc"])
            hcar_ref = []
            emit_mamba_scan(tr, G3, pb, A, B, e32_sb, bc_sb, negA, sm_d,
                            w_moutT, s_x, s_dt, s_zs, s_s1, hcar_ref)
            stp_holder = {}

            def tap_hbm(base, dram, n, keyf):
                if base not in taps:
                    return
                if "stp" not in stp_holder:
                    stp_holder["stp"] = G3.sb("stapx", [128, QC])
                stp = stp_holder["stp"]
                for j in range(n):
                    out = nc.dram_tensor(f"tap_{base}_{j}", [128, L], F32,
                                         kind="ExternalOutput").ap()
                    for q in range(NQ):
                        tr.dma("sync", out=stp[:], in_=dram[j, :, qslice(q)],
                               reads=[keyf(j)], writes=["stap" + base])
                        tr.dma("gpsimd", out=out[:, qslice(q)], in_=stp[:],
                               reads=["stap" + base], writes=["stapr" + base])
            tap_hbm("mx", s_x, NTI, lambda j: f"s_x{j}")
            tap_hbm("mzs", s_zs, NTI, lambda j: f"s_zs{j}")
            tap_hbm("mdt", s_dt, NTI, lambda j: f"s_dt{j}")
            tap_hbm("s1", s_s1, NT, lambda j: f"s_s1{j}")
            if "hcar" in taps:
                outh = nc.dram_tensor("tap_hcar", [128, NTI * DST], F32,
                                      kind="ExternalOutput").ap()
                tr.dma("gpsimd", out=outh, in_=hcar_ref[0][:], reads=["schc"])
            if "ylast" in taps:
                for j in range(4):
                    out = nc.dram_tensor(f"tap_ylast_{j}", [128, L], F32,
                                         kind="ExternalOutput").ap()
                    tr.dma("gpsimd", out=out, in_=B[j][:],
                           reads=[f"scy{4*j}", f"scy{4*j+1}", f"scy{4*j+2}", f"scy{4*j+3}"])
            if "bclast" in taps:
                for j in range(4):
                    out = nc.dram_tensor(f"tap_bclast_{j}", [128, L], F32,
                                         kind="ExternalOutput").ap()
                    tr.dma("gpsimd", out=out, in_=A[j][:],
                           reads=[f"scb{4*j}", f"scb{4*j+1}", f"scb{4*j+2}", f"scb{4*j+3}"])

        if upto >= 4:
            tr.barrier()
            G3.close()
            B = PB_holder["B"]
            PC = Grp(nc)
            smf = PC.sb("smf_sb", [128, 5 * NT])
            bqkvh = PC.sb("bqkvh_sb", [64, 48])
            bq8 = PC.sb("bq8", [64, 16])

            l0gw = PC.sb("l0gw_sb", [128, NT, NE])
            fc1b = PC.sb("fc1b_sb", [128, NE * 32])
            fc2b = PC.sb("fc2b_sb", [128, NE * NT])
            smf2 = PC.sb("smf2_sb", [128, 2 * NT])
            tr.dma("sync", out=smf[:], in_=smf_in, writes=["smallsF"])
            tr.dma("sync", out=bqkvh[:], in_=bqkvh_in, writes=["smallsF"])

            for k in range(NT):
                tr.dma("sync", out=l0gw[:, k, :], in_=l0gw_in[k, :, :],
                       writes=["l0gw"])
            tr.dma("sync", out=fc1b[:], in_=fc1b_in, writes=["smallsF"])
            tr.dma("sync", out=fc2b[:], in_=fc2b_in, writes=["smallsF"])
            tr.dma("sync", out=smf2[:], in_=smf2_in, writes=["smallsF"])
            w_ln1w, w_ln1b = smf[:, 0:NT], smf[:, NT:2 * NT]
            w_ln2w, w_ln2b = smf[:, 2 * NT:3 * NT], smf[:, 3 * NT:4 * NT]
            w_bo = smf[:, 4 * NT:5 * NT]
            G4 = Grp(nc)
            emit_ffn_prologue(tr, G4, pb, ones, cpool, A, B, s_xn,
                              w_ln1w, w_ln1b, w_bo, bqkvh, bq8)
            for t in range(NT):
                tap(f"nxf{t}", A[t], f"nx{t}")

        if upto >= 5:
            tr.barrier()
            G4.close()
            G5 = Grp(nc)
            masks_sb = G5.sb("masks", [128, 4, QC])
            tr.dma("sync", out=masks_sb[:], in_=c_masks, writes=["masks"])
            emit_attn_l0(tr, G5, pb, ones, A, B, w_qkv, w_wo, bqkvh, bq8,
                         masks_sb)
            for t in range(NT):
                tap(f"x1_{t}", B[t], f"x1_{t}")

        if upto >= 6:
            tr.barrier()
            G5.close()
            G6 = Grp(nc)
            emit_ln2_store(tr, G6, pb, ones, cpool, B, w_ln2w, w_ln2b, s_nx2)
            tr.barrier()
            G6.close()

        if upto >= 7:
            G7 = Grp(nc)
            emit_moe_routed(tr, G7, pb, ones, ident, A, B, w_fc1T, w_fc2T,
                            fc1b, fc2b, l0gw, c_iotaf, c_rowc, c_idxfill,
                            s_nx2, s_nx2T, s_idx, s_moeT)
            for t in range(NT):
                tap(f"ffn{t}", B[t], f"x1_{t}")

        if upto >= 8:
            tr.barrier()
            G7.close()
            G8 = Grp(nc)
            emit_combine_l0(tr, G8, pb, B, x_in, s_s1, s_x1r)
            for t in range(NT):
                tap(f"xr{t}", B[t], f"xr{t}")

        if upto >= 9:
            tr.barrier()
            G8.close()
            G9 = Grp(nc)
            emit_layernorm(tr, G9, pb, ones, cpool, B,
                           [f"xr{t}" for t in range(NT)], B,
                           [f"xn1{t}" for t in range(NT)], tag="ln9")
            tr.barrier()
            G9.close()

        if upto >= 10:
            G10 = Grp(nc)
            e8b = G10.sb("e8b", [8, 8 * 128])
            tr.dma("sync", out=e8b[:], in_=c_e8, writes=["c_e8"])
            emit_l1ffn(tr, G10, pb, ones, ident, e8b, A, B, w_e0T, w_e1T,
                       w_owT, l1gw_in, s_g1)

        if upto >= 11:
            tr.barrier()
            G10.close()
            G11 = Grp(nc)
            ropec_sb = G11.sb("ropec_sb", [128, NTI, 64])
            ropes_sb = G11.sb("ropes_sb", [128, NTI, 64])
            for t in range(NTI):
                tr.dma("sync", out=ropec_sb[:, t, :], in_=ropec_in[t, :, :],
                       writes=["rope"])
                tr.dma("sync", out=ropes_sb[:, t, :], in_=ropes_in[t, :, :],
                       writes=["rope"])
            emit_attn_l1(tr, G11, pb, ones, ident, cpool, A, B, w_dqT, w_dkvT,
                         w_uqT, w_ukT, w_uvT, w_l1wo, smf2, ropec_sb, ropes_sb,
                         s_xn1)
            for t in range(NT):
                tap(f"s1a{t}", B[t], f"s1a{t}")

        if upto >= 12:
            tr.barrier()
            G11.close()
            G12 = Grp(nc)
            emit_final(tr, G12, B, s_x1r, s_g1, y_out)

        tr.finish()
    return nc


def _prep_common(i):
    c = {}
    A = np.ascontiguousarray
    c["w_minT"] = A(i["L0_attn_W_in"].T)
    c["w_mxT"] = A(i["L0_attn_W_x"].T.reshape(NTI, 128, 96))
    c["w_mdtT"] = A(i["L0_attn_W_dt"].T)
    c["w_moutT"] = A(i["L0_attn_W_out"].T)
    c["c_ones"] = np.ones((128, 128), np.float32)
    c["c_ident"] = np.eye(128, dtype=np.float32)
    e32 = np.zeros((32, 32, 128), np.float32)
    for j in range(32):
        e32[j, j, :] = 1.0
    c["c_e32"] = A(e32.reshape(32, 32 * 128))
    c["sm_alog"] = A(i["L0_attn_A_log"].reshape(NTI, 128, DST)
                     .transpose(1, 0, 2).reshape(128, NTI * DST))
    c["sm_d"] = A(i["L0_attn_D"].reshape(NTI, 128).T)
    c["sm_convb"] = A(i["L0_attn_conv_b"].reshape(NTI, 128).T)
    c["sm_convw"] = A(i["L0_attn_conv_w"][:, 0, :].reshape(NTI, 128, 4)
                      .transpose(1, 0, 2).reshape(128, NTI * 4))
    c["sm_bdt"] = A(i["L0_attn_b_dt"].reshape(NTI, 128).T)
    c["w_qkv"] = A(i["L0_ffn_Wqkv"])
    c["w_wo"] = A(i["L0_ffn_Wo"])
    m = np.zeros((128, 4, QC), np.float32)
    for j in range(4):
        for kl in range(128):
            m[kl, j, :] = (np.arange(QC) >= kl + 128 * j).astype(np.float32)
    c["c_masks"] = m
    smf = np.zeros((128, 5 * NT), np.float32)
    for idx, nm in enumerate(["L0_ffn_ln1_w", "L0_ffn_ln1_b", "L0_ffn_ln2_w",
                              "L0_ffn_ln2_b", "L0_ffn_bo"]):
        smf[:, idx * NT:(idx + 1) * NT] = i[nm].reshape(NT, 128).T
    c["smf"] = smf
    c["bqkvh"] = A(i["L0_ffn_bqkv"].reshape(3, 16, 64).transpose(2, 1, 0)
                   .reshape(64, 48))
    c["l0gw"] = A(i["L0_ffn_gate_W"].reshape(NT, 128, NE))
    c["fc1b"] = A(i["L0_ffn_fc1_b"].reshape(NE, 32, 128).transpose(2, 0, 1)
                  .reshape(128, NE * 32))
    c["fc2b"] = A(i["L0_ffn_fc2_b"].reshape(NE, NT, 128).transpose(2, 0, 1)
                  .reshape(128, NE * NT))
    e8m = np.zeros((8, 8, 128), np.float32)
    for j in range(8):
        e8m[j, j, :] = 1.0
    c["c_e8"] = A(e8m.reshape(8, 1024))
    c["c_iotaf"] = A((np.arange(NTI)[None, :] * 128
                      + np.arange(128)[:, None]).astype(np.float32))
    c["c_rowc"] = A((np.arange(8, dtype=np.float32) * CAP - 1.0).reshape(8, 1))
    fillr = np.zeros((8192, 3), np.float32)
    fillr[:, 0] = 2048.0
    fillr[:, 2] = 4096.0 + np.arange(8192)
    c["c_idxfill"] = A(fillr)
    c["w_fc1T"] = A(i["L0_ffn_fc1_w"].transpose(0, 2, 1))
    c["w_fc2T"] = A(i["L0_ffn_fc2_w"].transpose(0, 2, 1))
    c["l1gw"] = A(i["L1_ffn_gate_W"].T.reshape(NT, 128, NE))
    c["w_e0T"] = A(i["L1_ffn_expert_W"][0].T)
    c["w_e1T"] = A(i["L1_ffn_expert_W"][1].T)
    c["w_owT"] = A(i["L1_ffn_out_W"].T)
    c["w_dqT"] = A(i["L1_attn_w_down_q"].T)
    c["w_dkvT"] = A(i["L1_attn_w_down_kv"].T)
    c["w_uqT"] = A(i["L1_attn_w_up_q"].T)
    c["w_ukT"] = A(i["L1_attn_w_up_k"].T)
    c["w_uvT"] = A(i["L1_attn_w_up_v"].T)
    c["w_l1wo"] = A(i["L1_attn_w_out"].T)
    smf2 = np.zeros((128, 2 * NT), np.float32)
    smf2[:, 0:NT] = i["L1_attn_ln_scale"].reshape(NT, 128).T
    smf2[:, NT:2 * NT] = i["L1_attn_ln_bias"].reshape(NT, 128).T
    c["smf2"] = smf2
    inv_freq = 1.0 / 10000.0 ** (np.arange(0, 64, 2, dtype=np.float32) / 64)
    tpos = np.arange(L, dtype=np.float32)
    freqs = np.outer(tpos, inv_freq)
    cos_t = np.concatenate([np.cos(freqs), np.cos(freqs)], -1)[:, ::2]
    sin_t = np.concatenate([np.sin(freqs), np.sin(freqs)], -1)[:, ::2]
    c["ropec"] = A(np.concatenate([cos_t, cos_t], 1).reshape(NTI, 128, 64))
    c["ropes"] = A(np.concatenate([sin_t, sin_t], 1).reshape(NTI, 128, 64))
    return {k: np.asarray(v, np.float32) for k, v in c.items()}


def _per_core(i, b):
    x = np.asarray(i["L0_attn_hidden_states"][b], np.float32)
    return {"x_in": np.ascontiguousarray(x.T.reshape(NT, 128, L))}


def kernel(**inputs):
    common = _prep_common(inputs)
    nc = build_core(upto=99)
    in_maps = []
    for b in range(8):
        m = dict(common)
        m.update(_per_core(inputs, b))
        in_maps.append(m)
    res = run_bass_kernel_spmd(nc, in_maps, core_ids=list(range(8)))
    outs = []
    for b in range(8):
        y = res.results[b]["y"].reshape(D, L)
        outs.append(y.T)
    return np.stack(outs).astype(np.float32)


# version 06
# speedup vs baseline: 1.2923x; 1.0278x over previous
import contextlib
import numpy as np
import concourse.bass as bass
import concourse.mybir as mybir
from concourse.bass_utils import run_bass_kernel_spmd
from concourse.bass import IndirectOffsetOnAxis

F32 = mybir.dt.float32
AF = mybir.ActivationFunctionType
ALU = mybir.AluOpType

L = 2048
D = 1024
DI = 2048
DST = 16
DTR = 64
NH, HD = 16, 64
DFF = 4096
NE = 8
NT = D // 128
NTI = DI // 128
QC = 512
NQ = L // QC
EPS = 1e-5

ENGS = ("tensor", "vector", "scalar", "sync", "gpsimd")
NDMASEM = 24


class Tr:

    def __init__(self, nc, block, es):
        self.nc = nc
        self.block = block
        self.es = es
        self.sems = {}
        self.cnt = {}
        self.wm = {}
        self.wr = {}
        self.rd = {}
        self.pend = {}
        self.nwaits = 0
        self.nops = 0
        self._dmaidx = 0
        for e in ENGS:
            self.pend[e] = []

    def _sem(self, lane):
        if lane not in self.sems:
            self.sems[lane] = self.es.enter_context(self.nc.semaphore(lane))
            self.cnt[lane] = 0
        return self.sems[lane]

    def _wait(self, eng_name, eng, lane, val):
        if val <= 0:
            return
        if lane.startswith("F_"):
            lane = "P_" + lane[2:]
        key = (eng_name, lane)
        if self.wm.get(key, 0) >= val:
            return
        eng.wait_ge(self._sem(lane), val)
        self.wm[key] = val
        self.nwaits += 1

    def _lane_wait(self, eng_name, eng, lane, val, kind, k):
        if val is None:
            raise RuntimeError(f"{kind} on {k!r} pending unmarked event")
        self._wait(eng_name, eng, lane, val)

    def _deps(self, eng_name, eng, reads, writes, own_p=True):
        own = ("P_" + eng_name,) if own_p else ()
        for k in reads:
            w = self.wr.get(k)
            if w is not None and w[0] not in own:
                self._lane_wait(eng_name, eng, w[0], w[1], "read", k)
        for k in writes:
            w = self.wr.get(k)
            if w is not None and w[0] not in own:
                self._lane_wait(eng_name, eng, w[0], w[1], "WAW", k)
            for lane, val in self.rd.get(k, ()):
                if lane in own:
                    continue
                self._lane_wait(eng_name, eng, lane, val, "WAR", k)

    def op(self, eng_name, fn, reads=(), writes=(), mark=True, fenced=False):
        if fenced:
            assert mark
        sec = getattr(self.block, eng_name)
        lane = "P_" + eng_name
        self._sem(lane)
        tr = self

        @sec
        def _(eng):
            tr._deps(eng_name, eng, reads, writes)
            inst = fn(eng)
            if mark:
                inst.then_inc(tr._sem(lane), 1)
        if mark:
            self.cnt[lane] += 1
            val = self.cnt[lane]
            for k in self.pend[eng_name]:
                lw = self.wr.get(k)
                if lw is not None and lw[0] == lane and lw[1] is None:
                    self.wr[k] = (lane, val)
                rl = self.rd.get(k)
                if rl:
                    self.rd[k] = [(ln, val if (ln == lane and v is None) else v)
                                  for ln, v in rl]
            self.pend[eng_name] = []
            ev = (("F_" + eng_name) if fenced else lane, val)
        else:
            ev = (lane, None)
        for k in writes:
            self.wr[k] = ev
            self.rd[k] = []
            if not mark:
                self.pend[eng_name].append(k)
        for k in reads:
            self.rd.setdefault(k, []).append(ev)
            if not mark and k not in self.pend[eng_name]:
                self.pend[eng_name].append(k)
        self.nops += 1

    def dma(self, eng_name, out, in_, reads=(), writes=()):
        sec = getattr(self.block, eng_name)
        lane = f"D{self._dmaidx % NDMASEM}"
        self._dmaidx += 1
        self._sem(lane)
        prev = self.cnt[lane]
        self.cnt[lane] = prev + 16
        tr = self

        @sec
        def _(eng):
            tr._deps(eng_name, eng, reads, writes, own_p=False)
            tr._wait(eng_name, eng, lane, prev)
            inst = eng.dma_start(out=out, in_=in_)
            inst.then_inc(tr._sem(lane), 16)
        ev = (lane, self.cnt[lane])
        for k in writes:
            self.wr[k] = ev
            self.rd[k] = []
        for k in reads:
            self.rd.setdefault(k, []).append(ev)
        self.nops += 1

    def idma(self, fn, reads=(), writes=()):
        lane = f"D{self._dmaidx % NDMASEM}"
        self._dmaidx += 1
        self._sem(lane)
        prev = self.cnt[lane]
        self.cnt[lane] = prev + 16
        tr = self

        @self.block.gpsimd
        def _(eng):
            tr._deps("gpsimd", eng, reads, writes, own_p=False)
            tr._wait("gpsimd", eng, lane, prev)
            inst = fn(eng)
            inst.then_inc(tr._sem(lane), 16)
        ev = (lane, self.cnt[lane])
        for k in writes:
            self.wr[k] = ev
            self.rd[k] = []
        for k in reads:
            self.rd.setdefault(k, []).append(ev)
        self.nops += 1

    def barrier(self):
        vals = {ln: self.cnt.get(ln, 0) for ln in list(self.cnt)}
        for e in ENGS:
            lane = "P_" + e
            self._sem(lane)
            sec = getattr(self.block, e)
            tr = self
            myv = dict(vals)

            @sec
            def _(eng, e=e, myv=myv):
                for ln, v in myv.items():
                    if ln != "P_" + e:
                        tr._wait(e, eng, ln, v)
                eng.sem_inc(tr._sem("P_" + e), 1)
            self.cnt[lane] += 1
        for k in list(self.wr):
            lane, val = self.wr[k]
            if val is None:
                raise RuntimeError(f"barrier with pending unmarked write on {k!r}")
        for e in ENGS:
            for ln in list(self.cnt):
                if ln != "P_" + e:
                    self.wm[(e, ln)] = max(self.wm.get((e, ln), 0), vals.get(ln, 0))

    def finish(self):
        tr = self

        @self.block.gpsimd
        def _(gp):
            for ln in sorted(tr.sems):
                if ln.startswith("D"):
                    tr._wait("gpsimd", gp, ln, tr.cnt[ln])
            for e in ENGS:
                ln = "P_" + e
                if ln in tr.sems and e != "gpsimd":
                    tr._wait("gpsimd", gp, ln, tr.cnt[ln])


class Grp:
    def __init__(self, nc):
        self.nc = nc
        self.es = contextlib.ExitStack()

    def sb(self, name, shape, dtype=F32):
        return self.es.enter_context(self.nc.sbuf_tensor(name, shape, dtype))

    def close(self):
        self.es.close()


def qslice(q):
    return slice(q * QC, (q + 1) * QC)


def emit_layernorm(tr, g, pb, ones, cpool, in_tiles, in_keys, out_tiles, out_keys,
                   w_sb=None, b_sb=None, wkey=None, tag="ln", post=None):
    sq = g.sb(tag + "sq", [128, L])
    rowm = g.sb(tag + "rm", [1, L])
    rowr = g.sb(tag + "rr", [1, L])
    bnm = g.sb(tag + "bm", [128, L])
    brs = g.sb(tag + "br", [128, L])
    K = tag
    for t in range(NT):
        tr.op("scalar", lambda e, t=t: e.activation(sq[:], in_tiles[t][:], AF.Square,
                                                    bias=0.0, scale=1.0),
              reads=[in_keys[t]], writes=[K + "sq"])
        for q in range(NQ):
            tr.op("tensor", lambda e, t=t, q=q: e.matmul(
                pb[q][0:1, :], ones[:, 0:1], in_tiles[t][:, qslice(q)],
                start=(t == 0), stop=(t == NT - 1)),
                reads=[in_keys[t], "c_ones"], writes=[f"pb{q}"], mark=False)
            tr.op("tensor", lambda e, t=t, q=q: e.matmul(
                pb[4 + q][0:1, :], ones[:, 0:1], sq[:, qslice(q)],
                start=(t == 0), stop=(t == NT - 1)),
                reads=[K + "sq", "c_ones"], writes=[f"pb{4 + q}"],
                mark=(q == NQ - 1))
    for q in range(NQ):
        tr.op("vector", lambda e, q=q: e.tensor_scalar_mul(
            rowm[:, qslice(q)], pb[q][0:1, :], -1.0 / D),
            reads=[f"pb{q}"], writes=[K + "rm"], mark=(q == NQ - 1))
        tr.op("vector", lambda e, q=q: e.tensor_scalar_mul(
            rowr[:, qslice(q)], pb[4 + q][0:1, :], 1.0 / D),
            reads=[f"pb{4 + q}"], writes=[K + "rr"], mark=(q == NQ - 1))
    tr.op("vector", lambda e: e.tensor_mul(sq[0:1, 0:L], rowm[:], rowm[:]),
          reads=[K + "rm"], writes=[K + "sq"])
    tr.op("vector", lambda e: e.tensor_sub(rowr[:], rowr[:], sq[0:1, 0:L]),
          reads=[K + "sq", K + "rr"], writes=[K + "rr"])
    tr.op("scalar", lambda e: e.activation(rowr[:], rowr[:], AF.Sqrt, bias=cpool[0:1, 0:1], scale=1.0),
          reads=[K + "rr", "cpool"], writes=[K + "rr"])
    tr.op("vector", lambda e: e.reciprocal(rowr[:], rowr[:]),
          reads=[K + "rr"], writes=[K + "rr"])
    for q in range(NQ):
        tr.op("tensor", lambda e, q=q: e.matmul(pb[q][:, :], ones[0:1, :],
                                                rowm[:, qslice(q)], start=True, stop=True),
              reads=[K + "rm", "c_ones"], writes=[f"pb{q}"])
        tr.op("tensor", lambda e, q=q: e.matmul(pb[4 + q][:, :], ones[0:1, :],
                                                rowr[:, qslice(q)], start=True, stop=True),
              reads=[K + "rr", "c_ones"], writes=[f"pb{4 + q}"])
        tr.op("vector", lambda e, q=q: e.tensor_copy(bnm[:, qslice(q)], pb[q][:, :]),
              reads=[f"pb{q}"], writes=[K + "bm"], mark=(q == NQ - 1))
        tr.op("vector", lambda e, q=q: e.tensor_copy(brs[:, qslice(q)], pb[4 + q][:, :]),
              reads=[f"pb{4 + q}"], writes=[K + "br"], mark=(q == NQ - 1))
    for t in range(NT):
        tr.op("vector", lambda e, t=t: e.tensor_add(sq[:], in_tiles[t][:], bnm[:]),
              reads=[in_keys[t], K + "bm"], writes=[K + "sq"])
        if w_sb is None:
            tr.op("vector", lambda e, t=t: e.tensor_mul(out_tiles[t][:], sq[:], brs[:]),
                  reads=[K + "sq", K + "br"], writes=[out_keys[t]])
        else:
            tr.op("vector", lambda e, t=t: e.tensor_mul(sq[:], sq[:], brs[:]),
                  reads=[K + "sq", K + "br"], writes=[K + "sq"])
            tr.op("vector", lambda e, t=t: e.tensor_scalar(
                out_tiles[t][:], sq[:], w_sb[:, t:t + 1], b_sb[:, t:t + 1],
                ALU.mult, ALU.add),
                reads=[K + "sq", wkey], writes=[out_keys[t]])
        if post is not None:
            post(t)


def emit_mamba_A(tr, g, pb, cpool, bc_sb, xn, w_minT, w_mxT_sb, w_mdtT_sb,
                 sm_convw, sm_convb, sm_bdt, s_x, s_zs, s_dt):
    wbuf = [g.sb(f"maw{i}", [128, NT, 128]) for i in range(2)]
    xpad = g.sb("maxp", [128, L + 3])
    cb0 = g.sb("mac0", [128, L])
    cb1 = g.sb("mac1", [128, L])
    cb2 = g.sb("mac2", [128, L])
    xfin = [g.sb(f"maxf{i}", [128, L]) for i in range(2)]
    zbuf = cb0
    zs2 = [cb1, cb2]
    dtr_sb = g.sb("ma_dtr", [64, L])
    sp_sb = g.sb("masp", [128, L])
    dt_sb = [g.sb(f"madt{i}", [128, L]) for i in range(2)]

    tr.op("vector", lambda e: e.memset(xpad[:, 0:3], 0.0), writes=["maxpad_z"])

    for e_ in range(32):
        wk = f"maw{e_ % 2}"
        for k in range(NT):
            tr.dma("sync", out=wbuf[e_ % 2][:, k, :],
                   in_=w_minT[k * 128:(k + 1) * 128, e_ * 128:(e_ + 1) * 128],
                   writes=[wk] if k == NT - 1 else [wk + "part"])
        for q in range(NQ):
            for k in range(NT):
                tr.op("tensor", lambda e, e_=e_, q=q, k=k: e.matmul(
                    pb[q][:, :], wbuf[e_ % 2][:, k, :], xn[k][:, qslice(q)],
                    start=(k == 0), stop=(k == NT - 1)),
                    reads=[wk, wk + "part", f"xn{k}"], writes=[f"pb{q}"],
                    mark=(q == NQ - 1 and k == NT - 1))
        if e_ < 16:
            for q in range(NQ):
                tr.op("vector", lambda e, q=q: e.tensor_scalar(
                    xpad[:, 3 + q * QC:3 + (q + 1) * QC], pb[q][:, :],
                    0.0, 1.0, ALU.max, ALU.min),
                    reads=[f"pb{q}", "maxpad_z"], writes=["maxp"], mark=(q == NQ - 1))
            tr.op("vector", lambda e, e_=e_: e.tensor_scalar(
                cb0[:], xpad[:, 0:L], sm_convw[:, 4 * e_:4 * e_ + 1], None, ALU.mult),
                reads=["maxp", "smalls"], writes=["mac0"])
            tr.op("vector", lambda e, e_=e_: e.scalar_tensor_tensor(
                cb1[:], xpad[:, 1:1 + L], sm_convw[:, 4 * e_ + 1:4 * e_ + 2], cb0[:],
                ALU.mult, ALU.add), reads=["maxp", "mac0", "smalls"], writes=["mac1"])
            tr.op("vector", lambda e, e_=e_: e.scalar_tensor_tensor(
                cb0[:], xpad[:, 2:2 + L], sm_convw[:, 4 * e_ + 2:4 * e_ + 3], cb1[:],
                ALU.mult, ALU.add), reads=["maxp", "mac1", "smalls"], writes=["mac0"])
            tr.op("vector", lambda e, e_=e_: e.scalar_tensor_tensor(
                cb1[:], xpad[:, 3:3 + L], sm_convw[:, 4 * e_ + 3:4 * e_ + 4], cb0[:],
                ALU.mult, ALU.add), reads=["maxp", "mac0", "smalls"], writes=["mac1"])
            tr.op("scalar", lambda e, e_=e_: e.activation(
                cb2[:], cb1[:], AF.Silu, bias=sm_convb[:, e_:e_ + 1], scale=1.0),
                reads=["mac1", "smalls"], writes=["mac2"])
            xk = f"maxf{e_ % 2}"
            tr.op("vector", lambda e, e_=e_: e.tensor_scalar(
                xfin[e_ % 2][:], cb2[:], 0.0, 1.0, ALU.max, ALU.min),
                reads=["mac2"], writes=[xk])
            tr.dma("gpsimd", out=s_x[e_, :, :], in_=xfin[e_ % 2][:],
                   reads=[xk], writes=[f"s_x{e_}"])
            for q in range(NQ):
                tr.op("tensor", lambda e, e_=e_, q=q: e.matmul(
                    pb[4 + q][0:96, :], w_mxT_sb[:, e_, :], xfin[e_ % 2][:, qslice(q)],
                    start=(e_ == 0), stop=(e_ == 15)),
                    reads=[xk, "w_mxT"], writes=[f"pb{4 + q}"],
                    mark=(q == NQ - 1))
        else:
            zk = "mac1" if e_ % 2 == 0 else "mac2"
            for q in range(NQ):
                tr.op("vector", lambda e, q=q: e.tensor_scalar(
                    zbuf[:, qslice(q)], pb[q][:, :], 0.0, 1.0, ALU.max, ALU.min),
                    reads=[f"pb{q}"], writes=["mac0"], mark=(q == NQ - 1))
            tr.op("scalar", lambda e, e_=e_: e.activation(
                zs2[e_ % 2][:], zbuf[:], AF.Silu, bias=0.0, scale=1.0),
                reads=["mac0"], writes=[zk])
            tr.dma("gpsimd", out=s_zs[e_ - 16, :, :], in_=zs2[e_ % 2][:],
                   reads=[zk], writes=[f"s_zs{e_ - 16}"])

    for q in range(NQ):
        tr.op("vector", lambda e, q=q: e.tensor_scalar(
            dtr_sb[:, qslice(q)], pb[4 + q][0:64, :], 0.0, 1.0, ALU.max, ALU.min),
            reads=[f"pb{4 + q}"], writes=["ma_dtr"], mark=(q == NQ - 1))
        tr.op("vector", lambda e, q=q: e.tensor_copy(
            bc_sb[:, qslice(q)], pb[4 + q][64:96, :]),
            reads=[f"pb{4 + q}"], writes=["ma_bc"], mark=(q == NQ - 1))

    for t in range(NTI):
        for q in range(NQ):
            bank = 4 + ((t * NQ + q) % 4)
            tr.op("tensor", lambda e, t=t, q=q, bank=bank: e.matmul(
                pb[bank][:, :], w_mdtT_sb[:, t * 128:(t + 1) * 128],
                dtr_sb[:, qslice(q)], start=True, stop=True),
                reads=["ma_dtr", "w_mdtT"], writes=[f"pb{bank}"])
            tr.op("vector", lambda e, t=t, q=q, bank=bank: e.tensor_scalar(
                sp_sb[:, qslice(q)], pb[bank][:, :], sm_bdt[:, t:t + 1], 21.0,
                ALU.add, ALU.min),
                reads=[f"pb{bank}", "smalls"], writes=["masp"], mark=(q == NQ - 1))
        tr.op("scalar", lambda e, t=t: e.activation(cb0[:], sp_sb[:], AF.Exp,
                                                    bias=0.0, scale=1.0),
              reads=["masp"], writes=["mac0"])
        tr.op("scalar", lambda e, t=t: e.activation(cb1[:], cb0[:], AF.Ln,
                                                    bias=cpool[:, 1:2], scale=1.0),
              reads=["mac0", "cpool"], writes=["mac1"])
        dk = f"madt{t % 2}"
        tr.op("vector", lambda e, t=t: e.tensor_scalar(
            dt_sb[t % 2][:], cb1[:], 1e-4, 20.0, ALU.max, ALU.min),
            reads=["mac1"], writes=[dk])
        tr.dma("gpsimd", out=s_dt[t, :, :], in_=dt_sb[t % 2][:],
               reads=[dk], writes=[f"s_dt{t}"])
    return dtr_sb


def emit_mamba_scan(tr, g, pb, A, Bar, e32_sb, bc_sb, negA, sm_d, w_moutT,
                    s_x, s_dt, s_zs, s_s1, hcar_ref=None):
    def BCv(j):
        return A[j // 4][:, (j % 4) * QC:(j % 4 + 1) * QC]

    def YCv(t):
        return Bar[t // 4][:, (t % 4) * QC:(t % 4 + 1) * QC]

    def YAv(t):
        return Bar[4 + t // 4][:, (t % 4) * QC:(t % 4 + 1) * QC]

    dtb = [g.sb(f"scdt{i}", [128, QC]) for i in range(2)]
    xb = [g.sb(f"scx{i}", [128, QC]) for i in range(2)]
    dtu = g.sb("scdu", [128, QC])
    dAb = [g.sb(f"scda{i}", [128, QC]) for i in range(4)]
    bfb = g.sb("scbf", [128, QC])
    hsq = [g.sb(f"sch{i}", [128, QC]) for i in range(4)]
    qb = g.sb("scq", [128, QC])
    qb2 = g.sb("scq2", [128, QC])
    hcar = g.sb("schc", [128, NTI * DST])
    if hcar_ref is not None:
        hcar_ref.append(hcar)
    zsb = [g.sb(f"sczs{i}", [128, QC]) for i in range(2)]
    wob = [g.sb("scwo0", [128, NTI, 128])]
    ssb = [g.sb(f"scs{i}", [128, QC]) for i in range(2)]

    tr.op("vector", lambda e: e.memset(hcar[:], 0.0),
          writes=["schc"] + [f"shc{t}_{n}" for t in range(NTI) for n in range(DST)])

    for c in range(NQ):
        for j in range(32):
            bank = j % 2
            tr.op("tensor", lambda e, c=c, j=j, bank=bank: e.matmul(
                pb[bank][:, :], e32_sb[:, j * 128:(j + 1) * 128],
                bc_sb[:, qslice(c)], start=True, stop=True),
                reads=["ma_bc", "c_e32"], writes=[f"pb{bank}"])
            tr.op("scalar", lambda e, j=j, bank=bank: e.activation(
                BCv(j), pb[bank][:, :], AF.Copy, bias=0.0, scale=1.0),
                reads=[f"pb{bank}"], writes=[f"scb{j}"])
        for t in range(NTI):
            kdt, kx, kzs = f"scdt{t % 2}", f"scx{t % 2}", f"sczs{t % 2}"
            tr.dma("sync", out=dtb[t % 2][:], in_=s_dt[t, :, qslice(c)],
                   reads=[f"s_dt{t}"], writes=[kdt])
            tr.dma("scalar", out=xb[t % 2][:], in_=s_x[t, :, qslice(c)],
                   reads=[f"s_x{t}"], writes=[kx])
            tr.dma("scalar", out=zsb[t % 2][:], in_=s_zs[t, :, qslice(c)],
                   reads=[f"s_zs{t}"], writes=[kzs])
            tr.op("vector", lambda e, t=t: e.tensor_mul(dtu[:], dtb[t % 2][:],
                                                        xb[t % 2][:]),
                  reads=[kdt, kx], writes=["scdu"])
            for n in range(DST):
                ka, kh = f"scda{n % 4}", f"sch{n % 4}"
                tr.op("scalar", lambda e, t=t, n=n: e.activation(
                    dAb[n % 4][:], dtb[t % 2][:], AF.Exp, bias=0.0,
                    scale=negA[:, t * DST + n:t * DST + n + 1]),
                    reads=[kdt, "negA"], writes=[ka])
                tr.op("vector", lambda e, n=n: e.tensor_mul(
                    bfb[:], dtu[:], BCv(n)),
                    reads=["scdu", f"scb{n}"], writes=["scbf"])
                tr.op("vector", lambda e, t=t, n=n: e.tensor_tensor_scan(
                    hsq[n % 4][:], dAb[n % 4][:], bfb[:],
                    hcar[:, t * DST + n:t * DST + n + 1], ALU.mult, ALU.add),
                    reads=[ka, "scbf", f"shc{t}_{n}"], writes=[kh], fenced=True)
                tr.op("gpsimd", lambda e, t=t, n=n: e.tensor_copy(
                    hcar[:, t * DST + n:t * DST + n + 1], hsq[n % 4][:, QC - 1:QC]),
                    reads=[kh], writes=[f"shc{t}_{n}"])
                if n == 0:
                    tr.op("gpsimd", lambda e, t=t, n=n: e.tensor_mul(
                        YCv(t), hsq[n % 4][:], BCv(16 + n)),
                        reads=[kh, f"scb{16 + n}"], writes=[f"scy{t}"])
                elif n == 1:
                    tr.op("vector", lambda e, t=t, n=n: e.tensor_mul(
                        YAv(t), hsq[n % 4][:], BCv(16 + n)),
                        reads=[kh, f"scb{16 + n}"], writes=[f"sca{t}"])
                elif n % 2 == 0:
                    tr.op("gpsimd", lambda e, t=t, n=n: e.tensor_mul(
                        qb[:], hsq[n % 4][:], BCv(16 + n)),
                        reads=[kh, f"scb{16 + n}"], writes=["scq"])
                    tr.op("gpsimd", lambda e, t=t: e.tensor_add(
                        YCv(t), YCv(t), qb[:]),
                        reads=["scq", f"scy{t}"], writes=[f"scy{t}"])
                else:
                    tr.op("vector", lambda e, t=t, n=n: e.tensor_mul(
                        qb2[:], hsq[n % 4][:], BCv(16 + n)),
                        reads=[kh, f"scb{16 + n}"], writes=["scq2"])
                    tr.op("vector", lambda e, t=t: e.tensor_add(
                        YAv(t), YAv(t), qb2[:]),
                        reads=["scq2", f"sca{t}"], writes=[f"sca{t}"])
            if "nofin" in getattr(tr, "flags", ()):
                continue
            tr.op("vector", lambda e, t=t: e.tensor_add(
                YCv(t), YCv(t), YAv(t)),
                reads=[f"sca{t}", f"scy{t}"], writes=[f"scy{t}"])
            tr.op("vector", lambda e, t=t: e.scalar_tensor_tensor(
                YCv(t), xb[t % 2][:], sm_d[:, t:t + 1], YCv(t),
                ALU.mult, ALU.add), reads=[kx, f"scy{t}", "smalls"], writes=[f"scy{t}"])
            tr.op("vector", lambda e, t=t: e.tensor_scalar(
                YCv(t), YCv(t), 0.0, 1.0, ALU.max, ALU.min),
                reads=[f"scy{t}"], writes=[f"scy{t}"])
            tr.op("vector", lambda e, t=t: e.tensor_mul(
                YCv(t), YCv(t), zsb[t % 2][:]),
                reads=[f"scy{t}", kzs], writes=[f"scy{t}"])
            tr.op("vector", lambda e, t=t: e.tensor_scalar(
                YCv(t), YCv(t), 0.0, 1.0, ALU.max, ALU.min),
                reads=[f"scy{t}"], writes=[f"scy{t}"])
        for dm in range(NT):
            kw = "scwo0"
            for t in range(NTI):
                tr.dma("sync", out=wob[0][:, t, :],
                       in_=w_moutT[t * 128:(t + 1) * 128, dm * 128:(dm + 1) * 128],
                       writes=[kw] if t == NTI - 1 else [kw + "p"])
            bank = 2 + (dm % 2)
            for t in range(NTI):
                tr.op("tensor", lambda e, dm=dm, t=t, bank=bank: e.matmul(
                    pb[bank][:, :], wob[0][:, t, :], YCv(t),
                    start=(t == 0), stop=(t == NTI - 1)),
                    reads=[kw, kw + "p", f"scy{t}"], writes=[f"pb{bank}"],
                    mark=(t == NTI - 1))
            ks = f"scs{dm % 2}"
            tr.op("vector", lambda e, bank=bank, dm=dm: e.tensor_copy(
                ssb[dm % 2][:], pb[bank][:, :]), reads=[f"pb{bank}"], writes=[ks])
            tr.dma("gpsimd", out=s_s1[dm, :, qslice(c)], in_=ssb[dm % 2][:],
                   reads=[ks], writes=[f"s_s1{dm}"])



def emit_ffn_prologue(tr, g, pb, ones, cpool, A, B, s_xn, w_ln1w, w_ln1b, w_bo,
                      w_bqkvh, bq8):
    for t in range(NT):
        tr.dma("sync", out=B[t][:], in_=s_xn[t, :, :],
               reads=[f"s_xn{t}"], writes=[f"xB{t}"])
    emit_layernorm(tr, g, pb, ones, cpool, B, [f"xB{t}" for t in range(NT)],
                   A, [f"nx{t}" for t in range(NT)],
                   w_sb=w_ln1w, b_sb=w_ln1b, wkey="smallsF", tag="ln1")
    for t in range(NT):
        tr.op("vector", lambda e, t=t: e.tensor_scalar(
            B[t][:], B[t][:], w_bo[:, t:t + 1], None, ALU.add),
            reads=[f"xB{t}", "smallsF"], writes=[f"x1_{t}", f"xB{t}"])
    tr.op("vector", lambda e: e.tensor_scalar_mul(bq8[:], w_bqkvh[:, 0:48:3], 0.125),
          reads=["smallsF"], writes=["bq8"])


def emit_attn_l0(tr, g, pb, ones, A, B, w_qkv, w_wo, w_bqkvh, bq8, masks_sb):
    wq = g.sb("a0wq", [128, NT, 128])
    wk = g.sb("a0wk", [128, NT, 128])
    wv = g.sb("a0wv", [128, NT, 128])
    wo = g.sb("a0wo", [64, 2, D])
    Qa = g.sb("a0qa", [65, L])
    Ka = g.sb("a0ka", [65, L])
    V65 = g.sb("a0v", [128, NTI, 65])
    att = g.sb("a0att", [64, L])
    sq64 = g.sb("a0sq", [64, QC])
    crow = g.sb("a0cr", [65, 8])
    rz = g.sb("a0rz", [65, QC])
    zbs = g.sb("a0zb", [64, QC])
    Pb = [g.sb(f"a0p{i}", [128, QC]) for i in range(2)]

    tr.op("vector", lambda e: e.memset(Ka[64:65, :], 1.0), writes=["ka_ones"])

    for gi in range(8):
        for k in range(NT):
            tr.dma("sync", out=wq[:, k, :], in_=w_qkv[k * 128:(k + 1) * 128,
                                                      gi * 128:(gi + 1) * 128],
                   writes=["a0wq"] if k == NT - 1 else ["a0wq_p"])
            tr.dma("scalar", out=wk[:, k, :], in_=w_qkv[k * 128:(k + 1) * 128,
                                                        D + gi * 128:D + (gi + 1) * 128],
                   writes=["a0wk"] if k == NT - 1 else ["a0wk_p"])
            tr.dma("gpsimd", out=wv[:, k, :], in_=w_qkv[k * 128:(k + 1) * 128,
                                                        2 * D + gi * 128:2 * D + (gi + 1) * 128],
                   writes=["a0wv"] if k == NT - 1 else ["a0wv_p"])
        for h2 in range(2):
            tr.dma("sync", out=wo[:, h2, :],
                   in_=w_wo[(2 * gi + h2) * 64:(2 * gi + h2 + 1) * 64, :],
                   writes=["a0wo"] if h2 == 1 else ["a0wo_p"])
        for h2 in range(2):
            h = 2 * gi + h2
            hs = slice(h2 * 64, h2 * 64 + 64)
            for qn, (wbuf, wkkey, dst, dkey, scl, bias_ap) in enumerate([
                    (wq, ["a0wq", "a0wq_p"], Qa, "qa", 0.125, bq8[:, h:h + 1]),
                    (wk, ["a0wk", "a0wk_p"], Ka, "ka", 1.0, w_bqkvh[:, 3 * h + 1:3 * h + 2])]):
                for q in range(NQ):
                    for k in range(NT):
                        tr.op("tensor", lambda e, wbuf=wbuf, hs=hs, k=k, q=q: e.matmul(
                            pb[0][0:64, :], wbuf[:, k, hs], A[k][:, qslice(q)],
                            start=(k == 0), stop=(k == NT - 1)),
                            reads=wkkey + [f"nx{k}"], writes=["pb0"],
                            mark=(k == NT - 1))
                    tr.op("scalar", lambda e, dst=dst, q=q, scl=scl, bias_ap=bias_ap:
                          e.activation(dst[0:64, qslice(q)], pb[0][0:64, :], AF.Copy,
                                       bias=0.0, scale=scl),
                          reads=["pb0"], writes=[dkey + "_s"])
                    tr.op("vector", lambda e, dst=dst, q=q, bias_ap=bias_ap:
                          e.tensor_scalar(dst[0:64, qslice(q)], dst[0:64, qslice(q)],
                                          bias_ap, None, ALU.add),
                          reads=[dkey + "_s", "bq8", "smallsF"], writes=[dkey])
            for tt in range(NTI):
                for k in range(NT):
                    tr.op("tensor", lambda e, tt=tt, k=k, hs=hs: e.matmul(
                        pb[1][:, 0:64], A[k][:, tt * 128:(tt + 1) * 128],
                        wv[:, k, hs], start=(k == 0), stop=(k == NT - 1)),
                        reads=["a0wv", "a0wv_p", f"nx{k}"], writes=["pb1"],
                        mark=(k == NT - 1))
                tr.op("vector", lambda e, tt=tt: e.tensor_copy(
                    V65[:, tt, 0:64], pb[1][:, 0:64]),
                    reads=["pb1"], writes=["v65"])
            tr.op("vector", lambda e: e.memset(V65[:, :, 64:65], 1.0),
                  writes=["v65"])
            for q in range(NQ):
                tr.op("scalar", lambda e, q=q: e.activation(
                    sq64[:, :], Ka[0:64, qslice(q)], AF.Square, bias=0.0, scale=1.0),
                    reads=["ka"], writes=["sq64"])
                tr.op("tensor", lambda e, q=q: e.matmul(
                    pb[2][0:65, :], ones[0:64, 0:65], sq64[:, :],
                    start=True, stop=True),
                    reads=["sq64", "c_ones"], writes=["pb2"])
                tr.op("vector", lambda e, q=q: e.tensor_reduce(
                    crow[64:65, q:q + 1], pb[2][64:65, :], mybir.AxisListType.X,
                    ALU.max),
                    reads=["pb2"], writes=["crow"])
            tr.op("vector", lambda e: e.tensor_reduce(
                crow[64:65, 4:5], crow[64:65, 0:4], mybir.AxisListType.X, ALU.max),
                reads=["crow"], writes=["rk2"])
            for q in range(NQ):
                tr.op("scalar", lambda e, q=q: e.activation(
                    sq64[:, :], Qa[0:64, qslice(q)], AF.Square, bias=0.0, scale=1.0),
                    reads=["qa"], writes=["sq64"])
                tr.op("tensor", lambda e, q=q: e.matmul(
                    pb[2][0:65, :], ones[0:64, 0:65], sq64[:, :],
                    start=True, stop=True),
                    reads=["sq64", "c_ones"], writes=["pb2"])
                tr.op("scalar", lambda e, q=q: e.activation(
                    Qa[64:65, qslice(q)], pb[2][64:65, :], AF.Sqrt, bias=0.0,
                    scale=crow[64:65, 4:5]),
                    reads=["pb2", "rk2", "cpool"], writes=["qa64"])
            tr.op("vector", lambda e: e.tensor_scalar_mul(Qa[64:65, :], Qa[64:65, :], -1.0),
                  reads=["qa64"], writes=["qa64"])
            for q in range(NQ):
                nkc = 4 * q + 4
                pvb = 4 + (q % 2)

                def emit_S(kc, q=q):
                    sb = 2 + (kc % 2)
                    tr.op("tensor", lambda e, kc=kc, q=q, sb=sb: e.matmul(
                        pb[sb][:, :], Ka[:, kc * 128:(kc + 1) * 128], Qa[:, qslice(q)],
                        start=True, stop=True),
                        reads=["ka", "ka_ones", "qa", "qa64"], writes=[f"pb{sb}"])
                    pk = f"a0p{kc % 2}"
                    tr.op("scalar", lambda e, kc=kc, sb=sb: e.activation(
                        Pb[kc % 2][:], pb[sb][:, :], AF.Exp, bias=0.0, scale=1.0),
                        reads=[f"pb{sb}"], writes=[pk])
                    if kc >= 4 * q:
                        j = kc - 4 * q
                        tr.op("vector", lambda e, kc=kc, j=j: e.tensor_mul(
                            Pb[kc % 2][:], Pb[kc % 2][:], masks_sb[:, j, :]),
                            reads=[pk, "masks"], writes=[pk])

                emit_S(0)
                for kc in range(nkc):
                    if kc + 1 < nkc:
                        emit_S(kc + 1)
                    pk = f"a0p{kc % 2}"
                    tr.op("tensor", lambda e, kc=kc, q=q, pvb=pvb: e.matmul(
                        pb[pvb][0:65, :], V65[:, kc, :], Pb[kc % 2][:],
                        start=(kc == 0), stop=(kc == nkc - 1)),
                        reads=[pk, "v65"], writes=[f"pb{pvb}"])
                tr.op("vector", lambda e, q=q, pvb=pvb: e.reciprocal(
                    rz[64:65, :], pb[pvb][64:65, :]), reads=[f"pb{pvb}"], writes=["rz"])
                tr.op("tensor", lambda e, q=q: e.matmul(
                    pb[6][0:64, :], ones[64:65, 0:64], rz[64:65, :],
                    start=True, stop=True),
                    reads=["rz", "c_ones"], writes=["pb6"])
                tr.op("vector", lambda e, q=q: e.tensor_copy(zbs[:, :], pb[6][0:64, :]),
                      reads=["pb6"], writes=["zbs"])
                tr.op("vector", lambda e, q=q, pvb=pvb: e.tensor_mul(
                    att[:, qslice(q)], pb[pvb][0:64, :], zbs[:, :]),
                    reads=[f"pb{pvb}", "zbs"], writes=["att"])
                tr.op("vector", lambda e, q=q, h=h: e.tensor_scalar(
                    att[:, qslice(q)], att[:, qslice(q)],
                    w_bqkvh[:, 3 * h + 2:3 * h + 3], None, ALU.add),
                    reads=["att", "smallsF"], writes=["att"])
            for dm in range(NT):
                for q in range(NQ):
                    wb = 6 + ((dm * NQ + q) % 2)
                    tr.op("tensor", lambda e, dm=dm, q=q, h2=h2, wb=wb: e.matmul(
                        pb[wb][:, :], wo[:, h2, dm * 128:(dm + 1) * 128],
                        att[:, qslice(q)], start=True, stop=True),
                        reads=["att", "a0wo", "a0wo_p"], writes=[f"pb{wb}"])
                    tr.op("vector", lambda e, dm=dm, q=q, wb=wb: e.tensor_add(
                        B[dm][:, qslice(q)], B[dm][:, qslice(q)], pb[wb][:, :]),
                        reads=[f"pb{wb}", f"x1_{dm}"], writes=[f"x1_{dm}"],
                        mark=True)


def emit_ln2_store(tr, g, pb, ones, cpool, B, w_ln2w, w_ln2b, s_nx2):
    tmp = [g.sb(f"l2t{i}", [128, L]) for i in range(2)]
    def post(t):
        tr.dma("gpsimd", out=s_nx2[t, :, :], in_=tmp[t % 2][:],
               reads=[f"nx2t{t % 2}"], writes=[f"s_nx2{t}"])

    emit_layernorm(tr, g, pb, ones, cpool, B, [f"x1_{t}" for t in range(NT)],
                   [tmp[t % 2] for t in range(NT)],
                   [f"nx2t{t % 2}" for t in range(NT)],
                   w_sb=w_ln2w, b_sb=w_ln2b, wkey="smallsF", tag="ln2", post=post)



def emit_moe(tr, g, pb, ones, ident, e8_sb, A, B, w_fc1T, w_fc2T, fc1b, fc2b,
             l0gw, s_nx2):
    nxq_bufs = [g.sb(f"monx{i}", [128, NT, QC]) for i in range(2)]
    wf1 = [g.sb(f"mow1{i}", [128, NT, 256]) for i in range(2)]
    wf2 = [g.sb(f"mow2{i}", [128, D]) for i in range(2)]
    sT8 = g.sb("most", [8, L])
    sbs = g.sb("mosb", [128, QC])
    tmp = g.sb("motmp", [128, QC])
    gsb = g.sb("mogs", [128, 64])
    g8 = gsb[:, 0:8]
    i1 = gsb[:, 8:16]
    mk8 = gsb[:, 16:24]
    i2 = gsb[:, 24:32]
    s8a = gsb[:, 32:40]
    s8f = gsb[:, 40:48]
    gsc = gsb[:, 48:52]

    def H1(ft):
        return A[ft // 4][:, (ft % 4) * QC:(ft % 4 + 1) * QC]

    for e_ in range(NE):
        for qc in range(NQ):
            it = e_ * NQ + qc
            nxq = nxq_bufs[it % 2]
            nk = f"monx{it % 2}"
            for k in range(NT):
                tr.dma("sync", out=nxq[:, k, :], in_=s_nx2[k, :, qslice(qc)],
                       reads=[f"s_nx2{k}"],
                       writes=[nk] if k == NT - 1 else [nk + "_p"])
            if e_ == 0:
                for tt2 in range(4):
                    for k in range(NT):
                        tr.op("tensor", lambda e, k=k, tt2=tt2: e.matmul(
                            pb[0][:, 0:8], nxq[:, k, tt2 * 128:(tt2 + 1) * 128],
                            l0gw[:, k, :], start=(k == 0), stop=(k == NT - 1)),
                            reads=[nk, nk + "_p", "l0gw"], writes=["pb0"],
                            mark=(k == NT - 1))
                    tr.op("vector", lambda e: e.tensor_copy(g8, pb[0][:, 0:8]),
                          reads=["pb0"], writes=["mog"])
                    tr.op("vector", lambda e: e.tensor_reduce(
                        gsc[:, 0:1], g8, mybir.AxisListType.X, ALU.max),
                        reads=["mog"], writes=["mog"])
                    tr.op("vector", lambda e: e.tensor_scalar(
                        i1, g8, gsc[:, 0:1], 0.0, ALU.subtract, ALU.is_ge),
                        reads=["mog"], writes=["mog"])
                    tr.op("vector", lambda e: e.scalar_tensor_tensor(
                        mk8, i1, -1e30, g8, ALU.mult, ALU.add),
                        reads=["mog"], writes=["mog"])
                    tr.op("vector", lambda e: e.tensor_reduce(
                        gsc[:, 1:2], mk8, mybir.AxisListType.X, ALU.max, negate=True),
                        reads=["mog"], writes=["mog"])
                    tr.op("scalar", lambda e: e.activation(
                        gsc[:, 2:3], gsc[:, 0:1], AF.Sigmoid, bias=gsc[:, 1:2],
                        scale=1.0), reads=["mog"], writes=["mog"])
                    tr.op("vector", lambda e: e.tensor_scalar(
                        gsc[:, 3:4], gsc[:, 2:3], -1.0, 1.0, ALU.mult, ALU.add),
                        reads=["mog"], writes=["mog"])
                    tr.op("vector", lambda e: e.tensor_scalar(
                        i2, mk8, gsc[:, 1:2], 0.0, ALU.add, ALU.is_ge),
                        reads=["mog"], writes=["mog"])
                    tr.op("vector", lambda e: e.tensor_scalar(
                        s8a, i1, gsc[:, 2:3], None, ALU.mult),
                        reads=["mog"], writes=["mog"])
                    tr.op("vector", lambda e: e.scalar_tensor_tensor(
                        s8f, i2, gsc[:, 3:4], s8a, ALU.mult, ALU.add),
                        reads=["mog"], writes=["mog2"])
                    tr.op("tensor", lambda e: e.transpose(
                        pb[1][0:8, 0:128], s8f, ident[:]),
                        reads=["mog2", "c_ident"], writes=["pb1"])
                    tr.op("vector", lambda e, qc=qc, tt2=tt2: e.tensor_copy(
                        sT8[0:8, qc * QC + tt2 * 128: qc * QC + (tt2 + 1) * 128],
                        pb[1][0:8, 0:128]),
                        reads=["pb1"], writes=["mosT"])
            tr.op("tensor", lambda e, e_=e_, qc=qc: e.matmul(
                pb[1][:, :], e8_sb[:, e_ * 128:(e_ + 1) * 128], sT8[0:8, qslice(qc)],
                start=True, stop=True),
                reads=["mosT", "c_e8"], writes=["pb1"])
            tr.op("vector", lambda e: e.tensor_copy(sbs[:], pb[1][:, :]),
                  reads=["pb1"], writes=["mosb"])
            for fg in range(16):
                wk = f"mow1{fg % 2}"
                for k in range(NT):
                    tr.dma("sync", out=wf1[fg % 2][:, k, :],
                           in_=w_fc1T[e_, k * 128:(k + 1) * 128,
                                      fg * 256:(fg + 1) * 256],
                           writes=[wk] if k == NT - 1 else [wk + "p"])
                for f2 in range(2):
                    ft = fg * 2 + f2
                    bank = 2 + (ft % 6)
                    for k in range(NT):
                        tr.op("tensor", lambda e, fg=fg, f2=f2, k=k, bank=bank: e.matmul(
                            pb[bank][:, :], wf1[fg % 2][:, k, f2 * 128:(f2 + 1) * 128],
                            nxq[:, k, :], start=(k == 0), stop=(k == NT - 1)),
                            reads=[wk, wk + "p", nk, nk + "_p"],
                            writes=[f"pb{bank}"], mark=(k == NT - 1))
                    tr.op("scalar", lambda e, e_=e_, ft=ft, bank=bank: e.activation(
                        H1(ft), pb[bank][:, :], AF.Gelu,
                        bias=fc1b[:, e_ * 32 + ft:e_ * 32 + ft + 1], scale=1.0),
                        reads=[f"pb{bank}", "smallsF"], writes=[f"h1_{ft}"])
            for ft in range(32):
                wk2 = f"mow2{ft % 2}"
                tr.dma("sync", out=wf2[ft % 2][:],
                       in_=w_fc2T[e_, ft * 128:(ft + 1) * 128, :], writes=[wk2])
                for dm in range(NT):
                    tr.op("tensor", lambda e, ft=ft, dm=dm: e.matmul(
                        pb[dm][:, :], wf2[ft % 2][:, dm * 128:(dm + 1) * 128],
                        H1(ft), start=(ft == 0), stop=(ft == 31)),
                        reads=[wk2, f"h1_{ft}"], writes=[f"pb{dm}"],
                        mark=(dm == NT - 1))
            for dm in range(NT):
                tr.op("vector", lambda e, e_=e_, dm=dm: e.scalar_tensor_tensor(
                    tmp[:], pb[dm][:, :], fc2b[:, e_ * NT + dm:e_ * NT + dm + 1],
                    sbs[:], ALU.add, ALU.mult),
                    reads=[f"pb{dm}", "mosb", "smallsF"], writes=["motmp"])
                tr.op("vector", lambda e, qc=qc, dm=dm: e.tensor_add(
                    B[dm][:, qslice(qc)], B[dm][:, qslice(qc)], tmp[:]),
                    reads=["motmp", f"x1_{dm}"], writes=[f"x1_{dm}"])



def emit_combine_l0(tr, g, pb, B, x_in, s_s1, s_x1r):
    gb = g.sb("cbg", [128, L])
    sb1 = [g.sb(f"cbs{i}", [128, QC]) for i in range(2)]
    xb0 = [g.sb(f"cbx{i}", [128, QC]) for i in range(2)]
    for t in range(NT):
        tr.op("scalar", lambda e, t=t: e.activation(gb[:], B[t][:], AF.Silu,
                                                    bias=0.0, scale=1.0),
              reads=[f"x1_{t}"], writes=["cbg"])
        for q in range(NQ):
            i = (t * NQ + q) % 2
            tr.dma("sync", out=sb1[i][:], in_=s_s1[t, :, qslice(q)],
                   reads=[f"s_s1{t}"], writes=[f"cbs{i}"])
            tr.dma("sync", out=xb0[i][:], in_=x_in[t, :, qslice(q)],
                   writes=[f"cbx{i}"])
            tr.op("vector", lambda e, t=t, q=q, i=i: e.tensor_mul(
                B[t][:, qslice(q)], gb[:, qslice(q)], sb1[i][:]),
                reads=["cbg", f"cbs{i}", f"x1_{t}"], writes=[f"xr{t}", f"x1_{t}"])
            tr.op("vector", lambda e, t=t, q=q, i=i: e.tensor_add(
                B[t][:, qslice(q)], B[t][:, qslice(q)], xb0[i][:]),
                reads=[f"cbx{i}"], writes=[f"xr{t}"])
        tr.dma("gpsimd", out=s_x1r[t, :, :], in_=B[t][:],
               reads=[f"xr{t}"], writes=[f"s_x1r{t}"])


def emit_l1ffn(tr, g, pb, ones, ident, e8_sb, A, B, w_e0T, w_e1T, w_owT,
               l1gw, s_g1):
    l1g = g.sb("l1gw_sb", [128, NT, NE])
    owt = [g.sb(f"l1owt{i}", [128, NT, 128]) for i in range(2)]
    hacc = g.sb("l1h", [128, NT, QC])
    sT2 = g.sb("l1st", [2, L])
    sbs0 = g.sb("l1s0", [128, QC])
    sbs1 = g.sb("l1s1", [128, QC])
    tmp = g.sb("l1tmp", [128, QC])
    g1b = [g.sb(f"l1g1{i}", [128, QC]) for i in range(2)]
    gsb = g.sb("l1gs", [128, 32])
    g8, p8, ge = gsb[:, 0:8], gsb[:, 8:16], gsb[:, 16:24]
    gsc = gsb[:, 24:30]

    def E0v(k):
        return A[k // 2][:, (k % 2) * D:(k % 2 + 1) * D]

    def E1v(k):
        return A[4 + k // 2][:, (k % 2) * D:(k % 2 + 1) * D]

    for k in range(NT):
        tr.dma("sync", out=l1g[:, k, :], in_=l1gw[k, :, :], writes=["l1gw"])
        tr.dma("sync", out=E0v(k), in_=w_e0T[k * 128:(k + 1) * 128, :],
               writes=[f"e0w{k}"])
        tr.dma("sync", out=E1v(k), in_=w_e1T[k * 128:(k + 1) * 128, :],
               writes=[f"e1w{k}"])
    for tt in range(NTI):
        for k in range(NT):
            tr.op("tensor", lambda e, tt=tt, k=k: e.matmul(
                pb[0][:, 0:8], B[k][:, tt * 128:(tt + 1) * 128], l1g[:, k, :],
                start=(k == 0), stop=(k == NT - 1)),
                reads=["l1gw", f"xn1{k}"], writes=["pb0"], mark=(k == NT - 1))
        tr.op("vector", lambda e: e.tensor_copy(g8, pb[0][:, 0:8]),
              reads=["pb0"], writes=["l1g8"])
        tr.op("vector", lambda e: e.tensor_reduce(
            gsc[:, 0:1], g8, mybir.AxisListType.X, ALU.max, negate=True),
            reads=["l1g8"], writes=["l1g8"])
        tr.op("scalar", lambda e: e.activation(
            p8, g8, AF.Exp, bias=gsc[:, 0:1], scale=1.0,
            accum_out=gsc[:, 1:2]), reads=["l1g8"], writes=["l1g8"])
        tr.op("vector", lambda e: e.reciprocal(gsc[:, 2:3], gsc[:, 1:2]),
              reads=["l1g8"], writes=["l1g8"])
        tr.op("vector", lambda e: e.tensor_scalar(
            p8, p8, gsc[:, 2:3], None, ALU.mult),
            reads=["l1g8"], writes=["l1g8"])
        tr.op("vector", lambda e: e.tensor_scalar(
            ge, p8, 0.3, None, ALU.is_gt), reads=["l1g8"], writes=["l1g8"])
        tr.op("vector", lambda e: e.tensor_reduce(
            gsc[:, 3:4], ge, mybir.AxisListType.X, ALU.add),
            reads=["l1g8"], writes=["l1g8"])
        tr.op("vector", lambda e: e.tensor_scalar(
            gsc[:, 4:5], gsc[:, 3:4], -1.0, 8.0, ALU.mult, ALU.add),
            reads=["l1g8"], writes=["l1g8"])
        tr.op("vector", lambda e: e.tensor_mul(
            gsc[:, 4:5], gsc[:, 4:5], p8[:, 0:1]),
            reads=["l1g8"], writes=["l1g8"])
        tr.op("vector", lambda e: e.tensor_mul(
            gsc[:, 5:6], gsc[:, 3:4], p8[:, 1:2]),
            reads=["l1g8"], writes=["l1s01"])
        tr.op("tensor", lambda e: e.transpose(
            pb[1][0:2, 0:128], gsc[:, 4:6], ident[:]),
            reads=["l1s01", "c_ident"], writes=["pb1"])
        tr.op("vector", lambda e, tt=tt: e.tensor_copy(
            sT2[0:2, tt * 128:(tt + 1) * 128], pb[1][0:2, 0:128]),
            reads=["pb1"], writes=["l1sT"])
    for qc in range(NQ):
        tr.op("tensor", lambda e, qc=qc: e.matmul(
            pb[1][:, :], e8_sb[0:2, 0:128], sT2[0:2, qslice(qc)],
            start=True, stop=True), reads=["l1sT", "c_e8"], writes=["pb1"])
        tr.op("vector", lambda e: e.tensor_copy(sbs0[:], pb[1][:, :]),
              reads=["pb1"], writes=["l1sb0"])
        tr.op("tensor", lambda e, qc=qc: e.matmul(
            pb[1][:, :], e8_sb[0:2, 128:256], sT2[0:2, qslice(qc)],
            start=True, stop=True), reads=["l1sT", "c_e8"], writes=["pb1"])
        tr.op("vector", lambda e: e.tensor_copy(sbs1[:], pb[1][:, :]),
              reads=["pb1"], writes=["l1sb1"])
        for ft in range(NT):
            b0 = 2 + (ft % 2)
            for k in range(NT):
                tr.op("tensor", lambda e, ft=ft, k=k, qc=qc, b0=b0: e.matmul(
                    pb[b0][:, :], E0v(k)[:, ft * 128:(ft + 1) * 128],
                    B[k][:, qslice(qc)], start=(k == 0), stop=(k == NT - 1)),
                    reads=[f"e0w{k}", f"xn1{k}"], writes=[f"pb{b0}"],
                    mark=(k == NT - 1))
            tr.op("vector", lambda e, ft=ft, b0=b0: e.tensor_mul(
                hacc[:, ft, :], pb[b0][:, :], sbs0[:]),
                reads=[f"pb{b0}", "l1sb0"], writes=[f"l1h{ft}"])
            b1 = 4 + (ft % 2)
            for k in range(NT):
                tr.op("tensor", lambda e, ft=ft, k=k, qc=qc, b1=b1: e.matmul(
                    pb[b1][:, :], E1v(k)[:, ft * 128:(ft + 1) * 128],
                    B[k][:, qslice(qc)], start=(k == 0), stop=(k == NT - 1)),
                    reads=[f"e1w{k}", f"xn1{k}"], writes=[f"pb{b1}"],
                    mark=(k == NT - 1))
            tr.op("vector", lambda e, ft=ft, b1=b1: e.tensor_mul(
                tmp[:], pb[b1][:, :], sbs1[:]),
                reads=[f"pb{b1}", "l1sb1"], writes=["l1tmp"])
            tr.op("vector", lambda e, ft=ft: e.tensor_add(
                hacc[:, ft, :], hacc[:, ft, :], tmp[:]),
                reads=["l1tmp", f"l1h{ft}"], writes=[f"l1h{ft}"])
        for dm in range(NT):
            b2 = 6 + (dm % 2)
            wkow = f"l1owt{dm % 2}"
            for k in range(NT):
                tr.dma("sync", out=owt[dm % 2][:, k, :],
                       in_=w_owT[k * 128:(k + 1) * 128, dm * 128:(dm + 1) * 128],
                       writes=[wkow] if k == NT - 1 else [wkow + "p"])
            for k in range(NT):
                tr.op("tensor", lambda e, dm=dm, k=k, b2=b2: e.matmul(
                    pb[b2][:, :], owt[dm % 2][:, k, :],
                    hacc[:, k, :], start=(k == 0), stop=(k == NT - 1)),
                    reads=[wkow, wkow + "p", f"l1h{k}"], writes=[f"pb{b2}"],
                    mark=(k == NT - 1))
            gk = f"l1g1{dm % 2}"
            tr.op("scalar", lambda e, dm=dm, b2=b2: e.activation(
                g1b[dm % 2][:], pb[b2][:, :], AF.Silu, bias=0.0, scale=1.0),
                reads=[f"pb{b2}"], writes=[gk])
            tr.dma("gpsimd", out=s_g1[dm, :, qslice(qc)], in_=g1b[dm % 2][:],
                   reads=[gk], writes=[f"s_g1{dm}"])


def emit_attn_l1(tr, g, pb, ones, ident, cpool, A, B, w_dqT, w_dkvT, w_uqT,
                 w_ukT, w_uvT, w_l1wo, smf2, ropec, ropes, s_xn1):
    for t in range(NT):
        tr.dma("gpsimd", out=s_xn1[t, :, :], in_=B[t][:],
               reads=[f"xn1{t}"], writes=[f"s_xn1{t}"])
    Gln = Grp(tr.nc)
    emit_layernorm(tr, Gln, pb, ones, cpool, B,
                   [f"xn1{t}" for t in range(NT)], B,
                   [f"hi{t}" for t in range(NT)],
                   w_sb=smf2[:, 0:NT], b_sb=smf2[:, NT:2 * NT],
                   wkey="smallsF", tag="l1ln")
    tr.barrier()
    Gln.close()
    Gdq = Grp(tr.nc)
    wdq = Gdq.sb("l1wdq", [128, NT, QC])
    wdk = Gdq.sb("l1wdk", [128, NT, QC])
    for k in range(NT):
        tr.dma("sync", out=wdq[:, k, :], in_=w_dqT[k * 128:(k + 1) * 128, :],
               writes=["l1wdq"])
        tr.dma("sync", out=wdk[:, k, :], in_=w_dkvT[k * 128:(k + 1) * 128, :],
               writes=["l1wdk"])
    for ct in range(4):
        for q in range(NQ):
            bk = (ct * NQ + q) % 2
            for k in range(NT):
                tr.op("tensor", lambda e, ct=ct, q=q, k=k, bk=bk: e.matmul(
                    pb[bk][:, :], wdq[:, k, ct * 128:(ct + 1) * 128],
                    B[k][:, qslice(q)], start=(k == 0), stop=(k == NT - 1)),
                    reads=["l1wdq", f"hi{k}"], writes=[f"pb{bk}"],
                    mark=(k == NT - 1))
            tr.op("scalar", lambda e, ct=ct, q=q, bk=bk: e.activation(
                A[ct][:, qslice(q)], pb[bk][:, :], AF.Copy, bias=0.0,
                scale=0.125), reads=[f"pb{bk}"], writes=[f"cq{ct}"])
            bk2 = 2 + (ct * NQ + q) % 2
            for k in range(NT):
                tr.op("tensor", lambda e, ct=ct, q=q, k=k, bk2=bk2: e.matmul(
                    pb[bk2][:, :], wdk[:, k, ct * 128:(ct + 1) * 128],
                    B[k][:, qslice(q)], start=(k == 0), stop=(k == NT - 1)),
                    reads=["l1wdk", f"hi{k}"], writes=[f"pb{bk2}"],
                    mark=(k == NT - 1))
            tr.op("vector", lambda e, ct=ct, q=q, bk2=bk2: e.tensor_copy(
                A[4 + ct][:, qslice(q)], pb[bk2][:, :]),
                reads=[f"pb{bk2}"], writes=[f"ckv{ct}"])
    for t in range(NT):
        tr.dma("sync", out=B[t][:], in_=s_xn1[t, :, :],
               reads=[f"s_xn1{t}"], writes=[f"s1a{t}", f"hi{t}"])
    tr.barrier()
    Gdq.close()
    wuq = g.sb("l1wuq", [128, 4, 128])
    wuk = g.sb("l1wuk", [128, 4, 128])
    wuv = g.sb("l1wuv", [128, 4, 128])
    wo = g.sb("l1wo", [64, D])
    QT = g.sb("l1qt", [128, NTI, 128])
    KT = g.sb("l1kt", [128, NTI, 128])
    rtmp = g.sb("l1rt", [128, 2, 64])
    Qa = g.sb("l1qa", [65, L])
    Ka = g.sb("l1ka", [65, L])
    V65 = g.sb("l1v", [128, NTI, 65])
    att = g.sb("l1att", [64, QC])
    sq64 = g.sb("l1scr64", [64, QC])
    crow = g.sb("l1cr", [65, 8 + QC])
    rz = crow[:, 8:8 + QC]
    zbs = sq64
    Pb = [g.sb(f"l1p{i}", [128, QC]) for i in range(2)]
    tr.op("vector", lambda e: e.memset(Ka[64:65, :], 1.0), writes=["l1kaone"])

    for gi in range(8):
        for c in range(4):
            tr.dma("sync", out=wuq[:, c, :], in_=w_uqT[c * 128:(c + 1) * 128,
                                                       gi * 128:(gi + 1) * 128],
                   writes=["l1wuq"])
            tr.dma("scalar", out=wuk[:, c, :], in_=w_ukT[c * 128:(c + 1) * 128,
                                                         gi * 128:(gi + 1) * 128],
                   writes=["l1wuk"])
            tr.dma("gpsimd", out=wuv[:, c, :], in_=w_uvT[c * 128:(c + 1) * 128,
                                                         gi * 128:(gi + 1) * 128],
                   writes=["l1wuv"])
        for (wbuf, wkkey, dst, dkey, srck) in [
                (wuq, "l1wuq", QT, "l1qt", "cq"), (wuk, "l1wuk", KT, "l1kt", "ckv")]:
            srcA = A if srck == "cq" else None
            for tt in range(NTI):
                for c in range(4):
                    base = 0 if srck == "cq" else 4
                    tr.op("tensor", lambda e, tt=tt, c=c, wbuf=wbuf, base=base: e.matmul(
                        pb[6][:, 0:128], A[base + c][:, tt * 128:(tt + 1) * 128],
                        wbuf[:, c, :], start=(c == 0), stop=(c == 3)),
                        reads=[wkkey, f"{srck}{c}"], writes=["pb6"],
                        mark=(c == 3))
                tr.op("vector", lambda e, tt=tt: e.tensor_mul(
                    rtmp[:, 0, :], pb[6][:, 0:128:2], ropec[:, tt, :]),
                    reads=["pb6", "rope"], writes=["l1r0"])
                tr.op("vector", lambda e, tt=tt: e.tensor_mul(
                    rtmp[:, 1, :], pb[6][:, 1:128:2], ropes[:, tt, :]),
                    reads=["pb6", "rope"], writes=["l1r1"])
                tr.op("vector", lambda e, tt=tt, dst=dst: e.tensor_sub(
                    dst[:, tt, 0:128:2], rtmp[:, 0, :], rtmp[:, 1, :]),
                    reads=["l1r0", "l1r1"], writes=[dkey])
                tr.op("vector", lambda e, tt=tt: e.tensor_mul(
                    rtmp[:, 0, :], pb[6][:, 0:128:2], ropes[:, tt, :]),
                    reads=["pb6", "rope", "l1r0"], writes=["l1r0"])
                tr.op("vector", lambda e, tt=tt: e.tensor_mul(
                    rtmp[:, 1, :], pb[6][:, 1:128:2], ropec[:, tt, :]),
                    reads=["pb6", "rope", "l1r1"], writes=["l1r1"])
                tr.op("vector", lambda e, tt=tt, dst=dst: e.tensor_add(
                    dst[:, tt, 1:128:2], rtmp[:, 0, :], rtmp[:, 1, :]),
                    reads=["l1r0", "l1r1"], writes=[dkey])
        for h2 in range(2):
            h = 2 * gi + h2
            hs = slice(h2 * 64, h2 * 64 + 64)
            tr.dma("scalar", out=wo[:, :],
                   in_=w_l1wo[h * 64:(h + 1) * 64, :], writes=["l1wo"])
            for tt in range(NTI):
                tr.op("tensor", lambda e, tt=tt, hs=hs: e.transpose(
                    pb[6][0:64, 0:128], QT[:, tt, hs], ident[:]),
                    reads=["l1qt", "c_ident"], writes=["pb6"])
                tr.op("vector", lambda e, tt=tt: e.tensor_copy(
                    Qa[0:64, tt * 128:(tt + 1) * 128], pb[6][0:64, 0:128]),
                    reads=["pb6"], writes=["l1qa"])
                tr.op("tensor", lambda e, tt=tt, hs=hs: e.transpose(
                    pb[7][0:64, 0:128], KT[:, tt, hs], ident[:]),
                    reads=["l1kt", "c_ident"], writes=["pb7"])
                tr.op("vector", lambda e, tt=tt: e.tensor_copy(
                    Ka[0:64, tt * 128:(tt + 1) * 128], pb[7][0:64, 0:128]),
                    reads=["pb7"], writes=["l1kad"])
            for tt in range(NTI):
                for c in range(4):
                    tr.op("tensor", lambda e, tt=tt, c=c, hs=hs: e.matmul(
                        pb[1][:, 0:64], A[4 + c][:, tt * 128:(tt + 1) * 128],
                        wuv[:, c, hs], start=(c == 0), stop=(c == 3)),
                        reads=["l1wuv", f"ckv{c}"], writes=["pb1"],
                        mark=(c == 3))
                tr.op("vector", lambda e, tt=tt: e.tensor_copy(
                    V65[:, tt, 0:64], pb[1][:, 0:64]),
                    reads=["pb1"], writes=["l1v65"])
            tr.op("vector", lambda e: e.memset(V65[:, :, 64:65], 1.0),
                  writes=["l1v65"])
            for q in range(NQ):
                tr.op("scalar", lambda e, q=q: e.activation(
                    sq64[:, :], Ka[0:64, qslice(q)], AF.Square, bias=0.0, scale=1.0),
                    reads=["l1kad"], writes=["l1scr"])
                tr.op("tensor", lambda e, q=q: e.matmul(
                    pb[2][0:65, :], ones[0:64, 0:65], sq64[:, :],
                    start=True, stop=True),
                    reads=["l1scr", "c_ones"], writes=["pb2"])
                tr.op("vector", lambda e, q=q: e.tensor_reduce(
                    crow[64:65, q:q + 1], pb[2][64:65, :], mybir.AxisListType.X,
                    ALU.max), reads=["pb2"], writes=["l1crow"])
            tr.op("vector", lambda e: e.tensor_reduce(
                crow[64:65, 4:5], crow[64:65, 0:4], mybir.AxisListType.X, ALU.max),
                reads=["l1crow"], writes=["l1rk2"])
            for q in range(NQ):
                tr.op("scalar", lambda e, q=q: e.activation(
                    sq64[:, :], Qa[0:64, qslice(q)], AF.Square, bias=0.0, scale=1.0),
                    reads=["l1qa"], writes=["l1scr"])
                tr.op("tensor", lambda e, q=q: e.matmul(
                    pb[2][0:65, :], ones[0:64, 0:65], sq64[:, :],
                    start=True, stop=True),
                    reads=["l1scr", "c_ones"], writes=["pb2"])
                tr.op("scalar", lambda e, q=q: e.activation(
                    Qa[64:65, qslice(q)], pb[2][64:65, :], AF.Sqrt, bias=0.0,
                    scale=crow[64:65, 4:5]),
                    reads=["pb2", "l1rk2", "cpool"], writes=["l1qa64"])
            tr.op("vector", lambda e: e.tensor_scalar_mul(
                Qa[64:65, :], Qa[64:65, :], -1.0),
                reads=["l1qa64"], writes=["l1qa64"])
            for q in range(NQ):
                pvb = 4 + (q % 2)

                def emit_S1(kc, q=q):
                    sb = 2 + (kc % 2)
                    tr.op("tensor", lambda e, kc=kc, q=q, sb=sb: e.matmul(
                        pb[sb][:, :], Ka[:, kc * 128:(kc + 1) * 128],
                        Qa[:, qslice(q)], start=True, stop=True),
                        reads=["l1kad", "l1kaone", "l1qa", "l1qa64"],
                        writes=[f"pb{sb}"])
                    pk = f"l1p{kc % 2}"
                    tr.op("scalar", lambda e, kc=kc, sb=sb: e.activation(
                        Pb[kc % 2][:], pb[sb][:, :], AF.Exp, bias=0.0, scale=1.0),
                        reads=[f"pb{sb}"], writes=[pk])

                emit_S1(0)
                for kc in range(NTI):
                    if kc + 1 < NTI:
                        emit_S1(kc + 1)
                    pk = f"l1p{kc % 2}"
                    tr.op("tensor", lambda e, kc=kc, q=q, pvb=pvb: e.matmul(
                        pb[pvb][0:65, :], V65[:, kc, :], Pb[kc % 2][:],
                        start=(kc == 0), stop=(kc == NTI - 1)),
                        reads=[pk, "l1v65"], writes=[f"pb{pvb}"])
                tr.op("vector", lambda e, q=q, pvb=pvb: e.reciprocal(
                    rz[64:65, :], pb[pvb][64:65, :]),
                    reads=[f"pb{pvb}"], writes=["l1rz"])
                tr.op("tensor", lambda e, q=q: e.matmul(
                    pb[6][0:64, :], ones[64:65, 0:64], rz[64:65, :],
                    start=True, stop=True),
                    reads=["l1rz", "c_ones"], writes=["pb6"])
                tr.op("vector", lambda e, q=q: e.tensor_copy(zbs[:, :], pb[6][0:64, :]),
                      reads=["pb6"], writes=["l1scr"])
                tr.op("vector", lambda e, q=q, pvb=pvb: e.tensor_mul(
                    att[:, :], pb[pvb][0:64, :], zbs[:, :]),
                    reads=[f"pb{pvb}", "l1scr"], writes=["l1att"])
                for dm in range(NT):
                    wb = 6 + (dm % 2)
                    tr.op("tensor", lambda e, dm=dm, q=q, wb=wb: e.matmul(
                        pb[wb][:, :], wo[:, dm * 128:(dm + 1) * 128],
                        att[:, :], start=True, stop=True),
                        reads=["l1att", "l1wo"], writes=[f"pb{wb}"])
                    tr.op("vector", lambda e, dm=dm, q=q, wb=wb: e.tensor_add(
                        B[dm][:, qslice(q)], B[dm][:, qslice(q)], pb[wb][:, :]),
                        reads=[f"pb{wb}", f"s1a{dm}"], writes=[f"s1a{dm}"])


def emit_final(tr, g, B, s_x1r, s_g1, y_out):
    xb = [g.sb(f"fnx{i}", [128, QC]) for i in range(2)]
    gb = [g.sb(f"fng{i}", [128, QC]) for i in range(2)]
    ob = [g.sb(f"fno{i}", [128, QC]) for i in range(2)]
    for t in range(NT):
        for q in range(NQ):
            i = (t * NQ + q) % 2
            tr.dma("sync", out=xb[i][:], in_=s_x1r[t, :, qslice(q)],
                   reads=[f"s_x1r{t}"], writes=[f"fnx{i}"])
            tr.dma("sync", out=gb[i][:], in_=s_g1[t, :, qslice(q)],
                   reads=[f"s_g1{t}"], writes=[f"fng{i}"])
            tr.op("vector", lambda e, t=t, q=q, i=i: e.tensor_mul(
                ob[i][:], B[t][:, qslice(q)], gb[i][:]),
                reads=[f"s1a{t}", f"fng{i}"], writes=[f"fno{i}"])
            tr.op("vector", lambda e, t=t, q=q, i=i: e.tensor_add(
                ob[i][:], ob[i][:], xb[i][:]),
                reads=[f"fnx{i}", f"fno{i}"], writes=[f"fno{i}"])
            tr.dma("gpsimd", out=y_out[t, :, qslice(q)], in_=ob[i][:],
                   reads=[f"fno{i}"], writes=[f"y{t}"])



CAP = 1024


def emit_moe_routed(tr, g, pb, ones, ident, A, B, w_fc1T, w_fc2T, fc1b, fc2b,
                    l0gw, c_iotaf, c_rowc, c_idxfill, s_nx2, s_nx2T, s_idx,
                    s_moeT):
    U32 = mybir.dt.uint32
    nxq = g.sb("monx", [128, NT, QC])
    stage = g.sb("mostg", [128, D])
    xof = g.sb("moxof", [128, NT, QC])
    gsb = g.sb("mogs", [128, 64])
    g8, i1, mk8, i2 = gsb[:, 0:8], gsb[:, 8:16], gsb[:, 16:24], gsb[:, 24:32]
    s8f, u8, d8 = gsb[:, 32:40], gsb[:, 40:48], gsb[:, 48:56]
    gsc = gsb[:, 56:62]
    destf = g.sb("modst", [128, NTI, NE])
    svf = g.sb("mosvf", [128, NTI, NE])
    slotu = g.sb("moslu", [128, NTI, NE], U32)
    ci = g.sb("moci", [128, NTI])
    rowc = g.sb("morc", [8, 1])
    sT8 = g.sb("most", [8, L])
    scat = g.sb("mosc", [128, 3])
    ixb = g.sb("moix", [128, 3])
    giu = g.sb("mogiu", [128, 4], U32)
    dstu = g.sb("modtu", [128, 4], U32)
    svv = g.sb("mosvv", [128, 4])
    fill = g.sb("mofill", [128, 192])
    g7a = Grp(tr.nc)
    rowA = g7a.sb("morA", [8, L])
    rowB = g7a.sb("morB", [8, L])

    tr.dma("sync", out=ci[:], in_=c_iotaf, writes=["moci"])
    tr.dma("sync", out=rowc[:], in_=c_rowc, writes=["morc"])
    tr.dma("sync", out=fill[:], in_=c_idxfill, writes=["mofill"])
    tr.dma("gpsimd", out=s_idx[0:8192, :], in_=fill[:], reads=["mofill"],
           writes=["s_idx"])
    tr.op("vector", lambda e: e.memset(stage[0:1, :], 0.0), writes=["mostg"])
    tr.dma("gpsimd", out=s_nx2T[2048:2049, :], in_=stage[0:1, :],
           reads=["mostg"], writes=["s_nx2T_z"])

    for qc in range(NQ):
        for k in range(NT):
            tr.dma("sync", out=nxq[:, k, :], in_=s_nx2[k, :, qslice(qc)],
                   reads=[f"s_nx2{k}"],
                   writes=["monx"] if k == NT - 1 else ["monx_p"])
        for tt2 in range(4):
            tt = qc * 4 + tt2
            for k in range(NT):
                tr.op("tensor", lambda e, k=k, tt2=tt2: e.transpose(
                    pb[2 + k % 2][:, 0:128], nxq[:, k, tt2 * 128:(tt2 + 1) * 128],
                    ident[:]),
                    reads=["monx", "monx_p", "c_ident"], writes=[f"pb{2 + k % 2}"])
                tr.op("vector", lambda e, k=k: e.tensor_copy(
                    stage[:, k * 128:(k + 1) * 128], pb[2 + k % 2][:, 0:128]),
                    reads=[f"pb{2 + k % 2}"], writes=["mostg"])
            tr.dma("gpsimd", out=s_nx2T[tt * 128:(tt + 1) * 128, :], in_=stage[:],
                   reads=["mostg"], writes=[f"s_nx2T{tt}"])
            for k in range(NT):
                tr.op("tensor", lambda e, k=k, tt2=tt2: e.matmul(
                    pb[0][:, 0:8], nxq[:, k, tt2 * 128:(tt2 + 1) * 128],
                    l0gw[:, k, :], start=(k == 0), stop=(k == NT - 1)),
                    reads=["monx", "monx_p", "l0gw"], writes=["pb0"],
                    mark=(k == NT - 1))
            tr.op("vector", lambda e: e.tensor_copy(g8, pb[0][:, 0:8]),
                  reads=["pb0"], writes=["mog"])
            tr.op("vector", lambda e: e.tensor_reduce(
                gsc[:, 0:1], g8, mybir.AxisListType.X, ALU.max),
                reads=["mog"], writes=["mog"])
            tr.op("vector", lambda e: e.tensor_scalar(
                i1, g8, gsc[:, 0:1], 0.0, ALU.subtract, ALU.is_ge),
                reads=["mog"], writes=["mog"])
            tr.op("vector", lambda e: e.scalar_tensor_tensor(
                mk8, i1, -1e30, g8, ALU.mult, ALU.add),
                reads=["mog"], writes=["mog"])
            tr.op("vector", lambda e: e.tensor_reduce(
                gsc[:, 1:2], mk8, mybir.AxisListType.X, ALU.max, negate=True),
                reads=["mog"], writes=["mog"])
            tr.op("scalar", lambda e: e.activation(
                gsc[:, 2:3], gsc[:, 0:1], AF.Sigmoid, bias=gsc[:, 1:2],
                scale=1.0), reads=["mog"], writes=["mog"])
            tr.op("vector", lambda e: e.tensor_scalar(
                gsc[:, 3:4], gsc[:, 2:3], -1.0, 1.0, ALU.mult, ALU.add),
                reads=["mog"], writes=["mog"])
            tr.op("vector", lambda e: e.tensor_scalar(
                i2, mk8, gsc[:, 1:2], 0.0, ALU.add, ALU.is_ge),
                reads=["mog"], writes=["mog"])
            tr.op("vector", lambda e: e.tensor_scalar(
                d8, i1, gsc[:, 2:3], None, ALU.mult),
                reads=["mog"], writes=["mog"])
            tr.op("vector", lambda e, tt=tt: e.scalar_tensor_tensor(
                svf[:, tt, :], i2, gsc[:, 3:4], d8, ALU.mult, ALU.add),
                reads=["mog"], writes=["mosvf"])
            tr.op("vector", lambda e: e.tensor_add(u8, i1, i2),
                  reads=["mog"], writes=["mog"])
            tr.op("vector", lambda e: e.tensor_scalar(
                u8, u8, -1.0, 1.0, ALU.mult, ALU.add),
                reads=["mog"], writes=["mog"])
            tr.op("vector", lambda e: e.tensor_scalar(
                d8, i2, 2048.0, None, ALU.mult), reads=["mog"], writes=["mog"])
            tr.op("vector", lambda e: e.scalar_tensor_tensor(
                d8, u8, 8192.0, d8, ALU.mult, ALU.add),
                reads=["mog"], writes=["mog"])
            tr.op("vector", lambda e, tt=tt: e.tensor_scalar(
                destf[:, tt, :], d8, ci[:, tt:tt + 1], None, ALU.add),
                reads=["mog", "moci"], writes=["modst"])
            tr.op("tensor", lambda e, tt=tt: e.transpose(
                pb[1][0:8, 0:128], svf[:, tt, :], ident[:]),
                reads=["mosvf", "c_ident"], writes=["pb1"])
            tr.op("vector", lambda e, tt=tt: e.tensor_copy(
                sT8[0:8, tt * 128:(tt + 1) * 128], pb[1][0:8, 0:128]),
                reads=["pb1"], writes=["mosT"])
    tr.op("vector", lambda e: e.tensor_scalar(
        rowA[0:8, :], sT8[0:8, :], 0.0, None, ALU.is_gt),
        reads=["mosT"], writes=["morA"])
    tr.op("vector", lambda e: e.memset(rowB[0:8, :], 0.0), writes=["morB"])
    tr.op("vector", lambda e: e.tensor_tensor_scan(
        sT8[0:8, :], rowB[0:8, :], rowA[0:8, :], 0.0, ALU.add, ALU.add),
        reads=["morA", "morB"], writes=["mopos", "mosT"], fenced=True)
    tr.op("vector", lambda e: e.tensor_scalar(
        rowB[0:8, :], sT8[0:8, :], float(CAP), None, ALU.is_le),
        reads=["mopos"], writes=["morB"])
    tr.op("vector", lambda e: e.tensor_mul(rowA[0:8, :], rowA[0:8, :], rowB[0:8, :]),
          reads=["morA", "morB"], writes=["morA"])
    tr.op("vector", lambda e: e.tensor_scalar(
        sT8[0:8, :], sT8[0:8, :], rowc[:, 0:1], None, ALU.add),
        reads=["mopos", "morc"], writes=["mopos"])
    tr.op("gpsimd", lambda e: e.iota(rowB[0:8, :], [[1, L]], base=8192,
                                     channel_multiplier=0,
                                     allow_small_or_imprecise_dtypes=True),
          writes=["morB"])
    tr.op("vector", lambda e: e.tensor_sub(sT8[0:8, :], sT8[0:8, :], rowB[0:8, :]),
          reads=["mopos", "morB"], writes=["mopos"])
    tr.op("vector", lambda e: e.tensor_mul(sT8[0:8, :], sT8[0:8, :], rowA[0:8, :]),
          reads=["mopos", "morA"], writes=["mopos"])
    tr.op("vector", lambda e: e.tensor_add(sT8[0:8, :], sT8[0:8, :], rowB[0:8, :]),
          reads=["mopos", "morB"], writes=["mopos"])
    for tt in range(NTI):
        tr.op("tensor", lambda e, tt=tt: e.transpose(
            pb[1][0:128, 0:8], sT8[0:8, tt * 128:(tt + 1) * 128], ident[0:8, 0:8]),
            reads=["mopos", "c_ident"], writes=["pb1"])
        tr.op("vector", lambda e, tt=tt: e.tensor_copy(
            slotu[:, tt, :], pb[1][0:128, 0:8]),
            reads=["pb1"], writes=["moslu"])
    for tt in range(NTI):
        for e_ in range(NE):
            tr.op("vector", lambda e, tt=tt: e.tensor_copy(
                scat[:, 0:1], ci[:, tt:tt + 1]), reads=["moci"], writes=["mosc"])
            tr.op("vector", lambda e, tt=tt, e_=e_: e.tensor_copy(
                scat[:, 1:2], svf[:, tt, e_:e_ + 1]),
                reads=["mosvf"], writes=["mosc"])
            tr.op("vector", lambda e, tt=tt, e_=e_: e.tensor_copy(
                scat[:, 2:3], destf[:, tt, e_:e_ + 1]),
                reads=["modst"], writes=["mosc"])
            tr.idma(lambda e, tt=tt, e_=e_: e.indirect_dma_start(
                out=s_idx, out_offset=IndirectOffsetOnAxis(
                    ap=slotu[:, tt, e_:e_ + 1], axis=0),
                in_=scat[:], in_offset=None),
                reads=["mosc", "moslu", "s_idx"], writes=["s_idx"])
    tr.barrier()
    g7a.close()
    g7b = Grp(tr.nc)
    wf1 = [g7b.sb(f"mow1{i}", [128, NT, 256]) for i in range(2)]
    wf2 = [g7b.sb(f"mow2{i}", [128, D]) for i in range(2)]
    for e_ in range(NE):
        for qd in range(2):
            kxof = "moxof"
            for i in range(4):
                base = e_ * CAP + qd * 512 + i * 128
                tr.dma("sync", out=ixb[:], in_=s_idx[base:base + 128, :],
                       reads=["s_idx"], writes=["moix"])
                tr.op("vector", lambda e, i=i: e.tensor_copy(
                    giu[:, i:i + 1], ixb[:, 0:1]), reads=["moix"], writes=["mogiu"])
                tr.op("vector", lambda e, i=i: e.tensor_copy(
                    svv[:, i:i + 1], ixb[:, 1:2]), reads=["moix"], writes=["mosvv"])
                tr.op("vector", lambda e, i=i: e.tensor_copy(
                    dstu[:, i:i + 1], ixb[:, 2:3]), reads=["moix"], writes=["modtu"])
                tr.idma(lambda e, i=i: e.indirect_dma_start(
                    out=stage[:], out_offset=None, in_=s_nx2T,
                    in_offset=IndirectOffsetOnAxis(ap=giu[:, i:i + 1], axis=0)),
                    reads=["mogiu", "s_nx2T_z"]
                    + [f"s_nx2T{t}" for t in range(NTI)], writes=["mostg"])
                for k in range(NT):
                    tr.op("tensor", lambda e, k=k: e.transpose(
                        pb[6 + k % 2][:, 0:128], stage[:, k * 128:(k + 1) * 128],
                        ident[:]),
                        reads=["mostg", "c_ident"], writes=[f"pb{6 + k % 2}"])
                    tr.op("vector", lambda e, k=k, i=i: e.tensor_copy(
                        xof[:, k, i * 128:(i + 1) * 128], pb[6 + k % 2][:, 0:128]),
                        reads=[f"pb{6 + k % 2}"], writes=[kxof])
            for fg in range(16):
                wk = f"mow1{fg % 2}"
                for k in range(NT):
                    tr.dma("sync", out=wf1[fg % 2][:, k, :],
                           in_=w_fc1T[e_, k * 128:(k + 1) * 128,
                                      fg * 256:(fg + 1) * 256],
                           writes=[wk] if k == NT - 1 else [wk + "p"])
                for f2 in range(2):
                    ft = fg * 2 + f2
                    bank = 2 + (ft % 4)
                    for k in range(NT):
                        tr.op("tensor", lambda e, fg=fg, f2=f2, k=k, bank=bank: e.matmul(
                            pb[bank][:, :], wf1[fg % 2][:, k, f2 * 128:(f2 + 1) * 128],
                            xof[:, k, :], start=(k == 0), stop=(k == NT - 1)),
                            reads=[wk, wk + "p", kxof],
                            writes=[f"pb{bank}"], mark=(k == NT - 1))
                    tr.op("scalar", lambda e, e_=e_, ft=ft, bank=bank: e.activation(
                        A[ft // 4][:, (ft % 4) * QC:(ft % 4 + 1) * QC],
                        pb[bank][:, :], AF.Gelu,
                        bias=fc1b[:, e_ * 32 + ft:e_ * 32 + ft + 1], scale=1.0),
                        reads=[f"pb{bank}", "smallsF"], writes=[f"h1_{ft}"])
            for ft in range(32):
                wk2 = f"mow2{ft % 2}"
                tr.dma("sync", out=wf2[ft % 2][:],
                       in_=w_fc2T[e_, ft * 128:(ft + 1) * 128, :], writes=[wk2])
                for dm in range(NT):
                    tr.op("tensor", lambda e, ft=ft, dm=dm: e.matmul(
                        pb[dm][:, :], wf2[ft % 2][:, dm * 128:(dm + 1) * 128],
                        A[ft // 4][:, (ft % 4) * QC:(ft % 4 + 1) * QC],
                        start=(ft == 0), stop=(ft == 31)),
                        reads=[wk2, f"h1_{ft}"], writes=[f"pb{dm}"],
                        mark=(dm == NT - 1))
            for dm in range(NT):
                tr.op("scalar", lambda e, e_=e_, dm=dm: e.activation(
                    xof[:, dm, :], pb[dm][:, :], AF.Copy, bias=0.0, scale=1.0),
                    reads=[f"pb{dm}"], writes=[kxof])
                tr.op("vector", lambda e, e_=e_, dm=dm: e.tensor_scalar(
                    xof[:, dm, :], xof[:, dm, :],
                    fc2b[:, e_ * NT + dm:e_ * NT + dm + 1], None, ALU.add),
                    reads=[kxof, "smallsF"], writes=[kxof])
            for i in range(4):
                for dm in range(NT):
                    tr.op("tensor", lambda e, dm=dm, i=i: e.transpose(
                        pb[6 + dm % 2][:, 0:128], xof[:, dm, i * 128:(i + 1) * 128],
                        ident[:]),
                        reads=[kxof, "c_ident"], writes=[f"pb{6 + dm % 2}"])
                    tr.op("vector", lambda e, dm=dm, i=i: e.tensor_copy(
                        stage[:, dm * 128:(dm + 1) * 128], pb[6 + dm % 2][:, 0:128]),
                        reads=[f"pb{6 + dm % 2}"], writes=["mostg"])
                tr.op("vector", lambda e, i=i: e.tensor_scalar(
                    stage[:], stage[:], svv[:, i:i + 1], None, ALU.mult),
                    reads=["mostg", "mosvv"], writes=["mostg"])
                tr.idma(lambda e, i=i: e.indirect_dma_start(
                    out=s_moeT, out_offset=IndirectOffsetOnAxis(
                        ap=dstu[:, i:i + 1], axis=0),
                    in_=stage[:], in_offset=None),
                    reads=["mostg", "modtu"], writes=["s_moeT"])
    tr.barrier()
    g7b.close()
    for tt in range(NTI):
        for rank in range(2):
            tr.dma("sync", out=stage[:], in_=s_moeT[rank * 2048 + tt * 128:
                                                    rank * 2048 + (tt + 1) * 128, :],
                   reads=["s_moeT"], writes=["mostg"])
            for dm in range(NT):
                tr.op("tensor", lambda e, dm=dm: e.transpose(
                    pb[6 + dm % 2][:, 0:128], stage[:, dm * 128:(dm + 1) * 128],
                    ident[:]),
                    reads=["mostg", "c_ident"], writes=[f"pb{6 + dm % 2}"])
                tr.op("vector", lambda e, dm=dm, tt=tt: e.tensor_add(
                    B[dm][:, tt * 128:(tt + 1) * 128],
                    B[dm][:, tt * 128:(tt + 1) * 128], pb[6 + dm % 2][:, 0:128]),
                    reads=[f"pb{6 + dm % 2}", f"x1_{dm}"], writes=[f"x1_{dm}"])


def build_core(upto=99, taps=()):
    nc = bass.Bass("TRN2", target_bir_lowering=False)
    es = contextlib.ExitStack()

    def din(name, shape):
        return nc.dram_tensor(name, shape, F32, kind="ExternalInput").ap()

    x_in = din("x_in", [NT, 128, L])
    w_minT = din("w_minT", [D, 2 * DI])
    w_mxT = din("w_mxT", [NTI, 128, DTR + 2 * DST])
    w_mdtT = din("w_mdtT", [DTR, DI])
    w_moutT = din("w_moutT", [DI, D])
    c_ones = din("c_ones", [128, 128])
    c_ident = din("c_ident", [128, 128])
    c_e32 = din("c_e32", [32, 32 * 128])
    sm_alog = din("sm_alog", [128, NTI * DST])
    sm_d_in = din("sm_d", [128, NTI])
    sm_convb_in = din("sm_convb", [128, NTI])
    sm_convw_in = din("sm_convw", [128, NTI * 4])
    sm_bdt_in = din("sm_bdt", [128, NTI])

    w_qkv = din("w_qkv", [D, 3 * D])
    w_wo = din("w_wo", [D, D])
    c_masks = din("c_masks", [128, 4, QC])
    smf_in = din("smf", [128, 5 * NT])
    bqkvh_in = din("bqkvh", [64, 48])
    l0gw_in = din("l0gw", [NT, 128, NE])
    fc1b_in = din("fc1b", [128, NE * 32])
    fc2b_in = din("fc2b", [128, NE * NT])
    c_e8 = din("c_e8", [8, 8 * 128])
    c_iotaf = din("c_iotaf", [128, NTI])
    c_rowc = din("c_rowc", [8, 1])
    c_idxfill = din("c_idxfill", [8192, 3])
    l1gw_in = din("l1gw", [NT, 128, NE])
    w_e0T = din("w_e0T", [D, D])
    w_e1T = din("w_e1T", [D, D])
    w_owT = din("w_owT", [D, D])
    w_dqT = din("w_dqT", [D, 512])
    w_dkvT = din("w_dkvT", [D, 512])
    w_uqT = din("w_uqT", [512, D])
    w_ukT = din("w_ukT", [512, D])
    w_uvT = din("w_uvT", [512, D])
    w_l1wo = din("w_l1wo", [D, D])
    smf2_in = din("smf2", [128, 2 * NT])
    ropec_in = din("ropec", [NTI, 128, 64])
    ropes_in = din("ropes", [NTI, 128, 64])
    w_fc1T = din("w_fc1T", [NE, D, DFF])
    w_fc2T = din("w_fc2T", [NE, DFF, D])

    s_x = nc.dram_tensor("s_x", [NTI, 128, L], F32).ap()
    s_zs = nc.dram_tensor("s_zs", [NTI, 128, L], F32).ap()
    s_dt = nc.dram_tensor("s_dt", [NTI, 128, L], F32).ap()
    s_s1 = nc.dram_tensor("s_s1", [NT, 128, L], F32).ap()
    s_xn = nc.dram_tensor("s_xn", [NT, 128, L], F32).ap()
    s_bc = nc.dram_tensor("s_bc", [32, L], F32).ap()
    s_nx2 = nc.dram_tensor("s_nx2", [NT, 128, L], F32).ap()
    s_x1r = nc.dram_tensor("s_x1r", [NT, 128, L], F32).ap()
    s_nx2T = nc.dram_tensor("s_nx2T", [2049, D], F32).ap()
    s_idx = nc.dram_tensor("s_idx", [10240, 3], F32).ap()
    s_moeT = nc.dram_tensor("s_moeT", [12288, D], F32).ap()
    s_g1 = nc.dram_tensor("s_g1", [NT, 128, L], F32).ap()
    s_xn1 = nc.dram_tensor("s_xn1", [NT, 128, L], F32).ap()

    y_out = nc.dram_tensor("y", [NT, 128, L], F32, kind="ExternalOutput").ap()
    tap_outs = {}

    with es:
        block = es.enter_context(nc.Block())
        tr = Tr(nc, block, es)
        tr.flags = taps
        pb = [es.enter_context(nc.psum_tensor(f"pb{i}", [128, QC], F32))
              for i in range(8)]

        P = Grp(nc)
        ones = P.sb("ones", [128, 128])
        ident = P.sb("ident", [128, 128])
        negA = P.sb("negA", [128, NTI * DST])
        sm_d = P.sb("smd", [128, NTI])
        sm_convb = P.sb("smcb", [128, NTI])
        sm_convw = P.sb("smcw", [128, NTI * 4])
        sm_bdt = P.sb("smbd", [128, NTI])
        cpool = P.sb("cpool", [128, 4])
        A = [P.sb(f"arA{t}", [128, L]) for t in range(NT)]
        PB_holder = {}

        tr.op("vector", lambda e: e.memset(cpool[:, 0:1], EPS), writes=["cpool_a"])
        tr.op("vector", lambda e: e.memset(cpool[:, 1:2], 1.0), writes=["cpool_b"])
        tr.op("vector", lambda e: e.memset(cpool[:, 2:4], 0.0), writes=["cpool"],
              reads=["cpool_a", "cpool_b"])
        tr.dma("sync", out=ones[:], in_=c_ones, writes=["c_ones"])
        tr.dma("sync", out=ident[:], in_=c_ident, writes=["c_ident"])
        tr.dma("sync", out=negA[:], in_=sm_alog, writes=["negA_raw"])
        tr.dma("sync", out=sm_d[:], in_=sm_d_in, writes=["smalls_d"])
        tr.dma("sync", out=sm_convb[:], in_=sm_convb_in, writes=["smalls_cb"])
        tr.dma("sync", out=sm_convw[:], in_=sm_convw_in, writes=["smalls_cw"])
        tr.dma("sync", out=sm_bdt[:], in_=sm_bdt_in, writes=["smalls_bd"])
        tr.op("scalar", lambda e: e.activation(negA[:], negA[:], AF.Exp,
                                               bias=0.0, scale=1.0),
              reads=["negA_raw"], writes=["negA"])
        tr.op("vector", lambda e: e.tensor_scalar_mul(negA[:], negA[:], -1.0),
              reads=["negA"], writes=["negA"])
        tr.op("gpsimd", lambda e: e.tensor_copy(sm_d[:, 0:1], sm_d[:, 0:1]),
              reads=["smalls_d", "smalls_cb", "smalls_cw", "smalls_bd"],
              writes=["smalls"])

        xn = A
        for t in range(NT):
            tr.dma("sync", out=xn[t][:], in_=x_in[t, :, :], writes=[f"xn{t}"])

        G1a = Grp(nc)
        emit_layernorm(tr, G1a, pb, ones, cpool, xn, [f"xn{t}" for t in range(NT)],
                       xn, [f"xn{t}" for t in range(NT)], tag="ln0")
        tr.barrier()
        G1a.close()
        for t in range(NT):
            tr.dma("gpsimd", out=s_xn[t, :, :], in_=xn[t][:],
                   reads=[f"xn{t}"], writes=[f"s_xn{t}"])

        def tap(name, sb, key):
            if name in taps:
                out = nc.dram_tensor("tap_" + name, list(sb.shape), F32,
                                     kind="ExternalOutput").ap()
                tap_outs[name] = True
                tr.dma("gpsimd", out=out, in_=sb[:], reads=[key])

        for t in range(NT):
            tap(f"xn{t}", xn[t], f"xn{t}")

        if upto >= 2:
            G2 = Grp(nc)
            bc_sb = G2.sb("ma_bc", [32, L])
            w_mxT_sb = G2.sb("wmx", [128, NTI, DTR + 2 * DST])
            w_mdtT_sb = G2.sb("wmdt", [DTR, DI])
            for t in range(NTI):
                tr.dma("sync", out=w_mxT_sb[:, t, :], in_=w_mxT[t, :, :],
                       writes=["w_mxT"] if t == NTI - 1 else ["w_mxT_p"])
            tr.dma("sync", out=w_mdtT_sb[:], in_=w_mdtT, writes=["w_mdtT"])
            dtr_sb = emit_mamba_A(tr, G2, pb, cpool, bc_sb, xn, w_minT,
                                  w_mxT_sb, w_mdtT_sb, sm_convw, sm_convb,
                                  sm_bdt, s_x, s_zs, s_dt)
            tr.dma("gpsimd", out=s_bc, in_=bc_sb[:], reads=["ma_bc"],
                   writes=["s_bc"])
            tap("dtr", dtr_sb, "ma_dtr")
            tap("bc", bc_sb, "ma_bc")

        if upto >= 3:
            tr.barrier()
            G2.close()
            PB = Grp(nc)
            B = [PB.sb(f"arB{t}", [128, L]) for t in range(NT)]
            PB_holder["B"] = B
            G3 = Grp(nc)
            e32_sb = G3.sb("e32", [32, 32 * 128])
            bc_sb = G3.sb("ma_bc3", [32, L])
            tr.dma("sync", out=e32_sb[:], in_=c_e32, writes=["c_e32"])
            tr.dma("sync", out=bc_sb[:], in_=s_bc, reads=["s_bc"],
                   writes=["ma_bc"])
            hcar_ref = []
            emit_mamba_scan(tr, G3, pb, A, B, e32_sb, bc_sb, negA, sm_d,
                            w_moutT, s_x, s_dt, s_zs, s_s1, hcar_ref)
            stp_holder = {}

            def tap_hbm(base, dram, n, keyf):
                if base not in taps:
                    return
                if "stp" not in stp_holder:
                    stp_holder["stp"] = G3.sb("stapx", [128, QC])
                stp = stp_holder["stp"]
                for j in range(n):
                    out = nc.dram_tensor(f"tap_{base}_{j}", [128, L], F32,
                                         kind="ExternalOutput").ap()
                    for q in range(NQ):
                        tr.dma("sync", out=stp[:], in_=dram[j, :, qslice(q)],
                               reads=[keyf(j)], writes=["stap" + base])
                        tr.dma("gpsimd", out=out[:, qslice(q)], in_=stp[:],
                               reads=["stap" + base], writes=["stapr" + base])
            tap_hbm("mx", s_x, NTI, lambda j: f"s_x{j}")
            tap_hbm("mzs", s_zs, NTI, lambda j: f"s_zs{j}")
            tap_hbm("mdt", s_dt, NTI, lambda j: f"s_dt{j}")
            tap_hbm("s1", s_s1, NT, lambda j: f"s_s1{j}")
            if "hcar" in taps:
                outh = nc.dram_tensor("tap_hcar", [128, NTI * DST], F32,
                                      kind="ExternalOutput").ap()
                tr.dma("gpsimd", out=outh, in_=hcar_ref[0][:], reads=["schc"])
            if "ylast" in taps:
                for j in range(4):
                    out = nc.dram_tensor(f"tap_ylast_{j}", [128, L], F32,
                                         kind="ExternalOutput").ap()
                    tr.dma("gpsimd", out=out, in_=B[j][:],
                           reads=[f"scy{4*j}", f"scy{4*j+1}", f"scy{4*j+2}", f"scy{4*j+3}"])
            if "bclast" in taps:
                for j in range(4):
                    out = nc.dram_tensor(f"tap_bclast_{j}", [128, L], F32,
                                         kind="ExternalOutput").ap()
                    tr.dma("gpsimd", out=out, in_=A[j][:],
                           reads=[f"scb{4*j}", f"scb{4*j+1}", f"scb{4*j+2}", f"scb{4*j+3}"])

        if upto >= 4:
            tr.barrier()
            G3.close()
            B = PB_holder["B"]
            PC = Grp(nc)
            smf = PC.sb("smf_sb", [128, 5 * NT])
            bqkvh = PC.sb("bqkvh_sb", [64, 48])
            bq8 = PC.sb("bq8", [64, 16])

            l0gw = PC.sb("l0gw_sb", [128, NT, NE])
            fc1b = PC.sb("fc1b_sb", [128, NE * 32])
            fc2b = PC.sb("fc2b_sb", [128, NE * NT])
            smf2 = PC.sb("smf2_sb", [128, 2 * NT])
            tr.dma("sync", out=smf[:], in_=smf_in, writes=["smallsF"])
            tr.dma("sync", out=bqkvh[:], in_=bqkvh_in, writes=["smallsF"])

            for k in range(NT):
                tr.dma("sync", out=l0gw[:, k, :], in_=l0gw_in[k, :, :],
                       writes=["l0gw"])
            tr.dma("sync", out=fc1b[:], in_=fc1b_in, writes=["smallsF"])
            tr.dma("sync", out=fc2b[:], in_=fc2b_in, writes=["smallsF"])
            tr.dma("sync", out=smf2[:], in_=smf2_in, writes=["smallsF"])
            w_ln1w, w_ln1b = smf[:, 0:NT], smf[:, NT:2 * NT]
            w_ln2w, w_ln2b = smf[:, 2 * NT:3 * NT], smf[:, 3 * NT:4 * NT]
            w_bo = smf[:, 4 * NT:5 * NT]
            G4 = Grp(nc)
            emit_ffn_prologue(tr, G4, pb, ones, cpool, A, B, s_xn,
                              w_ln1w, w_ln1b, w_bo, bqkvh, bq8)
            for t in range(NT):
                tap(f"nxf{t}", A[t], f"nx{t}")

        if upto >= 5:
            tr.barrier()
            G4.close()
            G5 = Grp(nc)
            masks_sb = G5.sb("masks", [128, 4, QC])
            tr.dma("sync", out=masks_sb[:], in_=c_masks, writes=["masks"])
            emit_attn_l0(tr, G5, pb, ones, A, B, w_qkv, w_wo, bqkvh, bq8,
                         masks_sb)
            for t in range(NT):
                tap(f"x1_{t}", B[t], f"x1_{t}")

        if upto >= 6:
            tr.barrier()
            G5.close()
            G6 = Grp(nc)
            emit_ln2_store(tr, G6, pb, ones, cpool, B, w_ln2w, w_ln2b, s_nx2)
            tr.barrier()
            G6.close()

        if upto >= 7:
            G7 = Grp(nc)
            emit_moe_routed(tr, G7, pb, ones, ident, A, B, w_fc1T, w_fc2T,
                            fc1b, fc2b, l0gw, c_iotaf, c_rowc, c_idxfill,
                            s_nx2, s_nx2T, s_idx, s_moeT)
            for t in range(NT):
                tap(f"ffn{t}", B[t], f"x1_{t}")

        if upto >= 8:
            tr.barrier()
            G7.close()
            G8 = Grp(nc)
            emit_combine_l0(tr, G8, pb, B, x_in, s_s1, s_x1r)
            for t in range(NT):
                tap(f"xr{t}", B[t], f"xr{t}")

        if upto >= 9:
            tr.barrier()
            G8.close()
            G9 = Grp(nc)
            emit_layernorm(tr, G9, pb, ones, cpool, B,
                           [f"xr{t}" for t in range(NT)], B,
                           [f"xn1{t}" for t in range(NT)], tag="ln9")
            tr.barrier()
            G9.close()

        if upto >= 10:
            G10 = Grp(nc)
            e8b = G10.sb("e8b", [8, 8 * 128])
            tr.dma("sync", out=e8b[:], in_=c_e8, writes=["c_e8"])
            emit_l1ffn(tr, G10, pb, ones, ident, e8b, A, B, w_e0T, w_e1T,
                       w_owT, l1gw_in, s_g1)

        if upto >= 11:
            tr.barrier()
            G10.close()
            G11 = Grp(nc)
            ropec_sb = G11.sb("ropec_sb", [128, NTI, 64])
            ropes_sb = G11.sb("ropes_sb", [128, NTI, 64])
            for t in range(NTI):
                tr.dma("sync", out=ropec_sb[:, t, :], in_=ropec_in[t, :, :],
                       writes=["rope"])
                tr.dma("sync", out=ropes_sb[:, t, :], in_=ropes_in[t, :, :],
                       writes=["rope"])
            emit_attn_l1(tr, G11, pb, ones, ident, cpool, A, B, w_dqT, w_dkvT,
                         w_uqT, w_ukT, w_uvT, w_l1wo, smf2, ropec_sb, ropes_sb,
                         s_xn1)
            for t in range(NT):
                tap(f"s1a{t}", B[t], f"s1a{t}")

        if upto >= 12:
            tr.barrier()
            G11.close()
            G12 = Grp(nc)
            emit_final(tr, G12, B, s_x1r, s_g1, y_out)

        tr.finish()
    return nc


def _prep_common(i):
    c = {}
    A = np.ascontiguousarray
    c["w_minT"] = A(i["L0_attn_W_in"].T)
    c["w_mxT"] = A(i["L0_attn_W_x"].T.reshape(NTI, 128, 96))
    c["w_mdtT"] = A(i["L0_attn_W_dt"].T)
    c["w_moutT"] = A(i["L0_attn_W_out"].T)
    c["c_ones"] = np.ones((128, 128), np.float32)
    c["c_ident"] = np.eye(128, dtype=np.float32)
    e32 = np.zeros((32, 32, 128), np.float32)
    for j in range(32):
        e32[j, j, :] = 1.0
    c["c_e32"] = A(e32.reshape(32, 32 * 128))
    c["sm_alog"] = A(i["L0_attn_A_log"].reshape(NTI, 128, DST)
                     .transpose(1, 0, 2).reshape(128, NTI * DST))
    c["sm_d"] = A(i["L0_attn_D"].reshape(NTI, 128).T)
    c["sm_convb"] = A(i["L0_attn_conv_b"].reshape(NTI, 128).T)
    c["sm_convw"] = A(i["L0_attn_conv_w"][:, 0, :].reshape(NTI, 128, 4)
                      .transpose(1, 0, 2).reshape(128, NTI * 4))
    c["sm_bdt"] = A(i["L0_attn_b_dt"].reshape(NTI, 128).T)
    c["w_qkv"] = A(i["L0_ffn_Wqkv"])
    c["w_wo"] = A(i["L0_ffn_Wo"])
    m = np.zeros((128, 4, QC), np.float32)
    for j in range(4):
        for kl in range(128):
            m[kl, j, :] = (np.arange(QC) >= kl + 128 * j).astype(np.float32)
    c["c_masks"] = m
    smf = np.zeros((128, 5 * NT), np.float32)
    for idx, nm in enumerate(["L0_ffn_ln1_w", "L0_ffn_ln1_b", "L0_ffn_ln2_w",
                              "L0_ffn_ln2_b", "L0_ffn_bo"]):
        smf[:, idx * NT:(idx + 1) * NT] = i[nm].reshape(NT, 128).T
    c["smf"] = smf
    c["bqkvh"] = A(i["L0_ffn_bqkv"].reshape(3, 16, 64).transpose(2, 1, 0)
                   .reshape(64, 48))
    c["l0gw"] = A(i["L0_ffn_gate_W"].reshape(NT, 128, NE))
    c["fc1b"] = A(i["L0_ffn_fc1_b"].reshape(NE, 32, 128).transpose(2, 0, 1)
                  .reshape(128, NE * 32))
    c["fc2b"] = A(i["L0_ffn_fc2_b"].reshape(NE, NT, 128).transpose(2, 0, 1)
                  .reshape(128, NE * NT))
    e8m = np.zeros((8, 8, 128), np.float32)
    for j in range(8):
        e8m[j, j, :] = 1.0
    c["c_e8"] = A(e8m.reshape(8, 1024))
    c["c_iotaf"] = A((np.arange(NTI)[None, :] * 128
                      + np.arange(128)[:, None]).astype(np.float32))
    c["c_rowc"] = A((np.arange(8, dtype=np.float32) * CAP - 1.0).reshape(8, 1))
    fillr = np.zeros((8192, 3), np.float32)
    fillr[:, 0] = 2048.0
    fillr[:, 2] = 4096.0 + np.arange(8192)
    c["c_idxfill"] = A(fillr)
    c["w_fc1T"] = A(i["L0_ffn_fc1_w"].transpose(0, 2, 1))
    c["w_fc2T"] = A(i["L0_ffn_fc2_w"].transpose(0, 2, 1))
    c["l1gw"] = A(i["L1_ffn_gate_W"].T.reshape(NT, 128, NE))
    c["w_e0T"] = A(i["L1_ffn_expert_W"][0].T)
    c["w_e1T"] = A(i["L1_ffn_expert_W"][1].T)
    c["w_owT"] = A(i["L1_ffn_out_W"].T)
    c["w_dqT"] = A(i["L1_attn_w_down_q"].T)
    c["w_dkvT"] = A(i["L1_attn_w_down_kv"].T)
    c["w_uqT"] = A(i["L1_attn_w_up_q"].T)
    c["w_ukT"] = A(i["L1_attn_w_up_k"].T)
    c["w_uvT"] = A(i["L1_attn_w_up_v"].T)
    c["w_l1wo"] = A(i["L1_attn_w_out"].T)
    smf2 = np.zeros((128, 2 * NT), np.float32)
    smf2[:, 0:NT] = i["L1_attn_ln_scale"].reshape(NT, 128).T
    smf2[:, NT:2 * NT] = i["L1_attn_ln_bias"].reshape(NT, 128).T
    c["smf2"] = smf2
    inv_freq = 1.0 / 10000.0 ** (np.arange(0, 64, 2, dtype=np.float32) / 64)
    tpos = np.arange(L, dtype=np.float32)
    freqs = np.outer(tpos, inv_freq)
    cos_t = np.concatenate([np.cos(freqs), np.cos(freqs)], -1)[:, ::2]
    sin_t = np.concatenate([np.sin(freqs), np.sin(freqs)], -1)[:, ::2]
    c["ropec"] = A(np.concatenate([cos_t, cos_t], 1).reshape(NTI, 128, 64))
    c["ropes"] = A(np.concatenate([sin_t, sin_t], 1).reshape(NTI, 128, 64))
    return {k: np.asarray(v, np.float32) for k, v in c.items()}


def _per_core(i, b):
    x = np.asarray(i["L0_attn_hidden_states"][b], np.float32)
    return {"x_in": np.ascontiguousarray(x.T.reshape(NT, 128, L))}


def kernel(**inputs):
    common = _prep_common(inputs)
    nc = build_core(upto=99)
    in_maps = []
    for b in range(8):
        m = dict(common)
        m.update(_per_core(inputs, b))
        in_maps.append(m)
    res = run_bass_kernel_spmd(nc, in_maps, core_ids=list(range(8)))
    outs = []
    for b in range(8):
        y = res.results[b]["y"].reshape(D, L)
        outs.append(y.T)
    return np.stack(outs).astype(np.float32)
